# version 19
# baseline (speedup 1.0000x reference)
import math
from contextlib import ExitStack

import numpy as np
import ml_dtypes
import concourse.bass as bass
import concourse.mybir as mybir
from concourse.bass_utils import run_bass_kernel_spmd

F32 = mybir.dt.float32
BF16 = mybir.dt.bfloat16
I32 = mybir.dt.int32
AF = mybir.ActivationFunctionType
ALU = mybir.AluOpType
AX = mybir.AxisListType
BF = ml_dtypes.bfloat16

T = 4096
D = 1024
E = 2048
NST = 16
H0 = 3072
I1 = 2816
NH, HD = 16, 64
NT128 = T // 128
NT512 = T // 512


class _IP:

    def __init__(self, ins, pend):
        self._ins = ins
        self._pend = pend

    def then_inc(self, sem, k):
        self._pend[sem.num] = self._pend.get(sem.num, 0) + k
        self._ins.then_inc(sem, k)
        return self

    def annotate(self, a):
        self._ins.annotate(a)
        return self

    def __getattr__(self, name):
        return getattr(self._ins, name)


class _QP:

    def __init__(self, q, base, pend):
        self._q = q
        self._base = base
        self._pend = pend

    def wait_ge(self, sem, val):
        return self._q.wait_ge(sem, val + self._base.get(sem.num, 0))

    def sem_inc(self, sem, k):
        self._pend[sem.num] = self._pend.get(sem.num, 0) + k
        return self._q.sem_inc(sem, k)

    def __getattr__(self, name):
        attr = getattr(self._q, name)
        if not callable(attr):
            return attr
        pend = self._pend

        def call(*a, **kw):
            r = attr(*a, **kw)
            if r is not None and hasattr(r, "then_inc"):
                return _IP(r, pend)
            return r
        return call


class _BlockWrap:
    def __init__(self, p):
        self._p = p
        self._blk = None

    def __enter__(self):
        self._blk = self._p.nc.Block()
        self._blk.__enter__()
        return self

    def __exit__(self, *a):
        r = self._blk.__exit__(*a)
        for k, v in self._p._pend.items():
            self._p._base[k] = self._p._base.get(k, 0) + v
        self._p._pend.clear()
        return r

    def _mk(self, engname):
        def reg(f):
            getattr(self._blk, engname)(
                lambda q: f(_QP(q, self._p._base, self._p._pend)))
        return reg

    def __getattr__(self, name):
        if name in ("sync", "vector", "scalar", "tensor", "gpsimd"):
            return self._mk(name)
        return getattr(self._blk, name)


class Prog:
    def __init__(self):
        self.nc = bass.Bass("TRN2", target_bir_lowering=False)
        self.stack = ExitStack()
        self._names = set()
        self._scopes = []
        self._tag = ""
        self._base = {}
        self._pend = {}

    def _st(self):
        return self._scopes[-1] if self._scopes else self.stack

    def push(self, tag):
        self._scopes.append(ExitStack())
        self._tag = tag + "_"

    def pop(self):
        self._scopes.pop().close()
        self._tag = ""

    def sb(self, name, shape, dtype):
        name = self._tag + name
        assert name not in self._names, name
        self._names.add(name)
        return self._st().enter_context(self.nc.sbuf_tensor(name, shape, dtype))

    def ps(self, name, shape, dtype):
        name = self._tag + name
        assert name not in self._names, name
        self._names.add(name)
        return self._st().enter_context(self.nc.psum_tensor(name, shape, dtype))

    def sem(self, name):
        name = self._tag + name
        assert name not in self._names, name
        self._names.add(name)
        return self._st().enter_context(self.nc.semaphore(name))

    def block(self):
        return _BlockWrap(self)

    def close(self):
        self.stack.close()


class WeightLoad:

    def __init__(self, tag, dram_ap, O, IN, dest):
        self.tag = tag
        self.dram = dram_ap
        self.O, self.IN = O, IN
        self.dest = dest
        self.nj = IN // 128

    def emit_loads(self, q, sem):
        for j in range(self.nj):
            q.dma_start(
                out=self.dest[j][:],
                in_=self.dram[j * 128:(j + 1) * 128, :],
            ).then_inc(sem, 16)

    def full(self):
        return 16 * self.nj


def build_ffn0(p: Prog, io=None):
    nc = p.nc
    if io is None:
        xin = nc.dram_tensor("xin", [T, D], F32, kind="ExternalInput").ap()
        wg_d = nc.dram_tensor("wg", [D, H0], I32, kind="ExternalInput").ap()
        wu_d = nc.dram_tensor("wu", [D, H0], I32, kind="ExternalInput").ap()
        wd_d = nc.dram_tensor("wd", [H0, D], I32, kind="ExternalInput").ap()
        ident_d = nc.dram_tensor("ident", [128, 128], BF16, kind="ExternalInput").ap()
        xout = nc.dram_tensor("xout", [T, D], F32, kind="ExternalOutput").ap()
    else:
        xin, wg_d, wu_d, wd_d, ident_d, xout = io

    NKH = H0 // 128
    ND = D // 128

    wgT = [p.sb(f"wgT{c}", [128, H0], BF16) for c in range(ND)]
    wuT = [p.sb(f"wuT{c}", [128, H0], BF16) for c in range(ND)]
    wdT = [p.sb(f"wdT{k}", [128, D], BF16) for k in range(NKH)]

    xs = [p.sb(f"xs{b}", [128, D], F32) for b in range(2)]
    xr = [p.sb(f"xr{b}", [128, 512], F32) for b in range(2)]
    xo = [p.sb(f"xo{b}", [128, 512], F32) for b in range(2)]
    xn = p.sb("xn", [128, D], BF16)
    xnT = [p.sb(f"xnT{c}", [128, 512], BF16) for c in range(ND)]
    hsb = [p.sb(f"hsb{k}", [128, 512], BF16) for k in range(NKH)]
    sgb = [p.sb(f"sgb{b}", [128, 512], BF16) for b in range(2)]
    scr = p.sb("scr", [128, D], BF16)
    ident = p.sb("identt", [128, 128], BF16)

    stt = {nm: [p.sb(f"st_{nm}{i}", [128, 1], F32) for i in range(4)]
           for nm in ["s1", "s2", "mu", "v1", "m2", "var", "ln", "rs", "rs2", "nm"]}

    ps_tp = [p.ps(f"ps_tp{b}", [128, 128], BF16) for b in range(2)]
    ps_g = [p.ps(f"ps_g{b}", [128, 512], F32) for b in range(2)]
    ps_u = [p.ps(f"ps_u{b}", [128, 512], F32) for b in range(2)]
    ps_o = [p.ps(f"ps_o{b}", [128, 512], F32) for b in range(2)]

    s = {nm: p.sem(nm) for nm in ["wlg", "wlu", "wld", "li",
                                  "ldp0", "ldp1", "lrp0", "lrp1",
                                  "ve", "vv", "sc", "pe", "stp0", "stp1", "vs", "ss"]}

    wg_p = WeightLoad("wg", wg_d, H0, D, wgT)
    wu_p = WeightLoad("wu", wu_d, H0, D, wuT)
    wd_p = WeightLoad("wd", wd_d, D, H0, wdT)

    INV127 = 1.0 / 127.0

    VE = 68

    def ve_ln(si, i):
        return VE * si + 9 * i + 1

    def ve_ev(si, j):
        i, c = j // 8, j % 8
        return VE * si + 9 * i + 2 + c

    def ve_h(si, k):
        return VE * si + 36 + k + 1

    def ve_res(si, j):
        return VE * si + 60 + j + 1

    SC = 32

    def sc_sq(si, i):
        return SC * si + 2 * i + 1

    def sc_rs(si, i):
        return SC * si + 2 * i + 2

    def sc_si(si, k):
        return SC * si + 8 + k + 1

    PE = 88

    def pe_tp(si, j):
        return PE * si + j + 1

    def pe_g(si, k):
        return PE * si + 32 + 2 * k + 1

    def pe_u(si, k):
        return PE * si + 32 + 2 * k + 2

    def pe_o(si, j):
        return PE * si + 80 + j + 1

    def ld_x(g):
        return (f"ldp{g % 2}", 16 * (g // 2 + 1))

    def ld_r(cnt):
        return (f"lrp{cnt % 2}", 16 * (cnt // 2 + 1))

    with p.block() as blk:

        def gpsimd_f(q):
            wg_p.emit_loads(q, s["wlg"])
            wu_p.emit_loads(q, s["wlu"])
            wd_p.emit_loads(q, s["wld"])

        def scalarq_f(q):
            ssn = [0]

            def sinc(ins):
                ins.then_inc(s["ss"], 1)
                ssn[0] += 1
                return ssn[0]

            for si in range(NT512):
                for i in range(4):
                    g = si * 4 + i
                    nm_, v_ = ld_x(g)
                    q.wait_ge(s[nm_], v_)
                    prior = SC * si + 2 * i
                    if prior > 0:
                        q.wait_ge(s["sc"], prior)
                    q.activation(scr[:], xs[g % 2][:], AF.Square,
                                 accum_out=stt["s2"][i][:]).then_inc(s["sc"], 1)
                    q.wait_ge(s["vv"], g + 1)
                    n0 = sinc(q.activation(stt["ln"][i][:], stt["var"][i][:], AF.Ln))
                    q.wait_ge(s["ss"], n0)
                    q.activation(stt["rs"][i][:], stt["ln"][i][:], AF.Exp,
                                 scale=-0.5).then_inc(s["sc"], 1)
                for k in range(NKH):
                    q.wait_ge(s["pe"], pe_g(si, k))
                    if k >= 2 or si > 0:
                        kk, ss_ = (k - 2, si) if k >= 2 else (22 + k, si - 1)
                        q.wait_ge(s["ve"], ve_h(ss_, kk))
                    q.activation(sgb[k % 2][:], ps_g[k % 2][:],
                                 AF.Silu).then_inc(s["sc"], 1)

        def sync_f(q):
            q.dma_start(out=ident[:], in_=ident_d).then_inc(s["li"], 16)
            for si in range(NT512):
                for i in range(4):
                    g = si * 4 + i
                    if g >= 2:
                        q.wait_ge(s["ve"], ve_ln((g - 2) // 4, (g - 2) % 4))
                    q.dma_start(
                        out=xs[g % 2][:],
                        in_=xin[g * 128:(g + 1) * 128, :],
                    ).then_inc(s[f"ldp{g % 2}"], 16)
                def emit_store(jj):
                    sub_, dn_ = jj // 2, jj % 2
                    gj_ = si * 8 + jj
                    q.wait_ge(s["ve"], ve_res(si, jj))
                    q.dma_start(
                        out=xout[(si * 4 + sub_) * 128:(si * 4 + sub_ + 1) * 128,
                                 dn_ * 512:(dn_ + 1) * 512],
                        in_=xo[jj % 2][:],
                    ).then_inc(s[f"stp{gj_ % 2}"], 16)

                for j in range(8):
                    sub, dn = j // 2, j % 2
                    gj = si * 8 + j
                    if gj >= 2:
                        pj = gj - 2
                        q.wait_ge(s["ve"], ve_res(pj // 8, pj % 8))
                    q.dma_start(
                        out=xr[j % 2][:],
                        in_=xin[(si * 4 + sub) * 128:(si * 4 + sub + 1) * 128,
                                dn * 512:(dn + 1) * 512],
                    ).then_inc(s[f"lrp{gj % 2}"], 16)
                    if j >= 2:
                        emit_store(j - 2)
                emit_store(6)
                emit_store(7)
            q.wait_ge(s["stp0"], 16 * NT512 * 4)
            q.wait_ge(s["stp1"], 16 * NT512 * 4)

        def vector_f(q):
            vsn = [0]

            def vinc(ins):
                ins.then_inc(s["vs"], 1)
                vsn[0] += 1
                return vsn[0]

            def vwait(n):
                if n > 0:
                    q.wait_ge(s["vs"], n)

            for si in range(NT512):
                ev = {}
                for i in range(4):
                    g = si * 4 + i
                    nm_, v_ = ld_x(g)
                    q.wait_ge(s[nm_], v_)
                    ev[("s1", i)] = vinc(q.reduce_sum(stt["s1"][i][:],
                                                      xs[g % 2][:], AX.X))
                    vwait(ev[("s1", i)])
                    ev[("mu", i)] = vinc(q.tensor_scalar(
                        stt["mu"][i][:], stt["s1"][i][:], 1.0 / D, None, ALU.mult))
                    q.wait_ge(s["sc"], sc_sq(si, i))
                    ev[("v1", i)] = vinc(q.tensor_scalar(
                        stt["v1"][i][:], stt["s2"][i][:], 1.0 / D, 1e-5,
                        ALU.mult, ALU.add))
                    vwait(ev[("mu", i)])
                    ev[("m2", i)] = vinc(q.scalar_tensor_tensor(
                        stt["m2"][i][:], stt["mu"][i][:], -1.0, stt["mu"][i][:],
                        ALU.mult, ALU.mult))
                    vwait(ev[("m2", i)])
                    ins = q.tensor_tensor(stt["var"][i][:], stt["v1"][i][:],
                                          stt["m2"][i][:], ALU.add)
                    ev[("var", i)] = vinc(ins)
                    vwait(ev[("var", i)])
                    q.sem_inc(s["vv"], 1)
                    q.wait_ge(s["sc"], sc_rs(si, i))
                    ev[("nm0", i)] = vinc(q.tensor_scalar(
                        stt["nm"][i][:], stt["rs"][i][:], INV127, None, ALU.mult))
                    vwait(ev[("nm0", i)])
                    ev[("nm", i)] = vinc(q.scalar_tensor_tensor(
                        stt["nm"][i][:], stt["mu"][i][:], -1.0, stt["nm"][i][:],
                        ALU.mult, ALU.mult))
                    ev[("rs2", i)] = vinc(q.tensor_scalar(
                        stt["rs2"][i][:], stt["rs"][i][:], INV127, None, ALU.mult))
                    if g > 0:
                        pg = g - 1
                        q.wait_ge(s["pe"], pe_tp(pg // 4, (pg % 4) * 8 + 7))
                    vwait(max(ev[("rs2", i)], ev[("nm", i)]))
                    q.tensor_scalar(
                        xn[:], xs[g % 2][:], stt["rs2"][i].ap()[:, 0:1],
                        stt["nm"][i].ap()[:, 0:1], ALU.mult, ALU.add,
                    ).then_inc(s["ve"], 1)
                    for c in range(8):
                        j = i * 8 + c
                        q.wait_ge(s["pe"], pe_tp(si, j))
                        q.tensor_copy(xnT[c].ap()[:, i * 128:(i + 1) * 128],
                                      ps_tp[j % 2][:]).then_inc(s["ve"], 1)
                for k in range(NKH):
                    q.wait_ge(s["sc"], sc_si(si, k))
                    q.wait_ge(s["pe"], pe_u(si, k))
                    q.scalar_tensor_tensor(hsb[k][:], sgb[k % 2][:], INV127,
                                           ps_u[k % 2][:], ALU.mult,
                                           ALU.mult).then_inc(s["ve"], 1)
                for j in range(8):
                    gj = si * 8 + j
                    q.wait_ge(s["pe"], pe_o(si, j))
                    nm_, v_ = ld_r(gj)
                    q.wait_ge(s[nm_], v_)
                    if gj >= 2:
                        q.wait_ge(s[f"stp{gj % 2}"], 16 * ((gj - 2) // 2 + 1))
                    q.tensor_tensor(xo[j % 2][:], xr[j % 2][:], ps_o[j % 2][:],
                                    ALU.add).then_inc(s["ve"], 1)

        def tensor_f(q):
            q.wait_ge(s["li"], 16)
            for si in range(NT512):
                for j in range(32):
                    i, c = j // 8, j % 8
                    q.wait_ge(s["ve"], ve_ln(si, i))
                    gj = si * 32 + j
                    if gj >= 2:
                        pj = gj - 2
                        q.wait_ge(s["ve"], ve_ev(pj // 32, pj % 32))
                    q.transpose(ps_tp[j % 2][:],
                                xn.ap()[:, c * 128:(c + 1) * 128],
                                ident[:]).then_inc(s["pe"], 1)
                for k in range(NKH):
                    if si == 0 and k == 0:
                        q.wait_ge(s["wlg"], wg_p.full())
                        q.wait_ge(s["wlu"], wu_p.full())
                    if k == 0:
                        q.wait_ge(s["ve"], VE * si + 36)
                    if k >= 2 or si > 0:
                        kk, ss = (k - 2, si) if k >= 2 else (22 + k, si - 1)
                        q.wait_ge(s["sc"], sc_si(ss, kk))
                    for c in range(ND):
                        ins = q.matmul(ps_g[k % 2][:],
                                       wgT[c].ap()[:, k * 128:(k + 1) * 128],
                                       xnT[c][:], start=(c == 0),
                                       stop=(c == ND - 1))
                        if c == ND - 1:
                            ins.then_inc(s["pe"], 1)
                    if k >= 2 or si > 0:
                        kk, ss = (k - 2, si) if k >= 2 else (22 + k, si - 1)
                        q.wait_ge(s["ve"], ve_h(ss, kk))
                    for c in range(ND):
                        ins = q.matmul(ps_u[k % 2][:],
                                       wuT[c].ap()[:, k * 128:(k + 1) * 128],
                                       xnT[c][:], start=(c == 0),
                                       stop=(c == ND - 1))
                        if c == ND - 1:
                            ins.then_inc(s["pe"], 1)
                for j in range(8):
                    sub, dn = j // 2, j % 2
                    if si == 0 and j == 0:
                        q.wait_ge(s["wld"], wd_p.full())
                    q.wait_ge(s["ve"], ve_h(si, NKH - 1))
                    if j >= 2 or si > 0:
                        jj, ss = (j - 2, si) if j >= 2 else (6 + j, si - 1)
                        q.wait_ge(s["ve"], ve_res(ss, jj))
                    for k in range(NKH):
                        ins = q.matmul(ps_o[j % 2][:],
                                       hsb[k].ap()[:, sub * 128:(sub + 1) * 128],
                                       wdT[k].ap()[:, dn * 512:(dn + 1) * 512],
                                       start=(k == 0), stop=(k == NKH - 1))
                        if k == NKH - 1:
                            ins.then_inc(s["pe"], 1)

        blk.gpsimd(gpsimd_f)
        blk.scalar(scalarq_f)
        blk.sync(sync_f)
        blk.vector(vector_f)
        blk.tensor(tensor_f)

    return nc


def host_ident():
    return np.eye(128).astype(BF).view(np.uint16)


def ffn0_ref_np(x):
    raise NotImplementedError


I16 = mybir.dt.int16
I1H = I1 // 2
NKI = I1H // 128
M05 = float(3 * 2 ** 21)


class QuantEmitter:

    def __init__(self, p: Prog, maxw):
        nb = maxw // 16
        self.m = p.sb("qt_m", [128, nb], F32)
        self.sc6 = p.sb("qt_sc6", [128, nb], F32)
        self.rinv = p.sb("qt_rinv", [128, nb], F32)
        self.sb_ = p.sb("qt_sb", [128, nb], BF16)
        self.t = p.sb("qt_t", [128, maxw], F32)
        self.r05 = p.sb("qt_r05", [128, maxw], BF16)
        self.vka = p.sb("qt_vka", [128, maxw], I16)
        self.vkb = p.sb("qt_vkb", [128, maxw], I16)
        self.A = p.sb("qt_A", [128, maxw], BF16)
        self.B = p.sb("qt_B", [128, maxw], BF16)
        self.u = p.sb("qt_u", [128, maxw], BF16)
        self.qq = p.sb("qt_qq", [128, maxw], BF16)

    def emit(self, q, vinc, vwait, src_ap, dst_ap, W, vcur=None):
        nb = W // 16
        if vcur is not None:
            vwait(vcur())
        ev = {}
        ev["m"] = vinc(q.tensor_reduce(self.m.ap()[:, 0:nb],
                                       src_ap.rearrange("p (a b) -> p a b", b=16),
                                       AX.X, ALU.max, apply_absolute_value=True))
        vwait(ev["m"])
        ev["sc6"] = vinc(q.tensor_scalar(self.sc6.ap()[:, 0:nb], self.m.ap()[:, 0:nb],
                                         1.0 / 6.0, 1e-12, ALU.mult, ALU.max))
        vwait(ev["sc6"])
        ev["rinv"] = vinc(q.reciprocal(self.rinv.ap()[:, 0:nb], self.sc6.ap()[:, 0:nb]))
        ev["sb"] = vinc(q.tensor_scalar(self.sb_.ap()[:, 0:nb], self.sc6.ap()[:, 0:nb],
                                        1.0, None, ALU.mult))
        vwait(ev["rinv"])
        ev["t"] = vinc(q.tensor_tensor(
            self.t.ap()[:, 0:W].rearrange("p (a b) -> p a b", b=16),
            src_ap.rearrange("p (a b) -> p a b", b=16),
            self.rinv.ap()[:, 0:nb].unsqueeze(2).broadcast_to([128, nb, 16]),
            ALU.mult))
        vwait(ev["t"])
        ev["r05"] = vinc(q.tensor_scalar(self.r05.ap()[:, 0:W], self.t.ap()[:, 0:W],
                                         M05, -M05, ALU.add, ALU.add))
        ev["vka"] = vinc(q.tensor_scalar(self.vka.ap()[:, 0:W],
                                         self.t.ap()[:, 0:W].bitcast(I16)[:, 1::2],
                                         0x20, None, ALU.add))
        vwait(ev["vka"])
        ev["vkb"] = vinc(q.tensor_scalar(self.vkb.ap()[:, 0:W], self.vka.ap()[:, 0:W],
                                         -64, None, ALU.bitwise_and))
        vwait(ev["r05"])
        ev["A"] = vinc(q.tensor_scalar(self.A.ap()[:, 0:W], self.r05.ap()[:, 0:W],
                                       -2.0, 2.0, ALU.max, ALU.min))
        vwait(ev["vkb"])
        ev["B"] = vinc(q.tensor_scalar(self.B.ap()[:, 0:W],
                                       self.vkb.ap()[:, 0:W].bitcast(BF16),
                                       -2.0, 2.0, ALU.max, ALU.min))
        vwait(ev["B"])
        ev["u"] = vinc(q.tensor_tensor(self.u.ap()[:, 0:W],
                                       self.vkb.ap()[:, 0:W].bitcast(BF16),
                                       self.B.ap()[:, 0:W], ALU.subtract))
        vwait(max(ev["A"], ev["u"]))
        ev["qq"] = vinc(q.tensor_tensor(self.qq.ap()[:, 0:W], self.A.ap()[:, 0:W],
                                        self.u.ap()[:, 0:W], ALU.add))
        vwait(max(ev["qq"], ev["sb"]))
        ins = q.tensor_tensor(
            dst_ap.rearrange("p (a b) -> p a b", b=16),
            self.qq.ap()[:, 0:W].rearrange("p (a b) -> p a b", b=16),
            self.sb_.ap()[:, 0:nb].unsqueeze(2).broadcast_to([128, nb, 16]),
            ALU.mult)
        c = vinc(ins)
        vwait(c)
        return ins, c



class FWPrep:

    def __init__(self, p, fwprep):
        wg_raw, wu_raw, wd_raw, wgq_sp, wuq_sp, wdq_sp = fwprep
        FQW = 704
        self.fwn = [p.sb(f"fwn{b}", [128, FQW], F32) for b in range(2)]
        self.fwqn = [p.sb(f"fwqn{b}", [128, FQW], BF16) for b in range(2)]
        self.qe = QuantEmitter(p, FQW)
        self.sl = [p.sem("fwl0"), p.sem("fwl1")]
        self.sq = p.sem("fwq")
        self.ss = p.sem("fws")
        pieces = []
        for raw, sp in ((wg_raw, wgq_sp), (wu_raw, wuq_sp)):
            for k in range(I1 // 128):
                for (c0, w) in ((0, 704), (704, 320)):
                    pieces.append((raw[k * 128:(k + 1) * 128, c0:c0 + w],
                                   sp[k * 128:(k + 1) * 128, c0:c0 + w], w))
        for k in range(D // 128):
            for j4 in range(4):
                c0 = j4 * 704
                pieces.append((wd_raw[k * 128:(k + 1) * 128, c0:c0 + 704],
                               wdq_sp[k * 128:(k + 1) * 128, c0:c0 + 704], 704))
        self.pieces = pieces
        self.NFP = len(pieces)
        self.vg = [0]

    def run_gp(self, q):
        for g in range(self.NFP):
            if g >= 2:
                q.wait_ge(self.sq, g - 1)
            src, dst, w = self.pieces[g]
            q.dma_start(out=self.fwn[g % 2].ap()[:, 0:w],
                        in_=src).then_inc(self.sl[g % 2], 16)
            if g >= 2:
                ps_, pd_, pw_ = self.pieces[g - 2]
                q.dma_start(out=pd_,
                            in_=self.fwqn[(g - 2) % 2].ap()[:, 0:pw_]
                            ).then_inc(self.ss, 16)
        q.wait_ge(self.sq, self.NFP)
        for g in (self.NFP - 2, self.NFP - 1):
            src, dst, w = self.pieces[g]
            q.dma_start(out=dst,
                        in_=self.fwqn[g % 2].ap()[:, 0:w]).then_inc(self.ss, 16)
        q.wait_ge(self.ss, 16 * self.NFP)

    def pump_v(self, q, vinc, vwait):
        if self.vg[0] >= self.NFP:
            return
        g = self.vg[0]
        q.wait_ge(self.sl[g % 2], 16 * (g // 2 + 1))
        if g >= 2:
            q.wait_ge(self.ss, 16 * (g - 1))
        src, dst, w = self.pieces[g]
        self.qe.emit(q, vinc, vwait, self.fwn[g % 2].ap()[:, 0:w],
                     self.fwqn[g % 2].ap()[:, 0:w], w, vcur=None)
        q.sem_inc(self.sq, 1)
        self.vg[0] += 1

    def drain_v(self, q, vinc, vwait):
        while self.vg[0] < self.NFP:
            self.pump_v(q, vinc, vwait)


def build_ffn1(p: Prog, half=None, io=None):
    nc = p.nc
    if io is None:
        xacc = nc.dram_tensor("xacc", [T, D], F32, kind="ExternalInput").ap()
        wgq_d = nc.dram_tensor("wgq", [I1, D], BF16, kind="ExternalInput").ap()
        wuq_d = nc.dram_tensor("wuq", [I1, D], BF16, kind="ExternalInput").ap()
        wdq_d = nc.dram_tensor("wdq", [D, I1], BF16, kind="ExternalInput").ap()
        ident_d = nc.dram_tensor("ident", [128, 128], BF16, kind="ExternalInput").ap()
        xout = nc.dram_tensor("xout", [T, D], F32, kind="ExternalOutput").ap()
    else:
        xacc, wgq_d, wuq_d, wdq_d, ident_d, xout = io

    ND = D // 128
    NKF = I1 // 128
    ITS = [(0, 512), (512, 512), (1024, 384), (1408, 512), (1920, 512), (2432, 384)]
    NIT = len(ITS)
    I1H2 = I1 // 2

    wgqT = [p.sb(f"wgqT{c}", [128, I1], BF16) for c in range(ND)]
    wuqT = [p.sb(f"wuqT{c}", [128, I1], BF16) for c in range(ND)]
    wdqT = [p.sb(f"wdqT{k}", [128, D], BF16) for k in range(NKF)]

    QW = 704
    qe = QuantEmitter(p, QW)

    def qemit(q, vinc, vwait, src_ap, dst_ap, cols):
        c0 = 0
        while c0 < cols:
            w = min(QW, cols - c0)
            qe.emit(q, vinc, vwait, src_ap[:, c0:c0 + w],
                    dst_ap[:, c0:c0 + w], w, vcur=None)
            c0 += w

    xs = [p.sb(f"xs{b}", [128, D], F32) for b in range(2)]
    xnf = p.sb("xnf", [128, D], F32)
    xq = [p.sb(f"xq{b}", [128, D], BF16) for b in range(2)]
    xqT = [p.sb(f"xqT{c}", [128, 256], BF16) for c in range(ND)]
    sgf = p.sb("sgf", [128, 512], F32)
    hf = p.sb("hf", [128, I1H2], F32)
    hq = [p.sb("hq0", [128, I1], BF16)] * 2
    hqT = [[p.sb(f"hqT{b}_{k}", [128, 128], BF16) for k in range(NKF)]
           for b in range(2)]
    xr = [p.sb(f"xr{b}", [128, 512], F32) for b in range(2)]
    xo = [p.sb(f"xo{b}", [128, 512], F32) for b in range(2)]
    ident = p.sb("identt", [128, 128], BF16)
    stt = {nm: [p.sb(f"st_{nm}{i}", [128, 1], F32) for i in range(4)]
           for nm in ["s1", "s2", "mu", "v1", "m2", "var", "ln", "rs", "nm"]}

    ps_tp = [p.ps(f"ps_tp{b}", [128, 128], BF16) for b in range(2)]
    ps_g = [p.ps(f"ps_g{b}", [128, 512], F32) for b in range(2)]
    ps_u = [p.ps(f"ps_u{b}", [128, 512], F32) for b in range(2)]
    ps_o = [p.ps(f"ps_o{b}", [128, 512], F32) for b in range(2)]

    s = {nm: p.sem(nm) for nm in [
        "tw", "li",
        "ldp0", "ldp1", "lrp0", "lrp1",
        "ve", "vv", "sc", "pe", "stp0", "stp1", "vs", "ss"]}

    WEV = 0
    NTW = 16 * (2 * ND + NKF)

    SUBV = 1 + 8 + NIT + 2 + NKF + 2
    VE = 4 * SUBV

    def ve_qx(si, i):
        return WEV + VE * si + SUBV * i + 1

    def ve_evx(si, i, c):
        return WEV + VE * si + SUBV * i + 1 + c + 1

    def ve_hf(si, i, it):
        return WEV + VE * si + SUBV * i + 9 + it + (1 if it < 3 else 2)

    def ve_qh(si, i, hh):
        return WEV + VE * si + SUBV * i + 13 + 4 * hh

    def ve_evh(si, i, k):
        return WEV + VE * si + SUBV * i + 11 + NIT + k + 1

    def ve_res(si, i, dn):
        return WEV + VE * si + SUBV * i + 11 + NIT + NKF + dn + 1

    SUBS = 2 + NIT
    SC = 4 * SUBS

    def sc_sq(si, i):
        return SC * si + SUBS * i + 1

    def sc_rs(si, i):
        return SC * si + SUBS * i + 2

    def sc_si(si, i, it):
        return SC * si + SUBS * i + 2 + it + 1

    SUBP = 8 + 2 * NIT + NKF + 2
    PE = 4 * SUBP

    def pe_tpx(si, i, c):
        return WEV + PE * si + SUBP * i + c + 1

    def pe_g(si, i, it):
        return WEV + PE * si + SUBP * i + 8 + 2 * it + 1

    def pe_u(si, i, it):
        return WEV + PE * si + SUBP * i + 8 + 2 * it + 2

    def pe_tph(si, i, k):
        return WEV + PE * si + SUBP * i + 8 + 2 * NIT + k + 1

    def pe_o(si, i, dn):
        return WEV + PE * si + SUBP * i + 8 + 2 * NIT + NKF + dn + 1

    tp_evac_ve = []
    for si in range(NT512):
        for i in range(4):
            for c in range(ND):
                tp_evac_ve.append(ve_evx(si, i, c))
            for k in range(NKF):
                tp_evac_ve.append(ve_evh(si, i, k))

    with p.block() as blk:

        def sync_f(q):
            q.dma_start(out=xs[0][:], in_=xacc[0:128, :]).then_inc(s["ldp0"], 16)
            q.dma_start(out=xs[1][:], in_=xacc[128:256, :]).then_inc(s["ldp1"], 16)
            q.dma_start(out=ident[:], in_=ident_d).then_inc(s["li"], 16)
            for c in range(ND):
                q.dma_start(out=wgqT[c][:], in_=wgq_d[:, c * 128:(c + 1) * 128],
                            transpose=True).then_inc(s["tw"], 16)
            for c in range(ND):
                q.dma_start(out=wuqT[c][:], in_=wuq_d[:, c * 128:(c + 1) * 128],
                            transpose=True).then_inc(s["tw"], 16)
            for k in range(NKF):
                q.dma_start(out=wdqT[k][:], in_=wdq_d[:, k * 128:(k + 1) * 128],
                            transpose=True).then_inc(s["tw"], 16)
            for si in range(NT512):
                for i in range(4):
                    g = si * 4 + i
                    if g >= 2:
                        q.wait_ge(s["ve"], ve_qx((g - 2) // 4, (g - 2) % 4))
                        q.dma_start(out=xs[g % 2][:],
                                    in_=xacc[g * 128:(g + 1) * 128, :],
                                    ).then_inc(s[f"ldp{g % 2}"], 16)
                    for dn in range(2):
                        gj = g * 2 + dn
                        if gj >= 2:
                            pj = gj - 2
                            q.wait_ge(s["ve"], ve_res(pj // 8, (pj % 8) // 2, pj % 2))
                        q.dma_start(
                            out=xr[gj % 2][:],
                            in_=xacc[g * 128:(g + 1) * 128, dn * 512:(dn + 1) * 512],
                        ).then_inc(s[f"lrp{gj % 2}"], 16)
                        if gj >= 2:
                            pj = gj - 2
                            q.dma_start(
                                out=xout[(pj // 2) * 128:(pj // 2 + 1) * 128,
                                         (pj % 2) * 512:(pj % 2 + 1) * 512],
                                in_=xo[pj % 2][:]).then_inc(s[f"stp{pj % 2}"], 16)
            for gj in range(NT128 * 2 - 2, NT128 * 2):
                q.wait_ge(s["ve"], ve_res(gj // 8, (gj % 8) // 2, gj % 2))
                q.dma_start(
                    out=xout[(gj // 2) * 128:(gj // 2 + 1) * 128,
                             (gj % 2) * 512:(gj % 2 + 1) * 512],
                    in_=xo[gj % 2][:]).then_inc(s[f"stp{gj % 2}"], 16)
            q.wait_ge(s["stp0"], 16 * NT128)
            q.wait_ge(s["stp1"], 16 * NT128)

        def vector_f(q):
            vsn = [0]

            def vinc(ins):
                ins.then_inc(s["vs"], 1)
                vsn[0] += 1
                return vsn[0]

            def vwait(n):
                if n > 0:
                    q.wait_ge(s["vs"], n)

            for si in range(NT512):
                for i in range(4):
                    g = si * 4 + i
                    q.wait_ge(s[f"ldp{g % 2}"], 16 * (g // 2 + 1))
                    a = vinc(q.reduce_sum(stt["s1"][i][:], xs[g % 2][:], AX.X))
                    vwait(a)
                    b = vinc(q.tensor_scalar(stt["mu"][i][:], stt["s1"][i][:],
                                             1.0 / D, None, ALU.mult))
                    q.wait_ge(s["sc"], sc_sq(si, i))
                    cviv = vinc(q.tensor_scalar(stt["v1"][i][:], stt["s2"][i][:],
                                                1.0 / D, 1e-5, ALU.mult, ALU.add))
                    vwait(b)
                    d = vinc(q.scalar_tensor_tensor(stt["m2"][i][:], stt["mu"][i][:],
                                                    -1.0, stt["mu"][i][:],
                                                    ALU.mult, ALU.mult))
                    vwait(d)
                    e = vinc(q.tensor_tensor(stt["var"][i][:], stt["v1"][i][:],
                                             stt["m2"][i][:], ALU.add))
                    vwait(e)
                    q.sem_inc(s["vv"], 1)
                    q.wait_ge(s["sc"], sc_rs(si, i))
                    f = vinc(q.scalar_tensor_tensor(stt["nm"][i][:], stt["mu"][i][:],
                                                    -1.0, stt["rs"][i][:],
                                                    ALU.mult, ALU.mult))
                    vwait(f)
                    gg = vinc(q.tensor_scalar(
                        xnf[:], xs[g % 2][:], stt["rs"][i].ap()[:, 0:1],
                        stt["nm"][i].ap()[:, 0:1], ALU.mult, ALU.add))
                    vwait(gg)
                    if g >= 2:
                        q.wait_ge(s["pe"], pe_tpx((g - 2) // 4, (g - 2) % 4, 7))
                    qemit(q, vinc, vwait, xnf.ap(), xq[g % 2].ap(), D)
                    q.sem_inc(s["ve"], 1)
                    for c in range(ND):
                        q.wait_ge(s["pe"], pe_tpx(si, i, c))
                        if c == 0 and g >= 2:
                            q.wait_ge(s["pe"], pe_u((g - 2) // 4, (g - 2) % 4, NIT - 1))
                        tot = g * (8 + NKF) + c
                        q.tensor_copy(xqT[c].ap()[:, (i % 2) * 128:(i % 2 + 1) * 128],
                                      ps_tp[tot % 2][:]).then_inc(s["ve"], 1)
                    for hh2 in range(2):
                        for it in range(3 * hh2, 3 * hh2 + 3):
                            c0, w = ITS[it]
                            q.wait_ge(s["sc"], sc_si(si, i, it))
                            pit = g * NIT + it
                            q.wait_ge(s["pe"], pe_u(si, i, it))
                            vwait(vsn[0])
                            hv = vinc(q.tensor_tensor(
                                hf.ap()[:, c0 - hh2 * I1H2:c0 - hh2 * I1H2 + w],
                                sgf.ap()[:, 0:w],
                                ps_u[pit % 2][:, 0:w], ALU.mult))
                            vwait(hv)
                            q.sem_inc(s["ve"], 1)
                        if hh2 == 0 and g >= 1:
                            q.wait_ge(s["pe"], pe_tph((g - 1) // 4, (g - 1) % 4, NKF - 1))
                        qemit(q, vinc, vwait, hf.ap()[:, 0:I1H2],
                              hq[0].ap()[:, hh2 * I1H2:(hh2 + 1) * I1H2], I1H2)
                        q.sem_inc(s["ve"], 1)
                    for k in range(NKF):
                        q.wait_ge(s["pe"], pe_tph(si, i, k))
                        if k == 0 and g >= 2:
                            q.wait_ge(s["pe"], pe_o((g - 2) // 4, (g - 2) % 4, 1))
                        tot = g * (8 + NKF) + 8 + k
                        q.tensor_copy(hqT[g % 2][k][:],
                                      ps_tp[tot % 2][:]).then_inc(s["ve"], 1)
                    for dn in range(2):
                        gj = g * 2 + dn
                        q.wait_ge(s["pe"], pe_o(si, i, dn))
                        q.wait_ge(s[f"lrp{gj % 2}"], 16 * (gj // 2 + 1))
                        if gj >= 2:
                            q.wait_ge(s[f"stp{gj % 2}"], 16 * ((gj - 2) // 2 + 1))
                        q.tensor_tensor(xo[gj % 2][:], xr[gj % 2][:],
                                        ps_o[dn % 2][:], ALU.add).then_inc(s["ve"], 1)

        def scalar_f(q):
            ssn = [0]

            def sinc(ins):
                ins.then_inc(s["ss"], 1)
                ssn[0] += 1
                return ssn[0]

            for si in range(NT512):
                for i in range(4):
                    g = si * 4 + i
                    q.wait_ge(s[f"ldp{g % 2}"], 16 * (g // 2 + 1))
                    prior = SC * si + SUBS * i
                    if prior > 0:
                        q.wait_ge(s["sc"], prior)
                    q.activation(xnf[:], xs[g % 2][:], AF.Square,
                                 accum_out=stt["s2"][i][:]).then_inc(s["sc"], 1)
                    q.wait_ge(s["vv"], g + 1)
                    n0 = sinc(q.activation(stt["ln"][i][:], stt["var"][i][:], AF.Ln))
                    q.wait_ge(s["ss"], n0)
                    q.activation(stt["rs"][i][:], stt["ln"][i][:], AF.Exp,
                                 scale=-0.5).then_inc(s["sc"], 1)
                    for it, (c0, w) in enumerate(ITS):
                        q.wait_ge(s["pe"], pe_g(si, i, it))
                        pit = g * NIT + it
                        if pit >= 1:
                            pp = pit - 1
                            q.wait_ge(s["ve"], ve_hf(pp // (4 * NIT),
                                                     (pp % (4 * NIT)) // NIT,
                                                     pp % NIT))
                        q.activation(sgf.ap()[:, 0:w], ps_g[pit % 2][:, 0:w],
                                     AF.Silu).then_inc(s["sc"], 1)

        def tensor_f(q):
            q.wait_ge(s["li"], 16)
            q.wait_ge(s["tw"], 16 * 2 * ND)
            for si in range(NT512):
                for i in range(4):
                    g = si * 4 + i
                    q.wait_ge(s["ve"], ve_qx(si, i))
                    for c in range(ND):
                        tot = g * (8 + NKF) + c
                        if tot >= 2:
                            q.wait_ge(s["ve"], tp_evac_ve[tot - 2])
                        q.transpose(ps_tp[tot % 2][:],
                                    xq[g % 2].ap()[:, c * 128:(c + 1) * 128],
                                    ident[:]).then_inc(s["pe"], 1)
                    for it, (c0, w) in enumerate(ITS):
                        pit = g * NIT + it
                        if pit >= 2:
                            pp = pit - 2
                            q.wait_ge(s["sc"], sc_si(pp // (4 * NIT),
                                                     (pp % (4 * NIT)) // NIT,
                                                     pp % NIT))
                        q.wait_ge(s["ve"], ve_evx(si, i, 7))
                        for c in range(ND):
                            ins = q.matmul(ps_g[pit % 2][:, 0:w],
                                           xqT[c].ap()[:, (i % 2) * 128:(i % 2 + 1) * 128],
                                           wgqT[c].ap()[:, c0:c0 + w],
                                           start=(c == 0), stop=(c == ND - 1))
                            if c == ND - 1:
                                ins.then_inc(s["pe"], 1)
                        if pit >= 2:
                            pp = pit - 2
                            q.wait_ge(s["ve"], ve_hf(pp // (4 * NIT),
                                                     (pp % (4 * NIT)) // NIT,
                                                     pp % NIT))
                        for c in range(ND):
                            ins = q.matmul(ps_u[pit % 2][:, 0:w],
                                           xqT[c].ap()[:, (i % 2) * 128:(i % 2 + 1) * 128],
                                           wuqT[c].ap()[:, c0:c0 + w],
                                           start=(c == 0), stop=(c == ND - 1))
                            if c == ND - 1:
                                ins.then_inc(s["pe"], 1)
                    for k in range(NKF):
                        if k < I1H2 // 128:
                            q.wait_ge(s["ve"], ve_qh(si, i, 0))
                        else:
                            q.wait_ge(s["ve"], ve_qh(si, i, 1))
                        tot = g * (8 + NKF) + 8 + k
                        q.wait_ge(s["ve"], tp_evac_ve[tot - 2])
                        q.transpose(ps_tp[tot % 2][:],
                                    hq[0].ap()[:, k * 128:(k + 1) * 128],
                                    ident[:]).then_inc(s["pe"], 1)
                    for dn in range(2):
                        if g == 0 and dn == 0:
                            q.wait_ge(s["tw"], NTW)
                        q.wait_ge(s["ve"], ve_evh(si, i, NKF - 1))
                        gj = g * 2 + dn
                        if gj >= 2:
                            pj = gj - 2
                            q.wait_ge(s["ve"], ve_res(pj // 8, (pj % 8) // 2, pj % 2))
                        for k in range(NKF):
                            ins = q.matmul(ps_o[dn % 2][:],
                                           hqT[g % 2][k][:],
                                           wdqT[k].ap()[:, dn * 512:(dn + 1) * 512],
                                           start=(k == 0), stop=(k == NKF - 1))
                            if k == NKF - 1:
                                ins.then_inc(s["pe"], 1)

        blk.sync(sync_f)
        blk.vector(vector_f)
        blk.scalar(scalar_f)
        blk.tensor(tensor_f)

    return nc


def build_attn(p: Prog, io=None):
    nc = p.nc
    if io is None:
        xin = nc.dram_tensor("xin", [T, D], F32, kind="ExternalInput").ap()
        wq_d = nc.dram_tensor("wq", [D, D], F32, kind="ExternalInput").ap()
        wk_d = nc.dram_tensor("wk", [D, D], F32, kind="ExternalInput").ap()
        wv_d = nc.dram_tensor("wv", [D, D], F32, kind="ExternalInput").ap()
        wo_d = nc.dram_tensor("wo", [D, D], F32, kind="ExternalInput").ap()
        bcol_d = nc.dram_tensor("bcol", [128, 2 * 8], F32, kind="ExternalInput").ap()
        bvb_d = nc.dram_tensor("bvb", [128, D], F32, kind="ExternalInput").ap()
        bob_d = nc.dram_tensor("bob", [128, D], F32, kind="ExternalInput").ap()
        cos_d = nc.dram_tensor("cos_t", [128, T], BF16, kind="ExternalInput").ap()
        sin_d = nc.dram_tensor("sin_t", [128, T], BF16, kind="ExternalInput").ap()
        msk_d = nc.dram_tensor("msk", [128, 4 * 512], BF16, kind="ExternalInput").ap()
        ident_d = nc.dram_tensor("ident", [128, 128], BF16, kind="ExternalInput").ap()
        negC_d = nc.dram_tensor("negC", [128, 1], F32, kind="ExternalInput").ap()
        ones_d = nc.dram_tensor("ones_r", [1, 64], F32, kind="ExternalInput").ap()
        xout = nc.dram_tensor("xout", [T, D], F32, kind="ExternalOutput").ap()
    else:
        (xin, wq_d, wk_d, wv_d, wo_d, bcol_d, bvb_d, bob_d,
         cos_d, sin_d, msk_d, ident_d, negC_d, ones_d, xout) = io

    ND = D // 128
    NQT = 8
    q_scr = nc.dram_tensor("q_scr", [D, T], BF16, kind="Internal").ap()
    k_scr = nc.dram_tensor("k_scr", [D, T], BF16, kind="Internal").ap()
    v_scr = nc.dram_tensor("v_scr", [T, NH * 65], BF16, kind="Internal").ap()

    wqT = [p.sb(f"wqT{c}", [128, D], BF16) for c in range(ND)]
    wkT = [p.sb(f"wkT{c}", [128, D], BF16) for c in range(ND)]
    wvT = [p.sb(f"wvT{c}", [128, D], BF16) for c in range(ND)]
    bcol = p.sb("bcol_s", [128, 2 * 8], F32)
    bvb = p.sb("bvb_s", [128, D], F32)
    bob = p.sb("bob_s", [128, D], F32)
    cos_t = p.sb("cos_sb", [128, T], BF16)
    sin_t = p.sb("sin_sb", [128, T], BF16)
    xs = [p.sb(f"xs{b}", [128, D], F32) for b in range(2)]
    xn = p.sb("xn", [128, D], BF16)
    xnT = [p.sb(f"xnT{c}", [128, 512], BF16) for c in range(ND)]
    qb = p.sb("qb", [128, 512], F32)
    mm1 = p.sb("mm1", [128, 512], BF16)
    mm2 = p.sb("mm2", [128, 512], BF16)
    qro = [p.sb(f"qro{b}", [128, 512], BF16) for b in range(4)]
    vst = [p.sb(f"vst{b}", [128, 520], BF16) for b in range(2)]
    stt = {nm: [p.sb(f"st_{nm}{i}", [128, 1], F32) for i in range(4)]
           for nm in ["s1", "s2", "mu", "v1", "m2", "var", "ln", "rs", "nm"]}
    scr = p.sb("scr", [128, D], BF16)
    ident = p.sb("identt", [128, 128], BF16)
    negC = p.sb("negC_s", [128, 1], F32)
    onesr = p.sb("onesr", [1, 64], F32)

    qtl = [[p.sb(f"qtl{b}_{c}", [128, 512], BF16) for c in range(ND)]
           for b in range(2)]
    ktl = [p.sb(f"ktl{b}", [128, 512], BF16) for b in range(2)]
    vtl = [p.sb(f"vtl{b}", [128, 130], BF16) for b in range(4)]
    pt = p.sb("pt", [128, 2048], BF16)
    pm = [p.sb(f"pm{b}", [128, 512], BF16) for b in range(4)]
    msk = p.sb("msk_sb", [128, 4 * 512], BF16)
    yall = [p.sb(f"yall{hp}", [128, 512], BF16) for hp in range(ND)]
    dden = [p.sb(f"dden{h}", [1, 512], F32) for h in range(2)]
    lnscr = p.sb("lnscr", [64, 512], F32)
    bcb = p.sb("bcb", [64, 1024], F32)
    woT = [p.sb(f"woT{c}", [128, D], BF16) for c in range(ND)]
    xr = [p.sb(f"xr{b}", [128, 512], F32) for b in range(2)]
    xo = [p.sb(f"xo{b}", [128, 512], F32) for b in range(2)]

    psA = p.ps("psA", [128, 2048], F32)
    psB = p.ps("psB", [128, 2048], F32)
    psB_bf = psB.ap().bitcast(BF16)

    def tp_view(tot):
        b = tot % 2
        return psB_bf[:, b * 1024:b * 1024 + 128]

    def qk_view(aidx):
        b = aidx % 2
        return psA.ap()[:, b * 512:(b + 1) * 512]

    s_view = psA.ap()
    y_view = [psB.ap()[0:65, 0:512], psB.ap()[0:65, 512:1024]]
    bc_view = psB.ap()[0:64, 1024:1536]
    o_view = psB.ap()[:, 1536:2048]

    s = {nm: p.sem(nm) for nm in [
        "wl", "li",
        "ldp0", "ldp1", "lq", "lk0", "lk1", "lk2", "lk3",
        "lv0", "lv1", "lv2", "lv3", "lrp0", "lrp1",
        "ve", "vv", "sc", "pe",
        "sq0", "sq1", "sk0", "sk1", "sv0", "sv1", "stp0", "stp1", "vs", "ss"]}

    VEA = 60

    def ve_ln(si, i):
        return VEA * si + 9 * i + 1

    def ve_ev(si, i, c):
        return VEA * si + 9 * i + 1 + c + 1

    def ve_qr(si, oc):
        return VEA * si + 36 + 2 * oc + 1

    def ve_kr(si, oc):
        return VEA * si + 36 + 2 * oc + 2

    def ve_vv(si, j):
        return VEA * si + 52 + j + 1

    VEA_TOT = VEA * NT512

    SCA = 8

    def sc_sq(si, i):
        return SCA * si + 2 * i + 1

    def sc_rs(si, i):
        return SCA * si + 2 * i + 2

    SCA_TOT = SCA * NT512

    PEA = 56

    def pe_tp(si, j):
        return PEA * si + j + 1

    def pe_q(si, oc):
        return PEA * si + 32 + 2 * oc + 1

    def pe_k(si, oc):
        return PEA * si + 32 + 2 * oc + 2

    def pe_v(si, j):
        return PEA * si + 48 + j + 1

    PEA_TOT = PEA * NT512

    def aidx_q(si, oc):
        return si * 24 + 2 * oc

    def aidx_k(si, oc):
        return si * 24 + 2 * oc + 1

    def aidx_v(si, j):
        return si * 24 + 16 + j

    def a_consumer_ve(aidx):
        si, r = aidx // 24, aidx % 24
        if r < 16:
            return ve_qr(si, r // 2) if r % 2 == 0 else ve_kr(si, r // 2)
        return ve_vv(si, r - 16)

    def NKB(qt):
        return 4 * (qt + 1)

    def NG(qt):
        return 2 * (qt + 1)

    def diag_idx(qt, kb):
        off = kb * 128 - qt * 512
        if off < 0:
            return None
        return off // 128

    peb, scb, veb = {}, {}, {}
    pe_c, sc_c, ve_c = [PEA_TOT], [SCA_TOT], [VEA_TOT]
    kb_list = []
    for qt in range(NQT):
        for hp in range(ND):
            for kb in range(NKB(qt)):
                kb_list.append((qt, hp, kb))
                pe_c[0] += 1
                peb[("s", qt, hp, kb)] = pe_c[0]
                sc_c[0] += 1
                scb[("e", qt, hp, kb)] = sc_c[0]
                nm = 2 if diag_idx(qt, kb) is not None else 0
                ve_c[0] += nm
                veb[("m", qt, hp, kb)] = ve_c[0]
            sc_c[0] += 1
            scb[("rcA", qt, hp)] = sc_c[0]
            sc_c[0] += 1
            scb[("rcB", qt, hp)] = sc_c[0]
            pe_c[0] += 1
            peb[("yA", qt, hp)] = pe_c[0]
            pe_c[0] += 1
            peb[("yB", qt, hp)] = pe_c[0]
            ve_c[0] += 1
            veb[("dA", qt, hp)] = ve_c[0]
            ve_c[0] += 1
            veb[("dB", qt, hp)] = ve_c[0]
            pe_c[0] += 1
            peb[("bcA", qt, hp)] = pe_c[0]
            ve_c[0] += 1
            veb[("bcbA", qt, hp)] = ve_c[0]
            pe_c[0] += 1
            peb[("bcB", qt, hp)] = pe_c[0]
            ve_c[0] += 1
            veb[("bcbB", qt, hp)] = ve_c[0]
            ve_c[0] += 1
            veb[("yA", qt, hp)] = ve_c[0]
            ve_c[0] += 1
            veb[("yB", qt, hp)] = ve_c[0]
        for j in range(8):
            pe_c[0] += 1
            peb[("o", qt, j)] = pe_c[0]
            ve_c[0] += 1
            veb[("bo", qt, j)] = ve_c[0]
            ve_c[0] += 1
            veb[("res", qt, j)] = ve_c[0]

    def kb_done_pe(qt, hp, kb):
        if kb + 2 < NKB(qt):
            return peb[("s", qt, hp, kb + 2)]
        return peb[("yB", qt, hp)]

    kcons = []
    for gk, (qt, hp, kb) in enumerate(kb_list):
        kcons.append(kb_done_pe(qt, hp, kb))
    kq_list = []
    kqcons = []
    for qt in range(NQT):
        for hp in range(ND):
            for kq in range(NKB(qt) // 4):
                kq_list.append((qt, hp, kq))
                last_kb = kq * 4 + 3
                kqcons.append(kb_done_pe(qt, hp, last_kb))

    NWL = 16 * (4 * ND + 8)

    with p.block() as blk:

        def gpsimd_f(q):
            for c in range(ND):
                q.dma_start(out=wqT[c][:], in_=wq_d[c * 128:(c + 1) * 128, :]).then_inc(s["wl"], 16)
            for c in range(ND):
                q.dma_start(out=wkT[c][:], in_=wk_d[c * 128:(c + 1) * 128, :]).then_inc(s["wl"], 16)
            for c in range(ND):
                q.dma_start(out=wvT[c][:], in_=wv_d[c * 128:(c + 1) * 128, :]).then_inc(s["wl"], 16)
            for c in range(ND):
                q.dma_start(out=woT[c][:], in_=wo_d[c * 128:(c + 1) * 128, :]).then_inc(s["wl"], 16)
            q.dma_start(out=bcol[:], in_=bcol_d).then_inc(s["wl"], 16)
            q.dma_start(out=bvb[:], in_=bvb_d).then_inc(s["wl"], 16)
            q.dma_start(out=bob[:], in_=bob_d).then_inc(s["wl"], 16)
            q.dma_start(out=cos_t[:], in_=cos_d).then_inc(s["wl"], 16)
            q.dma_start(out=sin_t[:], in_=sin_d).then_inc(s["wl"], 16)
            q.dma_start(out=msk[:], in_=msk_d).then_inc(s["wl"], 16)
            q.dma_start(out=negC[:], in_=negC_d).then_inc(s["wl"], 16)
            q.dma_start(out=onesr[:], in_=ones_d).then_inc(s["wl"], 16)
            q.wait_ge(s["sk0"], 16 * 32)
            q.wait_ge(s["sk1"], 16 * 32)
            q.wait_ge(s["sv0"], 16 * 32)
            q.wait_ge(s["sv1"], 16 * 32)
            gk = 0
            gkk = 0
            for qt in range(NQT):
                for hp in range(ND):
                    for kq in range(NKB(qt) // 4):
                        if gkk >= 2:
                            q.wait_ge(s["pe"], kqcons[gkk - 2])
                        q.dma_start(out=ktl[gkk % 2][:],
                                    in_=k_scr[hp * 128:(hp + 1) * 128,
                                              kq * 512:(kq + 1) * 512],
                                    ).then_inc(s[f"lk{gkk % 2}"], 16)
                        gkk += 1
                        for kb in range(kq * 4, kq * 4 + 4):
                            b4 = gk % 4
                            if gk >= 4:
                                q.wait_ge(s["pe"], kcons[gk - 4])
                            q.dma_start(out=vtl[b4][:],
                                        in_=v_scr[kb * 128:(kb + 1) * 128,
                                                  hp * 130:(hp + 1) * 130],
                                        ).then_inc(s[f"lv{b4}"], 16)
                            gk += 1

        def sync_f(q):
            q.dma_start(out=ident[:], in_=ident_d).then_inc(s["li"], 16)
            for si in range(NT512):
                for i in range(4):
                    g = si * 4 + i
                    if g >= 2:
                        q.wait_ge(s["ve"], ve_ln((g - 2) // 4, (g - 2) % 4))
                    q.dma_start(out=xs[g % 2][:],
                                in_=xin[g * 128:(g + 1) * 128, :]).then_inc(s[f"ldp{g % 2}"], 16)
                for oc in range(ND):
                    gq = si * ND + oc
                    q.wait_ge(s["ve"], ve_qr(si, oc))
                    q.dma_start(out=q_scr[oc * 128:(oc + 1) * 128, si * 512:(si + 1) * 512],
                                in_=qro[gq % 2][:]).then_inc(s[f"sq{gq % 2}"], 16)
                    q.wait_ge(s["ve"], ve_kr(si, oc))
                    q.dma_start(out=k_scr[oc * 128:(oc + 1) * 128, si * 512:(si + 1) * 512],
                                in_=qro[2 + gq % 2][:]).then_inc(s[f"sk{gq % 2}"], 16)
                for j in range(8):
                    gv = si * 8 + j
                    sub, ot = j // 2, j % 2
                    q.wait_ge(s["ve"], ve_vv(si, j))
                    q.dma_start(
                        out=v_scr[(si * 4 + sub) * 128:(si * 4 + sub + 1) * 128,
                                  ot * 520:(ot + 1) * 520],
                        in_=vst[gv % 2][:]).then_inc(s[f"sv{gv % 2}"], 16)
            for nm2 in ["sq0", "sq1", "sk0", "sk1", "sv0", "sv1"]:
                q.wait_ge(s[nm2], 16 * 32)
            for qt in range(NQT):
                for c in range(ND):
                    if qt >= 2:
                        q.wait_ge(s["ve"], veb[("res", qt - 2, 7)])
                    q.dma_start(out=qtl[qt % 2][c][:],
                                in_=q_scr[c * 128:(c + 1) * 128,
                                          qt * 512:(qt + 1) * 512]).then_inc(s["lq"], 16)
                for j in range(8):
                    gj = qt * 8 + j
                    qs, dn = j // 2, j % 2
                    if gj >= 2:
                        pj = gj - 2
                        q.wait_ge(s["ve"], veb[("res", pj // 8, pj % 8)])
                    q.dma_start(out=xr[gj % 2][:],
                                in_=xin[(qt * 4 + qs) * 128:(qt * 4 + qs + 1) * 128,
                                        dn * 512:(dn + 1) * 512],
                                ).then_inc(s[f"lrp{gj % 2}"], 16)
                    if gj >= 2:
                        pj = gj - 2
                        q.wait_ge(s["ve"], veb[("res", pj // 8, pj % 8)])
                        q.dma_start(
                            out=xout[(pj // 2) * 128:(pj // 2 + 1) * 128,
                                     (pj % 2) * 512:(pj % 2 + 1) * 512],
                            in_=xo[pj % 2][:]).then_inc(s[f"stp{pj % 2}"], 16)
            for gj in range(NQT * 8 - 2, NQT * 8):
                q.wait_ge(s["ve"], veb[("res", gj // 8, gj % 8)])
                q.dma_start(
                    out=xout[(gj // 2) * 128:(gj // 2 + 1) * 128,
                             (gj % 2) * 512:(gj % 2 + 1) * 512],
                    in_=xo[gj % 2][:]).then_inc(s[f"stp{gj % 2}"], 16)
            q.wait_ge(s["stp0"], 16 * NT128)
            q.wait_ge(s["stp1"], 16 * NT128)

        def vector_f(q):
            vsn = [0]

            def vinc(ins):
                ins.then_inc(s["vs"], 1)
                vsn[0] += 1
                return vsn[0]

            def vwait(n):
                if n > 0:
                    q.wait_ge(s["vs"], n)

            def emit_rope(psrc, bcol_col, si, dst):
                a = vinc(q.tensor_scalar(qb[:], psrc, bcol.ap()[:, bcol_col:bcol_col + 1],
                                         None, ALU.add))
                vwait(a)
                cs = cos_t.ap()[:, si * 512:(si + 1) * 512]
                sn = sin_t.ap()[:, si * 512:(si + 1) * 512]
                b = vinc(q.tensor_tensor(mm1[:], qb[:], cs, ALU.mult))
                last = b
                for h2 in range(2):
                    for hh in range(2):
                        p0 = h2 * 64 + hh * 32
                        psw = h2 * 64 + (1 - hh) * 32
                        last = vinc(q.tensor_tensor(mm2.ap()[p0:p0 + 32, :],
                                                    qb.ap()[psw:psw + 32, :],
                                                    sn[psw:psw + 32, :], ALU.mult))
                vwait(last)
                dd = vinc(q.tensor_tensor(dst, mm1[:], mm2[:], ALU.add))
                vwait(dd)

            for si in range(NT512):
                for i in range(4):
                    g = si * 4 + i
                    q.wait_ge(s[f"ldp{g % 2}"], 16 * (g // 2 + 1))
                    a = vinc(q.reduce_sum(stt["s1"][i][:], xs[g % 2][:], AX.X))
                    vwait(a)
                    b = vinc(q.tensor_scalar(stt["mu"][i][:], stt["s1"][i][:],
                                             1.0 / D, None, ALU.mult))
                    q.wait_ge(s["sc"], sc_sq(si, i))
                    cv = vinc(q.tensor_scalar(stt["v1"][i][:], stt["s2"][i][:],
                                              1.0 / D, 1e-5, ALU.mult, ALU.add))
                    vwait(b)
                    dv = vinc(q.scalar_tensor_tensor(stt["m2"][i][:], stt["mu"][i][:],
                                                     -1.0, stt["mu"][i][:],
                                                     ALU.mult, ALU.mult))
                    vwait(dv)
                    e = vinc(q.tensor_tensor(stt["var"][i][:], stt["v1"][i][:],
                                             stt["m2"][i][:], ALU.add))
                    vwait(e)
                    q.sem_inc(s["vv"], 1)
                    q.wait_ge(s["sc"], sc_rs(si, i))
                    f = vinc(q.scalar_tensor_tensor(stt["nm"][i][:], stt["mu"][i][:],
                                                    -1.0, stt["rs"][i][:],
                                                    ALU.mult, ALU.mult))
                    if g > 0:
                        pg = g - 1
                        q.wait_ge(s["pe"], pe_tp(pg // 4, (pg % 4) * 8 + 7))
                    vwait(f)
                    q.tensor_scalar(
                        xn[:], xs[g % 2][:], stt["rs"][i].ap()[:, 0:1],
                        stt["nm"][i].ap()[:, 0:1], ALU.mult, ALU.add,
                    ).then_inc(s["ve"], 1)
                    for c in range(ND):
                        jj = i * 8 + c
                        q.wait_ge(s["pe"], pe_tp(si, jj))
                        tot = si * 32 + jj
                        q.tensor_copy(xnT[c].ap()[:, i * 128:(i + 1) * 128],
                                      tp_view(tot)).then_inc(s["ve"], 1)
                for oc in range(ND):
                    gq = si * ND + oc
                    q.wait_ge(s["pe"], pe_q(si, oc))
                    if gq >= 2:
                        q.wait_ge(s[f"sq{gq % 2}"], 16 * ((gq - 2) // 2 + 1))
                    emit_rope(qk_view(aidx_q(si, oc)), oc, si, qro[gq % 2][:])
                    q.sem_inc(s["ve"], 1)
                    q.wait_ge(s["pe"], pe_k(si, oc))
                    if gq >= 2:
                        q.wait_ge(s[f"sk{gq % 2}"], 16 * ((gq - 2) // 2 + 1))
                    emit_rope(qk_view(aidx_k(si, oc)), 8 + oc, si, qro[2 + gq % 2][:])
                    q.sem_inc(s["ve"], 1)
                for j in range(8):
                    gv = si * 8 + j
                    sub, ot = j // 2, j % 2
                    q.wait_ge(s["pe"], pe_v(si, j))
                    if gv >= 2:
                        q.wait_ge(s[f"sv{gv % 2}"], 16 * ((gv - 2) // 2 + 1))
                    hv = vinc(q.tensor_tensor(
                        vst[gv % 2].ap().rearrange("p (h w) -> p h w", w=65)[:, :, 0:64],
                        qk_view(aidx_v(si, j)).rearrange("p (h w) -> p h w", w=64),
                        bvb.ap()[:, ot * 512:(ot + 1) * 512].rearrange("p (h w) -> p h w", w=64),
                        ALU.add))
                    vwait(hv)
                    q.memset(vst[gv % 2].ap().rearrange("p (h w) -> p h w", w=65)[:, :, 64:65],
                             1.0).then_inc(s["ve"], 1)
            for qt in range(NQT):
                for hp in range(ND):
                    for kb in range(NKB(qt)):
                        mi = diag_idx(qt, kb)
                        if mi is None:
                            continue
                        q.wait_ge(s["sc"], scb[("e", qt, hp, kb)])
                        for h in range(2):
                            q.tensor_tensor(
                                pm[(kb % 2) * 2 + h][:],
                                pt.ap()[:, ((kb % 2) * 2 + h) * 512:((kb % 2) * 2 + h + 1) * 512],
                                msk.ap()[:, mi * 512:(mi + 1) * 512],
                                ALU.mult).then_inc(s["ve"], 1)
                    q.wait_ge(s["pe"], peb[("yB", qt, hp)])
                    dv = vinc(q.tensor_copy(dden[0][:], y_view[0][64:65, :]))
                    vwait(dv)
                    q.sem_inc(s["ve"], 1)
                    dv = vinc(q.tensor_copy(dden[1][:], y_view[1][64:65, :]))
                    vwait(dv)
                    q.sem_inc(s["ve"], 1)
                    q.wait_ge(s["sc"], scb[("rcA", qt, hp)])
                    q.sem_inc(s["ve"], 1)
                    q.wait_ge(s["sc"], scb[("rcB", qt, hp)])
                    q.sem_inc(s["ve"], 1)
                    if qt >= 1:
                        q.wait_ge(s["pe"], peb[("o", qt - 1, 7)])
                    yv = vinc(q.tensor_tensor(yall[hp].ap()[0:64, :], y_view[0][0:64, :],
                                              bcb.ap()[:, 0:512], ALU.mult))
                    vwait(yv)
                    q.sem_inc(s["ve"], 1)
                    yv = vinc(q.tensor_tensor(yall[hp].ap()[64:128, :], y_view[1][0:64, :],
                                              bcb.ap()[:, 512:1024], ALU.mult))
                    vwait(yv)
                    q.sem_inc(s["ve"], 1)
                for j in range(8):
                    gj = qt * 8 + j
                    qs, dn = j // 2, j % 2
                    q.wait_ge(s["pe"], peb[("o", qt, j)])
                    if gj >= 2:
                        q.wait_ge(s[f"stp{gj % 2}"], 16 * ((gj - 2) // 2 + 1))
                    ov = vinc(q.tensor_tensor(xo[gj % 2][:], o_view,
                                              bob.ap()[:, dn * 512:(dn + 1) * 512],
                                              ALU.add))
                    vwait(ov)
                    q.sem_inc(s["ve"], 1)
                    q.wait_ge(s[f"lrp{gj % 2}"], 16 * (gj // 2 + 1))
                    o2 = vinc(q.tensor_tensor(xo[gj % 2][:], xo[gj % 2][:],
                                              xr[gj % 2][:], ALU.add))
                    vwait(o2)
                    q.sem_inc(s["ve"], 1)

        def scalar_f(q):
            ssn = [0]

            def sinc(ins):
                ins.then_inc(s["ss"], 1)
                ssn[0] += 1
                return ssn[0]

            for si in range(NT512):
                for i in range(4):
                    g = si * 4 + i
                    q.wait_ge(s[f"ldp{g % 2}"], 16 * (g // 2 + 1))
                    prior = SCA * si + 2 * i
                    if prior > 0:
                        q.wait_ge(s["sc"], prior)
                    q.activation(scr[:], xs[g % 2][:], AF.Square,
                                 accum_out=stt["s2"][i][:]).then_inc(s["sc"], 1)
                    q.wait_ge(s["vv"], g + 1)
                    n0 = sinc(q.activation(stt["ln"][i][:], stt["var"][i][:], AF.Ln))
                    q.wait_ge(s["ss"], n0)
                    q.activation(stt["rs"][i][:], stt["ln"][i][:], AF.Exp,
                                 scale=-0.5).then_inc(s["sc"], 1)
            for qt in range(NQT):
                for hp in range(ND):
                    for kb in range(NKB(qt)):
                        q.wait_ge(s["pe"], peb[("s", qt, hp, kb)])
                        h0 = (kb % 2) * 1024
                        q.activation(pt.ap()[:, h0:h0 + 1024],
                                     s_view[:, h0:h0 + 1024], AF.Exp, scale=0.125,
                                     bias=negC.ap()[:, 0:1]).then_inc(s["sc"], 1)
                    q.wait_ge(s["pe"], peb[("bcA", qt, hp)])
                    n1 = sinc(q.activation(lnscr.ap()[0:64, 0:512], bc_view, AF.Ln))
                    q.wait_ge(s["ss"], n1)
                    q.activation(bcb.ap()[:, 0:512], lnscr.ap()[0:64, 0:512],
                                 AF.Exp, scale=-1.0).then_inc(s["sc"], 1)
                    q.wait_ge(s["pe"], peb[("bcB", qt, hp)])
                    n1 = sinc(q.activation(lnscr.ap()[0:64, 0:512], bc_view, AF.Ln))
                    q.wait_ge(s["ss"], n1)
                    q.activation(bcb.ap()[:, 512:1024], lnscr.ap()[0:64, 0:512],
                                 AF.Exp, scale=-1.0).then_inc(s["sc"], 1)

        def tensor_f(q):
            q.wait_ge(s["li"], 16)
            q.wait_ge(s["wl"], NWL)
            for si in range(NT512):
                for i in range(4):
                    q.wait_ge(s["ve"], ve_ln(si, i))
                    for c in range(ND):
                        jj = i * 8 + c
                        tot = si * 32 + jj
                        if tot >= 2:
                            pj = tot - 2
                            q.wait_ge(s["ve"], ve_ev(pj // 32, (pj % 32) // 8, pj % 8))
                        q.transpose(tp_view(tot),
                                    xn.ap()[:, c * 128:(c + 1) * 128],
                                    ident[:]).then_inc(s["pe"], 1)
                q.wait_ge(s["ve"], VEA * si + 36)
                for oc in range(ND):
                    ai = aidx_q(si, oc)
                    if ai >= 2:
                        q.wait_ge(s["ve"], a_consumer_ve(ai - 2))
                    for c in range(ND):
                        ins = q.matmul(qk_view(ai),
                                       wqT[c].ap()[:, oc * 128:(oc + 1) * 128],
                                       xnT[c][:], start=(c == 0), stop=(c == ND - 1))
                        if c == ND - 1:
                            ins.then_inc(s["pe"], 1)
                    ai = aidx_k(si, oc)
                    if ai >= 2:
                        q.wait_ge(s["ve"], a_consumer_ve(ai - 2))
                    for c in range(ND):
                        ins = q.matmul(qk_view(ai),
                                       wkT[c].ap()[:, oc * 128:(oc + 1) * 128],
                                       xnT[c][:], start=(c == 0), stop=(c == ND - 1))
                        if c == ND - 1:
                            ins.then_inc(s["pe"], 1)
                for j in range(8):
                    sub, ot = j // 2, j % 2
                    ai = aidx_v(si, j)
                    q.wait_ge(s["ve"], a_consumer_ve(ai - 2))
                    for c in range(ND):
                        ins = q.matmul(qk_view(ai),
                                       xnT[c].ap()[:, sub * 128:(sub + 1) * 128],
                                       wvT[c].ap()[:, ot * 512:(ot + 1) * 512],
                                       start=(c == 0), stop=(c == ND - 1))
                        if c == ND - 1:
                            ins.then_inc(s["pe"], 1)
            q.wait_ge(s["ve"], VEA_TOT)
            gk = 0
            gkk = 0
            for qt in range(NQT):
                q.wait_ge(s["lq"], 16 * 8 * (qt + 1))
                for hp in range(ND):
                    base_gk = gk
                    base_gkk = gkk

                    def emit_scores(kb):
                        kqi = base_gkk + kb // 4
                        q.wait_ge(s[f"lk{kqi % 2}"], 16 * (kqi // 2 + 1))
                        if kb >= 2:
                            q.wait_ge(s["sc"], scb[("e", qt, hp, kb - 2)])
                        elif (qt, hp) != (0, 0):
                            php = (qt, hp - 1) if hp > 0 else (qt - 1, ND - 1)
                            q.wait_ge(s["sc"], scb[("e", php[0], php[1],
                                                    NKB(php[0]) - 2 + kb)])
                        h0 = (kb % 2) * 1024
                        kcol = (kb % 4) * 128
                        for h in range(2):
                            ins = q.matmul(
                                s_view[:, h0 + h * 512:h0 + (h + 1) * 512],
                                ktl[kqi % 2].ap()[h * 64:(h + 1) * 64,
                                                  kcol:kcol + 128],
                                qtl[qt % 2][hp].ap()[h * 64:(h + 1) * 64, :],
                                start=True, stop=True)
                            if h == 1:
                                ins.then_inc(s["pe"], 1)

                    emit_scores(0)
                    if NKB(qt) > 1:
                        emit_scores(1)
                    for kb in range(NKB(qt)):
                        b4v = (base_gk + kb) % 4
                        q.wait_ge(s[f"lv{b4v}"], 16 * ((base_gk + kb) // 4 + 1))
                        mi = diag_idx(qt, kb)
                        if mi is not None:
                            q.wait_ge(s["ve"], veb[("m", qt, hp, kb)])
                        else:
                            q.wait_ge(s["sc"], scb[("e", qt, hp, kb)])
                        if kb == 0 and (qt, hp) != (0, 0):
                            php = (qt, hp - 1) if hp > 0 else (qt - 1, ND - 1)
                            q.wait_ge(s["ve"], veb[("yB", *php)])
                        h0 = (kb % 2) * 1024
                        for h in range(2):
                            if mi is not None:
                                rhs = pm[(kb % 2) * 2 + h][:]
                            else:
                                rhs = pt.ap()[:, h0 + h * 512:h0 + (h + 1) * 512]
                            ins = q.matmul(
                                y_view[h],
                                vtl[b4v].ap()[:, h * 65:(h + 1) * 65],
                                rhs,
                                start=(kb == 0), stop=(kb == NKB(qt) - 1))
                            if kb == NKB(qt) - 1:
                                ins.then_inc(s["pe"], 1)
                        if kb + 2 < NKB(qt):
                            emit_scores(kb + 2)
                    gk += NKB(qt)
                    gkk += NKB(qt) // 4
                    q.wait_ge(s["ve"], veb[("dA", qt, hp)])
                    q.matmul(bc_view, onesr[:], dden[0][:],
                             start=True, stop=True).then_inc(s["pe"], 1)
                    q.wait_ge(s["ve"], veb[("bcbA", qt, hp)])
                    q.wait_ge(s["ve"], veb[("dB", qt, hp)])
                    q.matmul(bc_view, onesr[:], dden[1][:],
                             start=True, stop=True).then_inc(s["pe"], 1)
                q.wait_ge(s["ve"], veb[("yB", qt, ND - 1)])
                for j in range(8):
                    qs, dn = j // 2, j % 2
                    gj = qt * 8 + j
                    if gj >= 1:
                        pj = gj - 1
                        q.wait_ge(s["ve"], veb[("bo", pj // 8, pj % 8)])
                    for hp in range(ND):
                        ins = q.matmul(o_view,
                                       yall[hp].ap()[:, qs * 128:(qs + 1) * 128],
                                       woT[hp].ap()[:, dn * 512:(dn + 1) * 512],
                                       start=(hp == 0), stop=(hp == ND - 1))
                        if hp == ND - 1:
                            ins.then_inc(s["pe"], 1)

        blk.gpsimd(gpsimd_f)
        blk.sync(sync_f)
        blk.vector(vector_f)
        blk.scalar(scalar_f)
        blk.tensor(tensor_f)

    return nc


def attn_host_consts():
    hd = HD
    half = hd // 2
    theta = 1.0 / (10000.0 ** (np.arange(0, hd, 2, dtype=np.float64) / hd))
    pos = np.arange(1, T + 1, dtype=np.float64)
    ang = pos[:, None] * theta[None, :]
    ang2 = np.concatenate([ang, ang], axis=1)
    cos = np.cos(ang2).T.astype(BF)
    sin = np.sin(ang2).T
    sgn = np.ones((hd, 1)); sgn[:half] = -1.0
    sins = (sin * sgn)
    sins = np.concatenate([sins[half:], sins[:half]], axis=0).astype(BF)
    cos128 = np.concatenate([cos, cos], axis=0)
    sin128 = np.concatenate([sins, sins], axis=0)
    masks = np.zeros((128, 4 * 512), dtype=BF)
    for mi in range(4):
        dk = np.arange(128)[:, None]
        dq = np.arange(512)[None, :]
        masks[:, mi * 512:(mi + 1) * 512] = (dq >= dk + mi * 128).astype(BF)
    negc = np.full((128, 1), -12.0, dtype=np.float32)
    ones = np.ones((1, 64), dtype=np.float32)
    return dict(cos_t=cos128, sin_t=sin128, msk=masks, negC=negc, ones_r=ones)


NEC = E // 128


def mamba_scratch(nc):
    u_scr = nc.dram_tensor("u_scr", [E, T], BF16, kind="Internal").ap()
    z_scr = nc.dram_tensor("z_scr", [E, T], BF16, kind="Internal").ap()
    d_scr = nc.dram_tensor("d_scr", [E, T], BF16, kind="Internal").ap()
    b_scr = nc.dram_tensor("b_scr", [NST, T], F32, kind="Internal").ap()
    y_scr = nc.dram_tensor("y_scr", [E, T], BF16, kind="Internal").ap()
    return u_scr, z_scr, d_scr, b_scr, y_scr


def build_mamba_a(p: Prog, io=None, fwprep=None):
    nc = p.nc
    if io is None:
        xin = nc.dram_tensor("xin", [T, D], F32, kind="ExternalInput").ap()
        ip_d = nc.dram_tensor("ipT", [D, 2 * E], F32, kind="ExternalInput").ap()
        cw_d = nc.dram_tensor("cwcol", [128, NEC * 4], F32, kind="ExternalInput").ap()
        xpb_d = nc.dram_tensor("xpTB", [E, NST], F32, kind="ExternalInput").ap()
        xpd_d = nc.dram_tensor("xpTd", [E, 1], F32, kind="ExternalInput").ap()
        dtw_d = nc.dram_tensor("dtw_row", [1, E], F32, kind="ExternalInput").ap()
        dtb_d = nc.dram_tensor("dtb_col", [128, NEC], F32, kind="ExternalInput").ap()
        one_d = nc.dram_tensor("one_col", [128, 1], F32, kind="ExternalInput").ap()
        ident_d = nc.dram_tensor("ident", [128, 128], BF16, kind="ExternalInput").ap()
        u_out = nc.dram_tensor("u_scr", [E, T], BF16, kind="ExternalOutput").ap()
        z_out = nc.dram_tensor("z_scr", [E, T], BF16, kind="ExternalOutput").ap()
        dl_out = nc.dram_tensor("d_scr", [E, T], BF16, kind="ExternalOutput").ap()
        b_out = nc.dram_tensor("b_scr", [NST, T], F32, kind="ExternalOutput").ap()
    else:
        (xin, ip_d, cw_d, xpb_d, xpd_d, dtw_d, dtb_d, one_d, ident_d,
         u_out, z_out, dl_out, b_out) = io

    ND = D // 128
    ipT = [p.sb(f"ipT{c}", [128, 2 * E], BF16) for c in range(ND)]
    xpB = [p.sb(f"xpB{k}", [128, NST], BF16) for k in range(NEC)]
    xpd = [p.sb(f"xpd{k}", [128, 1], BF16) for k in range(NEC)]
    dtw = p.sb("dtw", [1, E], F32)
    dtb = p.sb("dtb", [128, NEC], F32)
    onec = p.sb("onec", [128, 1], F32)
    cw = p.sb("cw", [128, NEC * 4], F32)
    diag = [p.sb(f"diag{k}", [128, 4 * 128], BF16) for k in range(NEC)]
    xs = [p.sb(f"xs{b}", [128, D], F32) for b in range(2)]
    xn = p.sb("xn", [128, D], BF16)
    xnT = [p.sb(f"xnT{c}", [128, 512], BF16) for c in range(ND)]
    u_nat = [p.sb(f"u_nat{k}", [128, 515], BF16) for k in range(NEC)]
    u_act = [p.sb(f"u_act{k}", [128, 512], BF16) for k in range(NEC)]
    zb = [p.sb(f"zb{b}", [128, 512], BF16) for b in range(2)]
    db = [p.sb(f"db{b}", [128, 512], BF16) for b in range(2)]
    B17 = p.sb("B17", [NST, 512], F32)
    dt_sb = p.sb("dt_sb", [1, 512], F32)
    r1s = p.sb("r1s", [128, 512], F32)
    l1s = p.sb("l1s", [128, 512], F32)
    a1s = p.sb("a1s", [128, 512], F32)
    e1s = p.sb("e1s", [128, 512], F32)
    stt = {nm: [p.sb(f"st_{nm}{i}", [128, 1], F32) for i in range(4)]
           for nm in ["s1", "s2", "mu", "v1", "m2", "var", "ln", "rs", "nm"]}
    scr = p.sb("scr", [128, D], BF16)
    ident = p.sb("identt", [128, 128], BF16)

    psP = p.ps("psP", [128, 2048], F32)
    psX = p.ps("psX", [128, 2048], F32)
    pool_v = [psP.ap()[:, b * 512:(b + 1) * 512] for b in range(4)]
    xpB_v = psX.ap()[0:NST, 0:512]
    xpd_v = psX.ap()[0:1, 512:1024]
    psX_bf = psX.ap().bitcast(BF16)
    tp_v = [psX_bf[:, 2048:2176], psX_bf[:, 3072:3200]]

    s = {nm: p.sem(nm) for nm in [
        "wl", "li", "ldp0", "ldp1", "ve", "vv", "sc", "pe",
        "su0", "su1", "sz0", "sz1", "sd0", "sd1", "sb0", "vs", "ss"]}

    fw = FWPrep(p, fwprep) if fwprep is not None else None

    peb, veb, scb = {}, {}, {}
    pe_c, ve_c, sc_c = [0], [0], [0]
    pool_seq = []
    for si in range(NT512):
        for i in range(4):
            for c in range(8):
                pe_c[0] += 1
                peb[("tp", si, i * 8 + c)] = pe_c[0]
        for ec in range(NEC):
            pe_c[0] += 1
            peb[("u", si, ec)] = pe_c[0]
            pool_seq.append(("u", si, ec))
            pe_c[0] += 1
            peb[("cv", si, ec)] = pe_c[0]
            pool_seq.append(("cv", si, ec))
        for ec in range(NEC):
            pe_c[0] += 1
            peb[("z", si, ec)] = pe_c[0]
            pool_seq.append(("z", si, ec))
        pe_c[0] += 1
        peb[("xpB", si)] = pe_c[0]
        pe_c[0] += 1
        peb[("xpd", si)] = pe_c[0]
        for ec in range(NEC):
            pe_c[0] += 1
            peb[("dl", si, ec)] = pe_c[0]
            pool_seq.append(("dl", si, ec))
    for si in range(NT512):
        for i in range(4):
            ve_c[0] += 1
            veb[("ln", si, i)] = ve_c[0]
            for c in range(8):
                ve_c[0] += 1
                veb[("ev", si, i * 8 + c)] = ve_c[0]
        for ec in range(NEC):
            ve_c[0] += 1
            veb[("halo", si, ec)] = ve_c[0]
            ve_c[0] += 1
            veb[("ue", si, ec)] = ve_c[0]
        for ec in range(NEC):
            ve_c[0] += 1
            veb[("ze", si, ec)] = ve_c[0]
        ve_c[0] += 1
        veb[("B17", si)] = ve_c[0]
        ve_c[0] += 1
        veb[("dt", si)] = ve_c[0]
        for ec in range(NEC):
            ve_c[0] += 1
            veb[("dle", si, ec)] = ve_c[0]
    for si in range(NT512):
        for i in range(4):
            sc_c[0] += 1
            scb[("sq", si, i)] = sc_c[0]
            sc_c[0] += 1
            scb[("rs", si, i)] = sc_c[0]
        for ec in range(NEC):
            sc_c[0] += 1
            scb[("silu", si, ec)] = sc_c[0]
        for ec in range(NEC):
            sc_c[0] += 1
            scb[("r1", si, ec)] = sc_c[0]

    def pool_consumer(kind, si, ec):
        if kind == "u":
            return ("ve", veb[("ue", si, ec)])
        if kind == "cv":
            return ("sc", scb[("silu", si, ec)])
        if kind == "z":
            return ("ve", veb[("ze", si, ec)])
        return ("sc", scb[("r1", si, ec)])

    pool_of = {}
    for idx, key in enumerate(pool_seq):
        pool_of[key] = idx

    def pool_wait(q, key):
        idx = pool_of[key]
        if idx >= 4:
            k2 = pool_seq[idx - 4]
            sem_nm, val = pool_consumer(*k2)
            q.wait_ge(s[sem_nm], val)
        return pool_v[idx % 4]

    with p.block() as blk:

        def gpsimd_f(q):
            q.dma_start(out=cw[:], in_=cw_d).then_inc(s["wl"], 16)
            for c in range(ND):
                q.dma_start(out=ipT[c][:], in_=ip_d[c * 128:(c + 1) * 128, :]).then_inc(s["wl"], 16)
            for k in range(NEC):
                q.dma_start(out=xpB[k][:], in_=xpb_d[k * 128:(k + 1) * 128, :]).then_inc(s["wl"], 16)
                q.dma_start(out=xpd[k][:], in_=xpd_d[k * 128:(k + 1) * 128, :]).then_inc(s["wl"], 16)
            q.dma_start(out=dtw[:], in_=dtw_d).then_inc(s["wl"], 16)
            q.dma_start(out=dtb[:], in_=dtb_d).then_inc(s["wl"], 16)
            q.dma_start(out=onec[:], in_=one_d).then_inc(s["wl"], 16)
            if fw is not None:
                fw.run_gp(q)
        NWL = 16 * (ND + 2 * NEC + 4)

        def sync_f(q):
            q.dma_start(out=ident[:], in_=ident_d).then_inc(s["li"], 16)
            for si in range(NT512):
                for i in range(4):
                    g = si * 4 + i
                    if g >= 2:
                        q.wait_ge(s["ve"], veb[("ln", (g - 2) // 4, (g - 2) % 4)])
                    q.dma_start(out=xs[g % 2][:],
                                in_=xin[g * 128:(g + 1) * 128, :]).then_inc(s[f"ldp{g % 2}"], 16)
                for ec in range(NEC):
                    gu = si * NEC + ec
                    q.wait_ge(s["sc"], scb[("silu", si, ec)])
                    q.dma_start(out=u_out[ec * 128:(ec + 1) * 128, si * 512:(si + 1) * 512],
                                in_=u_act[ec][:]).then_inc(s[f"su{gu % 2}"], 16)
                for ec in range(NEC):
                    gz = si * NEC + ec
                    q.wait_ge(s["ve"], veb[("ze", si, ec)])
                    q.dma_start(out=z_out[ec * 128:(ec + 1) * 128, si * 512:(si + 1) * 512],
                                in_=zb[gz % 2][:]).then_inc(s[f"sz{gz % 2}"], 16)
                q.wait_ge(s["ve"], veb[("B17", si)])
                q.dma_start(out=b_out[:, si * 512:(si + 1) * 512],
                            in_=B17[:]).then_inc(s["sb0"], 16)
                for ec in range(NEC):
                    gd = si * NEC + ec
                    q.wait_ge(s["ve"], veb[("dle", si, ec)])
                    q.dma_start(out=dl_out[ec * 128:(ec + 1) * 128, si * 512:(si + 1) * 512],
                                in_=db[gd % 2][:]).then_inc(s[f"sd{gd % 2}"], 16)
            q.wait_ge(s["su0"], 16 * 64)
            q.wait_ge(s["su1"], 16 * 64)
            q.wait_ge(s["sz0"], 16 * 64)
            q.wait_ge(s["sz1"], 16 * 64)
            q.wait_ge(s["sd0"], 16 * 64)
            q.wait_ge(s["sd1"], 16 * 64)
            q.wait_ge(s["sb0"], 16 * 8)

        def vector_f(q):
            vsn = [0]

            def vinc(ins):
                ins.then_inc(s["vs"], 1)
                vsn[0] += 1
                return vsn[0]

            def vwait(n):
                if n > 0:
                    q.wait_ge(s["vs"], n)

            q.wait_ge(s["wl"], 16)
            q.wait_ge(s["li"], 16)
            dlast = 0
            for k in range(NEC):
                for j in range(4):
                    dlast = vinc(q.tensor_scalar(
                        diag[k].ap()[:, j * 128:(j + 1) * 128], ident[:],
                        cw.ap()[:, k * 4 + j:k * 4 + j + 1], None, ALU.mult))
            vwait(dlast)
            q.sem_inc(s["ve"], 0) if False else None
            for si in range(NT512):
                for i in range(4):
                    g = si * 4 + i
                    q.wait_ge(s[f"ldp{g % 2}"], 16 * (g // 2 + 1))
                    a = vinc(q.reduce_sum(stt["s1"][i][:], xs[g % 2][:], AX.X))
                    vwait(a)
                    b = vinc(q.tensor_scalar(stt["mu"][i][:], stt["s1"][i][:],
                                             1.0 / D, None, ALU.mult))
                    q.wait_ge(s["sc"], scb[("sq", si, i)])
                    cv = vinc(q.tensor_scalar(stt["v1"][i][:], stt["s2"][i][:],
                                              1.0 / D, 1e-5, ALU.mult, ALU.add))
                    vwait(b)
                    dv = vinc(q.scalar_tensor_tensor(stt["m2"][i][:], stt["mu"][i][:],
                                                     -1.0, stt["mu"][i][:],
                                                     ALU.mult, ALU.mult))
                    vwait(dv)
                    e = vinc(q.tensor_tensor(stt["var"][i][:], stt["v1"][i][:],
                                             stt["m2"][i][:], ALU.add))
                    vwait(e)
                    q.sem_inc(s["vv"], 1)
                    q.wait_ge(s["sc"], scb[("rs", si, i)])
                    f = vinc(q.scalar_tensor_tensor(stt["nm"][i][:], stt["mu"][i][:],
                                                    -1.0, stt["rs"][i][:],
                                                    ALU.mult, ALU.mult))
                    if g > 0:
                        pg = g - 1
                        q.wait_ge(s["pe"], peb[("tp", pg // 4, (pg % 4) * 8 + 7)])
                    vwait(f)
                    q.tensor_scalar(
                        xn[:], xs[g % 2][:], stt["rs"][i].ap()[:, 0:1],
                        stt["nm"][i].ap()[:, 0:1], ALU.mult, ALU.add,
                    ).then_inc(s["ve"], 1)
                    for c in range(8):
                        jj = i * 8 + c
                        q.wait_ge(s["pe"], peb[("tp", si, jj)])
                        q.tensor_copy(xnT[c].ap()[:, i * 128:(i + 1) * 128],
                                      tp_v[jj % 2]).then_inc(s["ve"], 1)
                for ec in range(NEC):
                    gu = si * NEC + ec
                    if si > 0:
                        q.wait_ge(s["pe"], peb[("cv", si - 1, ec)])
                        hc = vinc(q.tensor_copy(u_nat[ec].ap()[:, 0:3],
                                                u_nat[ec].ap()[:, 512:515]))
                        vwait(hc)
                        q.sem_inc(s["ve"], 1)
                    else:
                        hc = vinc(q.memset(u_nat[ec].ap()[:, 0:3], 0.0))
                        vwait(hc)
                        q.sem_inc(s["ve"], 1)
                    q.wait_ge(s["pe"], peb[("u", si, ec)])
                    ue = vinc(q.tensor_copy(u_nat[ec].ap()[:, 3:515],
                                            pool_v[pool_of[("u", si, ec)] % 4]))
                    vwait(ue)
                    q.sem_inc(s["ve"], 1)
                    if fw is not None:
                        fw.pump_v(q, vinc, vwait)
                for ec in range(NEC):
                    gz = si * NEC + ec
                    q.wait_ge(s["pe"], peb[("z", si, ec)])
                    if gz >= 2:
                        q.wait_ge(s[f"sz{gz % 2}"], 16 * ((gz - 2) // 2 + 1))
                    ze = vinc(q.tensor_copy(zb[gz % 2][:],
                                            pool_v[pool_of[("z", si, ec)] % 4]))
                    vwait(ze)
                    q.sem_inc(s["ve"], 1)
                    if fw is not None:
                        fw.pump_v(q, vinc, vwait)
                q.wait_ge(s["pe"], peb[("xpB", si)])
                if si >= 1:
                    q.wait_ge(s["sb0"], 16 * si)
                b17 = vinc(q.tensor_copy(B17[:], xpB_v))
                vwait(b17)
                q.sem_inc(s["ve"], 1)
                q.wait_ge(s["pe"], peb[("xpd", si)])
                dts = vinc(q.tensor_copy(dt_sb[:], xpd_v))
                vwait(dts)
                q.sem_inc(s["ve"], 1)
                for ec in range(NEC):
                    gd = si * NEC + ec
                    q.wait_ge(s["sc"], scb[("r1", si, ec)])
                    if gd >= 2:
                        q.wait_ge(s[f"sd{gd % 2}"], 16 * ((gd - 2) // 2 + 1))
                    de = vinc(q.tensor_tensor(db[gd % 2][:], r1s[:], l1s[:], ALU.add))
                    vwait(de)
                    q.sem_inc(s["ve"], 1)
                    if fw is not None:
                        fw.pump_v(q, vinc, vwait)
            if fw is not None:
                fw.drain_v(q, vinc, vwait)

        def scalar_f(q):
            ssn = [0]

            def sinc(ins):
                ins.then_inc(s["ss"], 1)
                ssn[0] += 1
                return ssn[0]

            for si in range(NT512):
                for i in range(4):
                    g = si * 4 + i
                    q.wait_ge(s[f"ldp{g % 2}"], 16 * (g // 2 + 1))
                    prior = scb[("sq", si, i)] - 1
                    if prior > 0:
                        q.wait_ge(s["sc"], prior)
                    q.activation(scr[:], xs[g % 2][:], AF.Square,
                                 accum_out=stt["s2"][i][:]).then_inc(s["sc"], 1)
                    q.wait_ge(s["vv"], g + 1)
                    n0 = sinc(q.activation(stt["ln"][i][:], stt["var"][i][:], AF.Ln))
                    q.wait_ge(s["ss"], n0)
                    q.activation(stt["rs"][i][:], stt["ln"][i][:], AF.Exp,
                                 scale=-0.5).then_inc(s["sc"], 1)
                for ec in range(NEC):
                    gu = si * NEC + ec
                    q.wait_ge(s["pe"], peb[("cv", si, ec)])
                    if si >= 1:
                        q.wait_ge(s["pe"], peb[("xpd", si - 1)])
                    if gu >= 2:
                        q.wait_ge(s[f"su{gu % 2}"], 16 * ((gu - 2) // 2 + 1))
                    q.activation(u_act[ec][:],
                                 pool_v[pool_of[("cv", si, ec)] % 4],
                                 AF.Silu).then_inc(s["sc"], 1)
                for ec in range(NEC):
                    q.wait_ge(s["pe"], peb[("dl", si, ec)])
                    pv = pool_v[pool_of[("dl", si, ec)] % 4]
                    bb = dtb.ap()[:, ec:ec + 1]
                    if (si, ec) != (0, 0):
                        pec = ec - 1 if ec > 0 else NEC - 1
                        psi = si if ec > 0 else si - 1
                        q.wait_ge(s["ve"], veb[("dle", psi, pec)])
                    n1 = sinc(q.activation(a1s[:], pv, AF.Abs, bias=bb))
                    q.wait_ge(s["ss"], n1)
                    n2 = sinc(q.activation(e1s[:], a1s[:], AF.Exp, scale=-1.0))
                    q.wait_ge(s["ss"], n2)
                    n3 = sinc(q.activation(l1s[:], e1s[:], AF.Ln,
                                           bias=onec.ap()[:, 0:1]))
                    q.wait_ge(s["ss"], n3)
                    q.activation(r1s[:], pv, AF.Relu, bias=bb).then_inc(s["sc"], 1)

        def tensor_f(q):
            q.wait_ge(s["li"], 16)
            q.wait_ge(s["vs"], 64)
            for si in range(NT512):
                for i in range(4):
                    q.wait_ge(s["ve"], veb[("ln", si, i)])
                    for c in range(8):
                        jj = i * 8 + c
                        tot = si * 32 + jj
                        if tot >= 2:
                            pj = tot - 2
                            q.wait_ge(s["ve"], veb[("ev", pj // 32, pj % 32)])
                        q.transpose(tp_v[jj % 2],
                                    xn.ap()[:, c * 128:(c + 1) * 128],
                                    ident[:]).then_inc(s["pe"], 1)
                q.wait_ge(s["ve"], veb[("ev", si, 31)])
                if si == 0:
                    q.wait_ge(s["wl"], NWL)
                for ec in range(NEC):
                    pv = pool_wait(q, ("u", si, ec))
                    for c in range(ND):
                        ins = q.matmul(pv, ipT[c].ap()[:, ec * 128:(ec + 1) * 128],
                                       xnT[c][:], start=(c == 0), stop=(c == ND - 1))
                        if c == ND - 1:
                            ins.then_inc(s["pe"], 1)
                    pv2 = pool_wait(q, ("cv", si, ec))
                    q.wait_ge(s["ve"], veb[("ue", si, ec)])
                    for j in range(4):
                        ins = q.matmul(pv2, diag[ec].ap()[:, j * 128:(j + 1) * 128],
                                       u_nat[ec].ap()[:, j:j + 512],
                                       start=(j == 0), stop=(j == 3))
                        if j == 3:
                            ins.then_inc(s["pe"], 1)
                for ec in range(NEC):
                    pv = pool_wait(q, ("z", si, ec))
                    for c in range(ND):
                        ins = q.matmul(pv, ipT[c].ap()[:, E + ec * 128:E + (ec + 1) * 128],
                                       xnT[c][:], start=(c == 0), stop=(c == ND - 1))
                        if c == ND - 1:
                            ins.then_inc(s["pe"], 1)
                if si >= 1:
                    q.wait_ge(s["ve"], veb[("B17", si - 1)])
                for k in range(NEC):
                    q.wait_ge(s["sc"], scb[("silu", si, k)])
                    ins = q.matmul(xpB_v, xpB[k][:], u_act[k][:],
                                   start=(k == 0), stop=(k == NEC - 1))
                    if k == NEC - 1:
                        ins.then_inc(s["pe"], 1)
                if si >= 1:
                    q.wait_ge(s["ve"], veb[("dt", si - 1)])
                for k in range(NEC):
                    ins = q.matmul(xpd_v, xpd[k][:], u_act[k][:],
                                   start=(k == 0), stop=(k == NEC - 1))
                    if k == NEC - 1:
                        ins.then_inc(s["pe"], 1)
                q.wait_ge(s["ve"], veb[("dt", si)])
                for ec in range(NEC):
                    pv = pool_wait(q, ("dl", si, ec))
                    q.matmul(pv, dtw.ap()[0:1, ec * 128:(ec + 1) * 128],
                             dt_sb[:], start=True, stop=True).then_inc(s["pe"], 1)

        blk.gpsimd(gpsimd_f)
        blk.sync(sync_f)
        blk.vector(vector_f)
        blk.scalar(scalar_f)
        blk.tensor(tensor_f)

    return nc


def build_mamba_b(p: Prog, io=None):
    nc = p.nc
    if io is None:
        u_in = nc.dram_tensor("u_scr", [E, T], BF16, kind="ExternalInput").ap()
        z_in = nc.dram_tensor("z_scr", [E, T], BF16, kind="ExternalInput").ap()
        d_in = nc.dram_tensor("d_scr", [E, T], BF16, kind="ExternalInput").ap()
        b_in = nc.dram_tensor("b_scr", [NST, T], F32, kind="ExternalInput").ap()
        ac_d = nc.dram_tensor("a_cols", [E, NST], F32, kind="ExternalInput").ap()
        dc_d = nc.dram_tensor("d_cols", [128, NEC], F32, kind="ExternalInput").ap()
        y_out = nc.dram_tensor("y_scr", [E, T], BF16, kind="ExternalOutput").ap()
    else:
        u_in, z_in, d_in, b_in, ac_d, dc_d, y_out = io

    acolr = [p.sb(f"acolr{k}", [128, NST], F32) for k in range(NEC)]
    acol = [p.sb(f"acol{k}", [128, NST], F32) for k in range(NEC)]
    dcol = p.sb("dcol", [128, NEC], F32)
    dch = [p.sb(f"dch{b}", [128, T], BF16) for b in range(2)]
    uch = [p.sb(f"uch{b}", [128, T], BF16) for b in range(2)]
    zch = [p.sb(f"zch{b}", [128, T], BF16) for b in range(2)]
    brep = [p.sb(f"brep{b}", [128, T], BF16) for b in range(2)]
    at = [p.sb(f"at{b}", [128, T], BF16) for b in range(2)]
    dbt = [p.sb(f"dbt{b}", [128, T], BF16) for b in range(2)]
    ht = p.sb("ht", [128, T], BF16)
    prod = p.sb("prod", [128, T], BF16)
    yacc = p.sb("yacc", [128, T], BF16)
    t1g = p.sb("t1g", [128, T], BF16)
    szg = p.sb("szg", [128, T], BF16)
    ygo = [p.sb(f"ygo{b}", [128, T], BF16) for b in range(2)]

    s = {nm: p.sem(nm) for nm in [
        "wl", "ld0", "ld1", "lu0", "lu1", "lz0", "lz1", "lb0", "lb1",
        "ve", "sc", "sy0", "sy1", "vs", "ss"]}

    with p.block() as blk:
        VEB = NEC * 0 + 16 * 3 + 2

        def ve_db(ec, n):
            return VEB * ec + 3 * n + 1

        def ve_scan(ec, n):
            return VEB * ec + 3 * n + 2

        def ve_acc(ec, n):
            return VEB * ec + 3 * n + 3

        def ve_t1(ec):
            return VEB * ec + 48 + 1

        def ve_yg(ec):
            return VEB * ec + 48 + 2

        SCB = 17

        def sc_a(ec, n):
            return SCB * ec + n + 1

        def sc_sz(ec):
            return SCB * ec + 17

        def gpsimd_f(q):
            for k in range(NEC):
                q.dma_start(out=acolr[k][:],
                            in_=ac_d[k * 128:(k + 1) * 128, :]).then_inc(s["wl"], 16)
            q.dma_start(out=dcol[:], in_=dc_d).then_inc(s["wl"], 16)
            for ec in range(NEC):
                for n in range(NST):
                    g = ec * NST + n
                    if g >= 2:
                        pg = g - 2
                        q.wait_ge(s["ve"], ve_acc(pg // NST, pg % NST))
                    q.dma_start(
                        out=brep[g % 2][:],
                        in_=b_in[n:n + 1, :].broadcast_to([128, T]),
                    ).then_inc(s[f"lb{g % 2}"], 16)
        NWL = 16 * (NEC + 1)

        def sync_f(q):
            for ec in range(NEC):
                if ec >= 2:
                    q.wait_ge(s["ve"], ve_acc(ec - 2, NST - 1))
                q.dma_start(out=dch[ec % 2][:],
                            in_=d_in[ec * 128:(ec + 1) * 128, :]).then_inc(s[f"ld{ec % 2}"], 16)
                if ec >= 2:
                    q.wait_ge(s["ve"], ve_t1(ec - 2))
                q.dma_start(out=uch[ec % 2][:],
                            in_=u_in[ec * 128:(ec + 1) * 128, :]).then_inc(s[f"lu{ec % 2}"], 16)
                if ec >= 2:
                    q.wait_ge(s["ve"], ve_yg(ec - 2))
                q.dma_start(out=zch[ec % 2][:],
                            in_=z_in[ec * 128:(ec + 1) * 128, :]).then_inc(s[f"lz{ec % 2}"], 16)
                q.wait_ge(s["ve"], ve_yg(ec))
                q.dma_start(out=y_out[ec * 128:(ec + 1) * 128, :],
                            in_=ygo[ec % 2][:]).then_inc(s[f"sy{ec % 2}"], 16)
            q.wait_ge(s["sy0"], 16 * 8)
            q.wait_ge(s["sy1"], 16 * 8)

        def vector_f(q):
            vsn = [0]

            def vinc(ins):
                ins.then_inc(s["vs"], 1)
                vsn[0] += 1
                return vsn[0]

            def vwait(n):
                if n > 0:
                    q.wait_ge(s["vs"], n)

            for ec in range(NEC):
                for n in range(NST):
                    g = ec * NST + n
                    q.wait_ge(s[f"lb{g % 2}"], 16 * (g // 2 + 1))
                    q.wait_ge(s[f"ld{ec % 2}"], 16 * (ec // 2 + 1))
                    db_ = vinc(q.tensor_tensor(dbt[g % 2][:], dch[ec % 2][:],
                                               brep[g % 2][:], ALU.mult))
                    ins = q.tensor_tensor(dbt[g % 2][:], dch[ec % 2][:],
                                          brep[g % 2][:], ALU.mult) if False else None
                    vwait(db_)
                    q.sem_inc(s["ve"], 1)
                    q.wait_ge(s["sc"], sc_a(ec, n))
                    sc_ = vinc(q.tensor_tensor_scan(ht[:], at[g % 2][:], dbt[g % 2][:],
                                                    0.0, ALU.mult, ALU.add))
                    vwait(sc_)
                    q.sem_inc(s["ve"], 1)
                    if n == 0:
                        pr = vinc(q.tensor_tensor(yacc[:], ht[:], brep[g % 2][:],
                                                  ALU.mult))
                    else:
                        pr = vinc(q.tensor_tensor(prod[:], ht[:], brep[g % 2][:],
                                                  ALU.mult))
                        vwait(pr)
                        pr = vinc(q.tensor_tensor(yacc[:], yacc[:], prod[:], ALU.add))
                    vwait(pr)
                    q.sem_inc(s["ve"], 1)
                q.wait_ge(s[f"lu{ec % 2}"], 16 * (ec // 2 + 1))
                t1_ = vinc(q.scalar_tensor_tensor(t1g[:], uch[ec % 2][:],
                                                  dcol.ap()[:, ec:ec + 1],
                                                  yacc[:], ALU.mult, ALU.add))
                vwait(t1_)
                q.sem_inc(s["ve"], 1)
                q.wait_ge(s["sc"], sc_sz(ec))
                if ec >= 2:
                    q.wait_ge(s[f"sy{ec % 2}"], 16 * ((ec - 2) // 2 + 1))
                yg_ = vinc(q.tensor_tensor(ygo[ec % 2][:], t1g[:], szg[:], ALU.mult))
                vwait(yg_)
                q.sem_inc(s["ve"], 1)

        def scalar_f(q):
            ssn = [0]

            def sinc(ins):
                ins.then_inc(s["ss"], 1)
                ssn[0] += 1
                return ssn[0]

            q.wait_ge(s["wl"], NWL)
            for k in range(NEC):
                n1 = sinc(q.activation(acol[k][:], acolr[k][:], AF.Exp))
                q.wait_ge(s["ss"], n1)
                n2 = sinc(q.activation(acol[k][:], acol[k][:], AF.Copy, scale=-1.0))
                q.wait_ge(s["ss"], n2)
            for ec in range(NEC):
                for n in range(NST):
                    g = ec * NST + n
                    q.wait_ge(s[f"ld{ec % 2}"], 16 * (ec // 2 + 1))
                    if g >= 2:
                        pg = g - 2
                        q.wait_ge(s["ve"], ve_scan(pg // NST, pg % NST))
                    q.activation(at[g % 2][:], dch[ec % 2][:], AF.Exp,
                                 scale=acol[ec].ap()[:, n:n + 1]).then_inc(s["sc"], 1)
                q.wait_ge(s[f"lz{ec % 2}"], 16 * (ec // 2 + 1))
                if ec >= 1:
                    q.wait_ge(s["ve"], ve_yg(ec - 1))
                q.activation(szg[:], zch[ec % 2][:], AF.Silu).then_inc(s["sc"], 1)

        blk.gpsimd(gpsimd_f)
        blk.sync(sync_f)
        blk.vector(vector_f)
        blk.scalar(scalar_f)

    return nc


def build_mamba_c(p: Prog, io=None):
    nc = p.nc
    if io is None:
        xin = nc.dram_tensor("xin", [T, D], F32, kind="ExternalInput").ap()
        y_in = nc.dram_tensor("y_scr", [E, T], BF16, kind="ExternalInput").ap()
        op_d = nc.dram_tensor("opT", [E, D], F32, kind="ExternalInput").ap()
        ob_d = nc.dram_tensor("ob_b", [128, D], F32, kind="ExternalInput").ap()
        xout = nc.dram_tensor("xout", [T, D], F32, kind="ExternalOutput").ap()
    else:
        xin, y_in, op_d, ob_d, xout = io

    opT = [p.sb(f"opT{k}", [128, D], BF16) for k in range(NEC)]
    ob = p.sb("ob", [128, D], F32)
    ygt = [[p.sb(f"ygt{b}_{k}", [128, 512], BF16) for k in range(NEC)]
           for b in range(2)]
    xr = [p.sb(f"xr{b}", [128, 512], F32) for b in range(2)]
    xo = [p.sb(f"xo{b}", [128, 512], F32) for b in range(2)]
    ps_o = [p.ps(f"ps_o{b}", [128, 512], F32) for b in range(2)]

    s = {nm: p.sem(nm) for nm in [
        "wl", "ly", "lrp0", "lrp1", "ve", "pe", "stp0", "stp1", "vs"]}

    def ve_res(gj):
        return gj + 1

    def pe_o(gj):
        return gj + 1

    with p.block() as blk:

        def gpsimd_f(q):
            for k in range(NEC):
                q.dma_start(out=opT[k][:],
                            in_=op_d[k * 128:(k + 1) * 128, :]).then_inc(s["wl"], 16)
            q.dma_start(out=ob[:], in_=ob_d).then_inc(s["wl"], 16)
        NWL = 16 * (NEC + 1)

        def sync_f(q):
            for si in range(NT512):
                for k in range(NEC):
                    if si >= 2:
                        q.wait_ge(s["pe"], pe_o((si - 2) * 8 + 7))
                    q.dma_start(out=ygt[si % 2][k][:],
                                in_=y_in[k * 128:(k + 1) * 128,
                                         si * 512:(si + 1) * 512]).then_inc(s["ly"], 16)
                for j in range(8):
                    gj = si * 8 + j
                    qs, dn = j // 2, j % 2
                    if gj >= 2:
                        pj = gj - 2
                        q.wait_ge(s["ve"], ve_res(pj))
                    q.dma_start(out=xr[gj % 2][:],
                                in_=xin[(si * 4 + qs) * 128:(si * 4 + qs + 1) * 128,
                                        dn * 512:(dn + 1) * 512],
                                ).then_inc(s[f"lrp{gj % 2}"], 16)
                    if gj >= 2:
                        pj = gj - 2
                        q.dma_start(
                            out=xout[(pj // 2) * 128:(pj // 2 + 1) * 128,
                                     (pj % 2) * 512:(pj % 2 + 1) * 512],
                            in_=xo[pj % 2][:]).then_inc(s[f"stp{pj % 2}"], 16)
            for gj in range(NT512 * 8 - 2, NT512 * 8):
                q.wait_ge(s["ve"], ve_res(gj))
                q.dma_start(
                    out=xout[(gj // 2) * 128:(gj // 2 + 1) * 128,
                             (gj % 2) * 512:(gj % 2 + 1) * 512],
                    in_=xo[gj % 2][:]).then_inc(s[f"stp{gj % 2}"], 16)
            q.wait_ge(s["stp0"], 16 * NT128)
            q.wait_ge(s["stp1"], 16 * NT128)

        def vector_f(q):
            vsn = [0]

            def vinc(ins):
                ins.then_inc(s["vs"], 1)
                vsn[0] += 1
                return vsn[0]

            def vwait(n):
                if n > 0:
                    q.wait_ge(s["vs"], n)

            for si in range(NT512):
                for j in range(8):
                    gj = si * 8 + j
                    qs, dn = j // 2, j % 2
                    q.wait_ge(s["pe"], pe_o(gj))
                    if gj >= 2:
                        q.wait_ge(s[f"stp{gj % 2}"], 16 * ((gj - 2) // 2 + 1))
                    ov = vinc(q.tensor_tensor(xo[gj % 2][:], ps_o[gj % 2][:],
                                              ob.ap()[:, dn * 512:(dn + 1) * 512],
                                              ALU.add))
                    vwait(ov)
                    q.sem_inc(s["ve"], 0) if False else None
                    q.wait_ge(s[f"lrp{gj % 2}"], 16 * (gj // 2 + 1))
                    o2 = vinc(q.tensor_tensor(xo[gj % 2][:], xo[gj % 2][:],
                                              xr[gj % 2][:], ALU.add))
                    vwait(o2)
                    q.sem_inc(s["ve"], 1)

        def tensor_f(q):
            q.wait_ge(s["wl"], NWL)
            for si in range(NT512):
                q.wait_ge(s["ly"], 16 * NEC * (si + 1))
                for j in range(8):
                    gj = si * 8 + j
                    qs, dn = j // 2, j % 2
                    if gj >= 2:
                        q.wait_ge(s["ve"], ve_res(gj - 2))
                    for k in range(NEC):
                        ins = q.matmul(ps_o[gj % 2][:],
                                       ygt[si % 2][k].ap()[:, qs * 128:(qs + 1) * 128],
                                       opT[k].ap()[:, dn * 512:(dn + 1) * 512],
                                       start=(k == 0), stop=(k == NEC - 1))
                        if k == NEC - 1:
                            ins.then_inc(s["pe"], 1)

        blk.gpsimd(gpsimd_f)
        blk.sync(sync_f)
        blk.vector(vector_f)
        blk.tensor(tensor_f)

    return nc


def _u16(d):
    return {k: (v.view(np.uint16) if v.dtype == BF else np.ascontiguousarray(v))
            for k, v in d.items()}


def build_fused(p: Prog):
    nc = p.nc
    dt = nc.dram_tensor
    x0 = dt("x0", [T, D], F32, kind="ExternalInput").ap()
    ident_d = dt("ident", [128, 128], BF16, kind="ExternalInput").ap()
    ip_d = dt("ipT", [D, 2 * E], F32, kind="ExternalInput").ap()
    cw_d = dt("cwcol", [128, NEC * 4], F32, kind="ExternalInput").ap()
    xpb_d = dt("xpTB", [E, NST], F32, kind="ExternalInput").ap()
    xpd_d = dt("xpTd", [E, 1], F32, kind="ExternalInput").ap()
    dtw_d = dt("dtw_row", [1, E], F32, kind="ExternalInput").ap()
    dtb_d = dt("dtb_col", [128, NEC], F32, kind="ExternalInput").ap()
    one_d = dt("one_col", [128, 1], F32, kind="ExternalInput").ap()
    ac_d = dt("a_cols", [E, NST], F32, kind="ExternalInput").ap()
    dc_d = dt("d_cols", [128, NEC], F32, kind="ExternalInput").ap()
    op_d = dt("opT", [E, D], F32, kind="ExternalInput").ap()
    ob_d = dt("ob_b", [128, D], F32, kind="ExternalInput").ap()
    wg0_d = dt("wg0", [D, H0], I32, kind="ExternalInput").ap()
    wu0_d = dt("wu0", [D, H0], I32, kind="ExternalInput").ap()
    wd0_d = dt("wd0", [H0, D], I32, kind="ExternalInput").ap()
    wq_d = dt("wq", [D, D], F32, kind="ExternalInput").ap()
    wk_d = dt("wk", [D, D], F32, kind="ExternalInput").ap()
    wv_d = dt("wv", [D, D], F32, kind="ExternalInput").ap()
    wo_d = dt("wo", [D, D], F32, kind="ExternalInput").ap()
    bcol_d = dt("bcol", [128, 16], F32, kind="ExternalInput").ap()
    bvb_d = dt("bvb", [128, D], F32, kind="ExternalInput").ap()
    bob_d = dt("bob", [128, D], F32, kind="ExternalInput").ap()
    cos_d = dt("cos_t", [128, T], BF16, kind="ExternalInput").ap()
    sin_d = dt("sin_t", [128, T], BF16, kind="ExternalInput").ap()
    msk_d = dt("msk", [128, 4 * 512], BF16, kind="ExternalInput").ap()
    negC_d = dt("negC", [128, 1], F32, kind="ExternalInput").ap()
    ones_d = dt("ones_r", [1, 64], F32, kind="ExternalInput").ap()
    wg1_d = dt("wg1", [I1, D], F32, kind="ExternalInput").ap()
    wu1_d = dt("wu1", [I1, D], F32, kind="ExternalInput").ap()
    wd1_d = dt("wd1", [D, I1], F32, kind="ExternalInput").ap()
    out_d = dt("out", [T, D], F32, kind="ExternalOutput").ap()
    u_scr, z_scr, d_scr, b_scr, y_scr = mamba_scratch(nc)
    x1 = dt("x1_scr", [T, D], F32, kind="Internal").ap()
    x2 = dt("x2_scr", [T, D], F32, kind="Internal").ap()
    x3 = dt("x3_scr", [T, D], F32, kind="Internal").ap()
    wgq1 = dt("wgq1_scr", [I1, D], BF16, kind="Internal").ap()
    wuq1 = dt("wuq1_scr", [I1, D], BF16, kind="Internal").ap()
    wdq1 = dt("wdq1_scr", [D, I1], BF16, kind="Internal").ap()

    p.push("ma")
    build_mamba_a(p, io=(x0, ip_d, cw_d, xpb_d, xpd_d, dtw_d, dtb_d, one_d,
                         ident_d, u_scr, z_scr, d_scr, b_scr),
                   fwprep=(wg1_d, wu1_d, wd1_d, wgq1, wuq1, wdq1))
    p.pop()
    p.push("mb")
    build_mamba_b(p, io=(u_scr, z_scr, d_scr, b_scr, ac_d, dc_d, y_scr))
    p.pop()
    p.push("mc")
    build_mamba_c(p, io=(x0, y_scr, op_d, ob_d, x1))
    p.pop()
    p.push("f0")
    build_ffn0(p, io=(x1, wg0_d, wu0_d, wd0_d, ident_d, x2))
    p.pop()
    p.push("at")
    build_attn(p, io=(x2, wq_d, wk_d, wv_d, wo_d, bcol_d, bvb_d, bob_d,
                      cos_d, sin_d, msk_d, ident_d, negC_d, ones_d, x3))
    p.pop()
    p.push("fa")
    build_ffn1(p, io=(x3, wgq1, wuq1, wdq1, ident_d, out_d))
    p.pop()
    return nc


def kernel_fused(**inputs):
    B = 8
    cores = list(range(B))
    hp = _host_prep(**inputs)
    p = Prog()
    nc = build_fused(p)
    maps = []
    for b in range(B):
        m = dict(hp["shared"])
        m["x0"] = hp["x"][b]
        maps.append(m)
    res = run_bass_kernel_spmd(nc, maps, core_ids=cores).results
    return np.stack([res[b]["out"] for b in range(B)], axis=0)


def _host_prep(L0_attn_x, L0_attn_in_proj_w, L0_attn_conv_w, L0_attn_x_proj_w,
               L0_attn_dt_w, L0_attn_dt_b, L0_attn_A_param, L0_attn_D_param,
               L0_attn_out_w, L0_attn_out_b, L0_ffn_w_gate, L0_ffn_w_up,
               L0_ffn_w_down, L1_attn_Wq, L1_attn_bq, L1_attn_Wk, L1_attn_bk,
               L1_attn_Wv, L1_attn_bv, L1_attn_Wo, L1_attn_bo, L1_ffn_w_gate,
               L1_ffn_w_up, L1_ffn_w_down):
    x = np.ascontiguousarray(np.asarray(L0_attn_x, dtype=np.float32))
    ident = np.eye(128).astype(BF)
    cwcol = np.zeros((128, NEC * 4), np.float32)
    cwa = np.asarray(L0_attn_conv_w)
    for k in range(NEC):
        for j in range(4):
            cwcol[:, k * 4 + j] = cwa[k * 128:(k + 1) * 128, 0, j]
    xpw = np.asarray(L0_attn_x_proj_w)
    dtb_col = np.zeros((128, NEC), np.float32)
    dtba = np.asarray(L0_attn_dt_b)
    for k in range(NEC):
        dtb_col[:, k] = dtba[k * 128:(k + 1) * 128]
    d_cols = np.zeros((128, NEC), np.float32)
    dpa = np.asarray(L0_attn_D_param)
    for k in range(NEC):
        d_cols[:, k] = dpa[k * 128:(k + 1) * 128]
    bq, bk = np.asarray(L1_attn_bq), np.asarray(L1_attn_bk)
    bcol = np.zeros((128, 16), np.float32)
    for c in range(8):
        bcol[:, c] = bq[c * 128:(c + 1) * 128]
        bcol[:, 8 + c] = bk[c * 128:(c + 1) * 128]
    shared = dict(
        ident=ident,
        ipT=np.ascontiguousarray(np.asarray(L0_attn_in_proj_w).T),
        cwcol=cwcol,
        xpTB=np.ascontiguousarray(xpw[1:].T),
        xpTd=np.ascontiguousarray(xpw[0:1].T),
        dtw_row=np.ascontiguousarray(np.asarray(L0_attn_dt_w).T),
        dtb_col=dtb_col,
        one_col=np.ones((128, 1), np.float32),
        a_cols=np.ascontiguousarray(np.asarray(L0_attn_A_param)),
        d_cols=d_cols,
        opT=np.ascontiguousarray(np.asarray(L0_attn_out_w).T),
        ob_b=np.ascontiguousarray(np.broadcast_to(np.asarray(L0_attn_out_b), (128, D))),
        wg0=np.ascontiguousarray(np.asarray(L0_ffn_w_gate).T),
        wu0=np.ascontiguousarray(np.asarray(L0_ffn_w_up).T),
        wd0=np.ascontiguousarray(np.asarray(L0_ffn_w_down).T),
        wq=np.ascontiguousarray(np.asarray(L1_attn_Wq).T),
        wk=np.ascontiguousarray(np.asarray(L1_attn_Wk).T),
        wv=np.ascontiguousarray(np.asarray(L1_attn_Wv).T),
        wo=np.ascontiguousarray(np.asarray(L1_attn_Wo).T),
        bcol=bcol,
        bvb=np.ascontiguousarray(np.broadcast_to(np.asarray(L1_attn_bv), (128, D))),
        bob=np.ascontiguousarray(np.broadcast_to(np.asarray(L1_attn_bo), (128, D))),
        wg1=np.ascontiguousarray(np.asarray(L1_ffn_w_gate)),
        wu1=np.ascontiguousarray(np.asarray(L1_ffn_w_up)),
        wd1=np.ascontiguousarray(np.asarray(L1_ffn_w_down)),
        **attn_host_consts(),
    )
    shared = _u16(shared)
    return dict(x=x, shared=shared)


def kernel(**inputs):
    return kernel_fused(**inputs)


# version 20
# speedup vs baseline: 1.1765x; 1.1765x over previous
import math
from contextlib import ExitStack

import numpy as np
import ml_dtypes
import concourse.bass as bass
import concourse.mybir as mybir
from concourse.bass_utils import run_bass_kernel_spmd

F32 = mybir.dt.float32
BF16 = mybir.dt.bfloat16
I32 = mybir.dt.int32
AF = mybir.ActivationFunctionType
ALU = mybir.AluOpType
AX = mybir.AxisListType
BF = ml_dtypes.bfloat16

T = 4096
D = 1024
E = 2048
NST = 16
H0 = 3072
I1 = 2816
NH, HD = 16, 64
NT128 = T // 128
NT512 = T // 512


class _IP:

    def __init__(self, ins, pend):
        self._ins = ins
        self._pend = pend

    def then_inc(self, sem, k):
        self._pend[sem.num] = self._pend.get(sem.num, 0) + k
        self._ins.then_inc(sem, k)
        return self

    def annotate(self, a):
        self._ins.annotate(a)
        return self

    def __getattr__(self, name):
        return getattr(self._ins, name)


class _QP:

    def __init__(self, q, base, pend):
        self._q = q
        self._base = base
        self._pend = pend

    def wait_ge(self, sem, val):
        return self._q.wait_ge(sem, val + self._base.get(sem.num, 0))

    def sem_inc(self, sem, k):
        self._pend[sem.num] = self._pend.get(sem.num, 0) + k
        return self._q.sem_inc(sem, k)

    def __getattr__(self, name):
        attr = getattr(self._q, name)
        if not callable(attr):
            return attr
        pend = self._pend

        def call(*a, **kw):
            r = attr(*a, **kw)
            if r is not None and hasattr(r, "then_inc"):
                return _IP(r, pend)
            return r
        return call


class _BlockWrap:
    def __init__(self, p):
        self._p = p
        self._blk = None

    def __enter__(self):
        self._blk = self._p.nc.Block()
        self._blk.__enter__()
        return self

    def __exit__(self, *a):
        r = self._blk.__exit__(*a)
        for k, v in self._p._pend.items():
            self._p._base[k] = self._p._base.get(k, 0) + v
        self._p._pend.clear()
        return r

    def _mk(self, engname):
        def reg(f):
            getattr(self._blk, engname)(
                lambda q: f(_QP(q, self._p._base, self._p._pend)))
        return reg

    def __getattr__(self, name):
        if name in ("sync", "vector", "scalar", "tensor", "gpsimd"):
            return self._mk(name)
        return getattr(self._blk, name)


class Prog:
    def __init__(self):
        self.nc = bass.Bass("TRN2", target_bir_lowering=False)
        self.stack = ExitStack()
        self._names = set()
        self._scopes = []
        self._tag = ""
        self._base = {}
        self._pend = {}

    def _st(self):
        return self._scopes[-1] if self._scopes else self.stack

    def push(self, tag):
        self._scopes.append(ExitStack())
        self._tag = tag + "_"

    def pop(self):
        self._scopes.pop().close()
        self._tag = ""

    def sb(self, name, shape, dtype):
        name = self._tag + name
        assert name not in self._names, name
        self._names.add(name)
        return self._st().enter_context(self.nc.sbuf_tensor(name, shape, dtype))

    def ps(self, name, shape, dtype):
        name = self._tag + name
        assert name not in self._names, name
        self._names.add(name)
        return self._st().enter_context(self.nc.psum_tensor(name, shape, dtype))

    def sem(self, name):
        name = self._tag + name
        assert name not in self._names, name
        self._names.add(name)
        return self._st().enter_context(self.nc.semaphore(name))

    def block(self):
        return _BlockWrap(self)

    def close(self):
        self.stack.close()


class WeightLoad:

    def __init__(self, tag, dram_ap, O, IN, dest):
        self.tag = tag
        self.dram = dram_ap
        self.O, self.IN = O, IN
        self.dest = dest
        self.nj = IN // 128

    def emit_loads(self, q, sem):
        for j in range(self.nj):
            q.dma_start(
                out=self.dest[j][:],
                in_=self.dram[j * 128:(j + 1) * 128, :],
            ).then_inc(sem, 16)

    def full(self):
        return 16 * self.nj


def build_ffn0(p: Prog, io=None):
    nc = p.nc
    if io is None:
        xin = nc.dram_tensor("xin", [T, D], F32, kind="ExternalInput").ap()
        wg_d = nc.dram_tensor("wg", [D, H0], I32, kind="ExternalInput").ap()
        wu_d = nc.dram_tensor("wu", [D, H0], I32, kind="ExternalInput").ap()
        wd_d = nc.dram_tensor("wd", [H0, D], I32, kind="ExternalInput").ap()
        ident_d = nc.dram_tensor("ident", [128, 128], BF16, kind="ExternalInput").ap()
        xout = nc.dram_tensor("xout", [T, D], F32, kind="ExternalOutput").ap()
    else:
        xin, wg_d, wu_d, wd_d, ident_d, xout = io

    NKH = H0 // 128
    ND = D // 128

    wgT = [p.sb(f"wgT{c}", [128, H0], BF16) for c in range(ND)]
    wuT = [p.sb(f"wuT{c}", [128, H0], BF16) for c in range(ND)]
    wdT = [p.sb(f"wdT{k}", [128, D], BF16) for k in range(NKH)]

    xs = [p.sb(f"xs{b}", [128, D], F32) for b in range(2)]
    xr = [p.sb(f"xr{b}", [128, 512], F32) for b in range(2)]
    xo = [p.sb(f"xo{b}", [128, 512], F32) for b in range(2)]
    xn = p.sb("xn", [128, D], BF16)
    xnT = [p.sb(f"xnT{c}", [128, 512], BF16) for c in range(ND)]
    hsb = [p.sb(f"hsb{k}", [128, 512], BF16) for k in range(NKH)]
    sgb = [p.sb(f"sgb{b}", [128, 512], BF16) for b in range(2)]
    scr = p.sb("scr", [128, D], BF16)
    ident = p.sb("identt", [128, 128], BF16)

    stt = {nm: [p.sb(f"st_{nm}{i}", [128, 1], F32) for i in range(4)]
           for nm in ["s1", "s2", "mu", "v1", "m2", "var", "ln", "rs", "rs2", "nm"]}

    ps_tp = [p.ps(f"ps_tp{b}", [128, 128], BF16) for b in range(2)]
    ps_g = [p.ps(f"ps_g{b}", [128, 512], F32) for b in range(2)]
    ps_u = [p.ps(f"ps_u{b}", [128, 512], F32) for b in range(2)]
    ps_o = [p.ps(f"ps_o{b}", [128, 512], F32) for b in range(2)]

    s = {nm: p.sem(nm) for nm in ["wlg", "wlu", "wld", "li",
                                  "ldp0", "ldp1", "lrp0", "lrp1",
                                  "ve", "vv", "sc", "pe", "stp0", "stp1", "vs", "ss"]}

    wg_p = WeightLoad("wg", wg_d, H0, D, wgT)
    wu_p = WeightLoad("wu", wu_d, H0, D, wuT)
    wd_p = WeightLoad("wd", wd_d, D, H0, wdT)

    INV127 = 1.0 / 127.0

    VE = 68

    def ve_ln(si, i):
        return VE * si + 9 * i + 1

    def ve_ev(si, j):
        i, c = j // 8, j % 8
        return VE * si + 9 * i + 2 + c

    def ve_h(si, k):
        return VE * si + 36 + k + 1

    def ve_res(si, j):
        return VE * si + 60 + j + 1

    SC = 32

    def sc_sq(si, i):
        return SC * si + 2 * i + 1

    def sc_rs(si, i):
        return SC * si + 2 * i + 2

    def sc_si(si, k):
        return SC * si + 8 + k + 1

    PE = 88

    def pe_tp(si, j):
        return PE * si + j + 1

    def pe_g(si, k):
        return PE * si + 32 + 2 * k + 1

    def pe_u(si, k):
        return PE * si + 32 + 2 * k + 2

    def pe_o(si, j):
        return PE * si + 80 + j + 1

    def ld_x(g):
        return (f"ldp{g % 2}", 16 * (g // 2 + 1))

    def ld_r(cnt):
        return (f"lrp{cnt % 2}", 16 * (cnt // 2 + 1))

    with p.block() as blk:

        def gpsimd_f(q):
            wg_p.emit_loads(q, s["wlg"])
            wu_p.emit_loads(q, s["wlu"])
            wd_p.emit_loads(q, s["wld"])

        def scalarq_f(q):
            ssn = [0]

            def sinc(ins):
                ins.then_inc(s["ss"], 1)
                ssn[0] += 1
                return ssn[0]

            for si in range(NT512):
                for i in range(4):
                    g = si * 4 + i
                    nm_, v_ = ld_x(g)
                    q.wait_ge(s[nm_], v_)
                    prior = SC * si + 2 * i
                    if prior > 0:
                        q.wait_ge(s["sc"], prior)
                    q.activation(scr[:], xs[g % 2][:], AF.Square,
                                 accum_out=stt["s2"][i][:]).then_inc(s["sc"], 1)
                    q.wait_ge(s["vv"], g + 1)
                    n0 = sinc(q.activation(stt["ln"][i][:], stt["var"][i][:], AF.Ln))
                    q.wait_ge(s["ss"], n0)
                    q.activation(stt["rs"][i][:], stt["ln"][i][:], AF.Exp,
                                 scale=-0.5).then_inc(s["sc"], 1)
                for k in range(NKH):
                    q.wait_ge(s["pe"], pe_g(si, k))
                    if k >= 2 or si > 0:
                        kk, ss_ = (k - 2, si) if k >= 2 else (22 + k, si - 1)
                        q.wait_ge(s["ve"], ve_h(ss_, kk))
                    q.activation(sgb[k % 2][:], ps_g[k % 2][:],
                                 AF.Silu).then_inc(s["sc"], 1)

        def sync_f(q):
            q.dma_start(out=ident[:], in_=ident_d).then_inc(s["li"], 16)
            for si in range(NT512):
                for i in range(4):
                    g = si * 4 + i
                    if g >= 2:
                        q.wait_ge(s["ve"], ve_ln((g - 2) // 4, (g - 2) % 4))
                    q.dma_start(
                        out=xs[g % 2][:],
                        in_=xin[g * 128:(g + 1) * 128, :],
                    ).then_inc(s[f"ldp{g % 2}"], 16)
                def emit_store(jj):
                    sub_, dn_ = jj // 2, jj % 2
                    gj_ = si * 8 + jj
                    q.wait_ge(s["ve"], ve_res(si, jj))
                    q.dma_start(
                        out=xout[(si * 4 + sub_) * 128:(si * 4 + sub_ + 1) * 128,
                                 dn_ * 512:(dn_ + 1) * 512],
                        in_=xo[jj % 2][:],
                    ).then_inc(s[f"stp{gj_ % 2}"], 16)

                for j in range(8):
                    sub, dn = j // 2, j % 2
                    gj = si * 8 + j
                    if gj >= 2:
                        pj = gj - 2
                        q.wait_ge(s["ve"], ve_res(pj // 8, pj % 8))
                    q.dma_start(
                        out=xr[j % 2][:],
                        in_=xin[(si * 4 + sub) * 128:(si * 4 + sub + 1) * 128,
                                dn * 512:(dn + 1) * 512],
                    ).then_inc(s[f"lrp{gj % 2}"], 16)
                    if j >= 2:
                        emit_store(j - 2)
                emit_store(6)
                emit_store(7)
            q.wait_ge(s["stp0"], 16 * NT512 * 4)
            q.wait_ge(s["stp1"], 16 * NT512 * 4)

        def vector_f(q):
            vsn = [0]

            def vinc(ins):
                ins.then_inc(s["vs"], 1)
                vsn[0] += 1
                return vsn[0]

            def vwait(n):
                if n > 0:
                    q.wait_ge(s["vs"], n)

            for si in range(NT512):
                ev = {}
                for i in range(4):
                    g = si * 4 + i
                    nm_, v_ = ld_x(g)
                    q.wait_ge(s[nm_], v_)
                    ev[("s1", i)] = vinc(q.reduce_sum(stt["s1"][i][:],
                                                      xs[g % 2][:], AX.X))
                    vwait(ev[("s1", i)])
                    ev[("mu", i)] = vinc(q.tensor_scalar(
                        stt["mu"][i][:], stt["s1"][i][:], 1.0 / D, None, ALU.mult))
                    q.wait_ge(s["sc"], sc_sq(si, i))
                    ev[("v1", i)] = vinc(q.tensor_scalar(
                        stt["v1"][i][:], stt["s2"][i][:], 1.0 / D, 1e-5,
                        ALU.mult, ALU.add))
                    vwait(ev[("mu", i)])
                    ev[("m2", i)] = vinc(q.scalar_tensor_tensor(
                        stt["m2"][i][:], stt["mu"][i][:], -1.0, stt["mu"][i][:],
                        ALU.mult, ALU.mult))
                    vwait(ev[("m2", i)])
                    ins = q.tensor_tensor(stt["var"][i][:], stt["v1"][i][:],
                                          stt["m2"][i][:], ALU.add)
                    ev[("var", i)] = vinc(ins)
                    vwait(ev[("var", i)])
                    q.sem_inc(s["vv"], 1)
                    q.wait_ge(s["sc"], sc_rs(si, i))
                    ev[("nm0", i)] = vinc(q.tensor_scalar(
                        stt["nm"][i][:], stt["rs"][i][:], INV127, None, ALU.mult))
                    vwait(ev[("nm0", i)])
                    ev[("nm", i)] = vinc(q.scalar_tensor_tensor(
                        stt["nm"][i][:], stt["mu"][i][:], -1.0, stt["nm"][i][:],
                        ALU.mult, ALU.mult))
                    ev[("rs2", i)] = vinc(q.tensor_scalar(
                        stt["rs2"][i][:], stt["rs"][i][:], INV127, None, ALU.mult))
                    if g > 0:
                        pg = g - 1
                        q.wait_ge(s["pe"], pe_tp(pg // 4, (pg % 4) * 8 + 7))
                    vwait(max(ev[("rs2", i)], ev[("nm", i)]))
                    q.tensor_scalar(
                        xn[:], xs[g % 2][:], stt["rs2"][i].ap()[:, 0:1],
                        stt["nm"][i].ap()[:, 0:1], ALU.mult, ALU.add,
                    ).then_inc(s["ve"], 1)
                    for c in range(8):
                        j = i * 8 + c
                        q.wait_ge(s["pe"], pe_tp(si, j))
                        q.tensor_copy(xnT[c].ap()[:, i * 128:(i + 1) * 128],
                                      ps_tp[j % 2][:]).then_inc(s["ve"], 1)
                for k in range(NKH):
                    q.wait_ge(s["sc"], sc_si(si, k))
                    q.wait_ge(s["pe"], pe_u(si, k))
                    q.scalar_tensor_tensor(hsb[k][:], sgb[k % 2][:], INV127,
                                           ps_u[k % 2][:], ALU.mult,
                                           ALU.mult).then_inc(s["ve"], 1)
                for j in range(8):
                    gj = si * 8 + j
                    q.wait_ge(s["pe"], pe_o(si, j))
                    nm_, v_ = ld_r(gj)
                    q.wait_ge(s[nm_], v_)
                    if gj >= 2:
                        q.wait_ge(s[f"stp{gj % 2}"], 16 * ((gj - 2) // 2 + 1))
                    q.tensor_tensor(xo[j % 2][:], xr[j % 2][:], ps_o[j % 2][:],
                                    ALU.add).then_inc(s["ve"], 1)

        def tensor_f(q):
            q.wait_ge(s["li"], 16)
            for si in range(NT512):
                for j in range(32):
                    i, c = j // 8, j % 8
                    q.wait_ge(s["ve"], ve_ln(si, i))
                    gj = si * 32 + j
                    if gj >= 2:
                        pj = gj - 2
                        q.wait_ge(s["ve"], ve_ev(pj // 32, pj % 32))
                    q.transpose(ps_tp[j % 2][:],
                                xn.ap()[:, c * 128:(c + 1) * 128],
                                ident[:]).then_inc(s["pe"], 1)
                for k in range(NKH):
                    if si == 0 and k == 0:
                        q.wait_ge(s["wlg"], wg_p.full())
                        q.wait_ge(s["wlu"], wu_p.full())
                    if k == 0:
                        q.wait_ge(s["ve"], VE * si + 36)
                    if k >= 2 or si > 0:
                        kk, ss = (k - 2, si) if k >= 2 else (22 + k, si - 1)
                        q.wait_ge(s["sc"], sc_si(ss, kk))
                    for c in range(ND):
                        ins = q.matmul(ps_g[k % 2][:],
                                       wgT[c].ap()[:, k * 128:(k + 1) * 128],
                                       xnT[c][:], start=(c == 0),
                                       stop=(c == ND - 1))
                        if c == ND - 1:
                            ins.then_inc(s["pe"], 1)
                    if k >= 2 or si > 0:
                        kk, ss = (k - 2, si) if k >= 2 else (22 + k, si - 1)
                        q.wait_ge(s["ve"], ve_h(ss, kk))
                    for c in range(ND):
                        ins = q.matmul(ps_u[k % 2][:],
                                       wuT[c].ap()[:, k * 128:(k + 1) * 128],
                                       xnT[c][:], start=(c == 0),
                                       stop=(c == ND - 1))
                        if c == ND - 1:
                            ins.then_inc(s["pe"], 1)
                for j in range(8):
                    sub, dn = j // 2, j % 2
                    if si == 0 and j == 0:
                        q.wait_ge(s["wld"], wd_p.full())
                    q.wait_ge(s["ve"], ve_h(si, NKH - 1))
                    if j >= 2 or si > 0:
                        jj, ss = (j - 2, si) if j >= 2 else (6 + j, si - 1)
                        q.wait_ge(s["ve"], ve_res(ss, jj))
                    for k in range(NKH):
                        ins = q.matmul(ps_o[j % 2][:],
                                       hsb[k].ap()[:, sub * 128:(sub + 1) * 128],
                                       wdT[k].ap()[:, dn * 512:(dn + 1) * 512],
                                       start=(k == 0), stop=(k == NKH - 1))
                        if k == NKH - 1:
                            ins.then_inc(s["pe"], 1)

        blk.gpsimd(gpsimd_f)
        blk.scalar(scalarq_f)
        blk.sync(sync_f)
        blk.vector(vector_f)
        blk.tensor(tensor_f)

    return nc


def host_ident():
    return np.eye(128).astype(BF).view(np.uint16)


def ffn0_ref_np(x):
    raise NotImplementedError


I16 = mybir.dt.int16
I1H = I1 // 2
NKI = I1H // 128
M05 = float(3 * 2 ** 21)


class QuantEmitter:

    def __init__(self, p: Prog, maxw):
        nb = maxw // 16
        self.m = p.sb("qt_m", [128, nb], F32)
        self.sc6 = p.sb("qt_sc6", [128, nb], F32)
        self.rinv = p.sb("qt_rinv", [128, nb], F32)
        self.sb_ = p.sb("qt_sb", [128, nb], BF16)
        self.t = p.sb("qt_t", [128, maxw], F32)
        self.r05 = p.sb("qt_r05", [128, maxw], BF16)
        self.vka = p.sb("qt_vka", [128, maxw], I16)
        self.vkb = p.sb("qt_vkb", [128, maxw], I16)
        self.A = p.sb("qt_A", [128, maxw], BF16)
        self.B = p.sb("qt_B", [128, maxw], BF16)
        self.u = p.sb("qt_u", [128, maxw], BF16)
        self.qq = p.sb("qt_qq", [128, maxw], BF16)

    def emit(self, q, vinc, vwait, src_ap, dst_ap, W, vcur=None):
        nb = W // 16
        if vcur is not None:
            vwait(vcur())
        ev = {}
        ev["m"] = vinc(q.tensor_reduce(self.m.ap()[:, 0:nb],
                                       src_ap.rearrange("p (a b) -> p a b", b=16),
                                       AX.X, ALU.max, apply_absolute_value=True))
        vwait(ev["m"])
        ev["sc6"] = vinc(q.tensor_scalar(self.sc6.ap()[:, 0:nb], self.m.ap()[:, 0:nb],
                                         1.0 / 6.0, 1e-12, ALU.mult, ALU.max))
        vwait(ev["sc6"])
        ev["rinv"] = vinc(q.reciprocal(self.rinv.ap()[:, 0:nb], self.sc6.ap()[:, 0:nb]))
        ev["sb"] = vinc(q.tensor_scalar(self.sb_.ap()[:, 0:nb], self.sc6.ap()[:, 0:nb],
                                        1.0, None, ALU.mult))
        vwait(ev["rinv"])
        ev["t"] = vinc(q.tensor_tensor(
            self.t.ap()[:, 0:W].rearrange("p (a b) -> p a b", b=16),
            src_ap.rearrange("p (a b) -> p a b", b=16),
            self.rinv.ap()[:, 0:nb].unsqueeze(2).broadcast_to([128, nb, 16]),
            ALU.mult))
        vwait(ev["t"])
        ev["r05"] = vinc(q.tensor_scalar(self.r05.ap()[:, 0:W], self.t.ap()[:, 0:W],
                                         M05, -M05, ALU.add, ALU.add))
        ev["vka"] = vinc(q.tensor_scalar(self.vka.ap()[:, 0:W],
                                         self.t.ap()[:, 0:W].bitcast(I16)[:, 1::2],
                                         0x20, None, ALU.add))
        vwait(ev["vka"])
        ev["vkb"] = vinc(q.tensor_scalar(self.vkb.ap()[:, 0:W], self.vka.ap()[:, 0:W],
                                         -64, None, ALU.bitwise_and))
        vwait(ev["r05"])
        ev["A"] = vinc(q.tensor_scalar(self.A.ap()[:, 0:W], self.r05.ap()[:, 0:W],
                                       -2.0, 2.0, ALU.max, ALU.min))
        vwait(ev["vkb"])
        ev["B"] = vinc(q.tensor_scalar(self.B.ap()[:, 0:W],
                                       self.vkb.ap()[:, 0:W].bitcast(BF16),
                                       -2.0, 2.0, ALU.max, ALU.min))
        vwait(ev["B"])
        ev["u"] = vinc(q.tensor_tensor(self.u.ap()[:, 0:W],
                                       self.vkb.ap()[:, 0:W].bitcast(BF16),
                                       self.B.ap()[:, 0:W], ALU.subtract))
        vwait(max(ev["A"], ev["u"]))
        ev["qq"] = vinc(q.tensor_tensor(self.qq.ap()[:, 0:W], self.A.ap()[:, 0:W],
                                        self.u.ap()[:, 0:W], ALU.add))
        vwait(max(ev["qq"], ev["sb"]))
        ins = q.tensor_tensor(
            dst_ap.rearrange("p (a b) -> p a b", b=16),
            self.qq.ap()[:, 0:W].rearrange("p (a b) -> p a b", b=16),
            self.sb_.ap()[:, 0:nb].unsqueeze(2).broadcast_to([128, nb, 16]),
            ALU.mult)
        c = vinc(ins)
        vwait(c)
        return ins, c



class FWPrep:

    def __init__(self, p, fwprep):
        wg_raw, wu_raw, wd_raw, wgq_sp, wuq_sp, wdq_sp = fwprep
        FQW = 704
        self.fwn = [p.sb(f"fwn{b}", [128, FQW], F32) for b in range(2)]
        self.fwqn = [p.sb(f"fwqn{b}", [128, FQW], BF16) for b in range(2)]
        self.qe = QuantEmitter(p, FQW)
        self.sl = [p.sem("fwl0"), p.sem("fwl1")]
        self.sq = p.sem("fwq")
        self.ss = p.sem("fws")
        pieces = []
        for raw, sp in ((wg_raw, wgq_sp), (wu_raw, wuq_sp)):
            for k in range(I1 // 128):
                for (c0, w) in ((0, 704), (704, 320)):
                    pieces.append((raw[k * 128:(k + 1) * 128, c0:c0 + w],
                                   sp[k * 128:(k + 1) * 128, c0:c0 + w], w))
        for k in range(D // 128):
            for j4 in range(4):
                c0 = j4 * 704
                pieces.append((wd_raw[k * 128:(k + 1) * 128, c0:c0 + 704],
                               wdq_sp[k * 128:(k + 1) * 128, c0:c0 + 704], 704))
        self.pieces = pieces
        self.NFP = len(pieces)
        self.vg = [0]

    def run_gp(self, q):
        for g in range(self.NFP):
            if g >= 2:
                q.wait_ge(self.sq, g - 1)
            src, dst, w = self.pieces[g]
            q.dma_start(out=self.fwn[g % 2].ap()[:, 0:w],
                        in_=src).then_inc(self.sl[g % 2], 16)
            if g >= 2:
                ps_, pd_, pw_ = self.pieces[g - 2]
                q.dma_start(out=pd_,
                            in_=self.fwqn[(g - 2) % 2].ap()[:, 0:pw_]
                            ).then_inc(self.ss, 16)
        q.wait_ge(self.sq, self.NFP)
        for g in (self.NFP - 2, self.NFP - 1):
            src, dst, w = self.pieces[g]
            q.dma_start(out=dst,
                        in_=self.fwqn[g % 2].ap()[:, 0:w]).then_inc(self.ss, 16)
        q.wait_ge(self.ss, 16 * self.NFP)

    def pump_v(self, q, vinc, vwait):
        if self.vg[0] >= self.NFP:
            return
        g = self.vg[0]
        q.wait_ge(self.sl[g % 2], 16 * (g // 2 + 1))
        if g >= 2:
            q.wait_ge(self.ss, 16 * (g - 1))
        src, dst, w = self.pieces[g]
        self.qe.emit(q, vinc, vwait, self.fwn[g % 2].ap()[:, 0:w],
                     self.fwqn[g % 2].ap()[:, 0:w], w, vcur=None)
        q.sem_inc(self.sq, 1)
        self.vg[0] += 1

    def drain_v(self, q, vinc, vwait):
        while self.vg[0] < self.NFP:
            self.pump_v(q, vinc, vwait)


def build_ffn1(p: Prog, half=None, io=None):
    nc = p.nc
    if io is None:
        xacc = nc.dram_tensor("xacc", [T, D], F32, kind="ExternalInput").ap()
        wgq_d = nc.dram_tensor("wgq", [I1, D], BF16, kind="ExternalInput").ap()
        wuq_d = nc.dram_tensor("wuq", [I1, D], BF16, kind="ExternalInput").ap()
        wdq_d = nc.dram_tensor("wdq", [D, I1], BF16, kind="ExternalInput").ap()
        ident_d = nc.dram_tensor("ident", [128, 128], BF16, kind="ExternalInput").ap()
        xout = nc.dram_tensor("xout", [T, D], F32, kind="ExternalOutput").ap()
    else:
        xacc, wgq_d, wuq_d, wdq_d, ident_d, xout = io

    ND = D // 128
    NKF = I1 // 128
    ITS = [(0, 512), (512, 512), (1024, 384), (1408, 512), (1920, 512), (2432, 384)]
    NIT = len(ITS)
    I1H2 = I1 // 2

    wgqT = [p.sb(f"wgqT{c}", [128, I1], BF16) for c in range(ND)]
    wuqT = [p.sb(f"wuqT{c}", [128, I1], BF16) for c in range(ND)]
    wdqT = [p.sb(f"wdqT{k}", [128, D], BF16) for k in range(NKF)]

    QW = 704
    qe = QuantEmitter(p, QW)

    def qemit(q, vinc, vwait, src_ap, dst_ap, cols):
        c0 = 0
        while c0 < cols:
            w = min(QW, cols - c0)
            qe.emit(q, vinc, vwait, src_ap[:, c0:c0 + w],
                    dst_ap[:, c0:c0 + w], w, vcur=None)
            c0 += w

    xs = [p.sb(f"xs{b}", [128, D], F32) for b in range(2)]
    xnf = p.sb("xnf", [128, D], F32)
    xq = [p.sb(f"xq{b}", [128, D], BF16) for b in range(2)]
    xqT = [p.sb(f"xqT{c}", [128, 256], BF16) for c in range(ND)]
    sgf = p.sb("sgf", [128, 512], F32)
    hf = p.sb("hf", [128, I1H2], F32)
    hq = [p.sb("hq0", [128, I1], BF16)] * 2
    hqT = [[p.sb(f"hqT{b}_{k}", [128, 128], BF16) for k in range(NKF)]
           for b in range(2)]
    xr = [p.sb(f"xr{b}", [128, 512], F32) for b in range(2)]
    xo = [p.sb(f"xo{b}", [128, 512], F32) for b in range(2)]
    ident = p.sb("identt", [128, 128], BF16)
    stt = {nm: [p.sb(f"st_{nm}{i}", [128, 1], F32) for i in range(4)]
           for nm in ["s1", "s2", "mu", "v1", "m2", "var", "ln", "rs", "nm"]}

    ps_tp = [p.ps(f"ps_tp{b}", [128, 128], BF16) for b in range(2)]
    ps_g = [p.ps(f"ps_g{b}", [128, 512], F32) for b in range(2)]
    ps_u = [p.ps(f"ps_u{b}", [128, 512], F32) for b in range(2)]
    ps_o = [p.ps(f"ps_o{b}", [128, 512], F32) for b in range(2)]

    s = {nm: p.sem(nm) for nm in [
        "tw", "li",
        "ldp0", "ldp1", "lrp0", "lrp1",
        "ve", "vv", "sc", "pe", "stp0", "stp1", "vs", "ss"]}

    WEV = 0
    NTW = 16 * (2 * ND + NKF)

    SUBV = 1 + 8 + NIT + 2 + NKF + 2
    VE = 4 * SUBV

    def ve_qx(si, i):
        return WEV + VE * si + SUBV * i + 1

    def ve_evx(si, i, c):
        return WEV + VE * si + SUBV * i + 1 + c + 1

    def ve_hf(si, i, it):
        return WEV + VE * si + SUBV * i + 9 + it + (1 if it < 3 else 2)

    def ve_qh(si, i, hh):
        return WEV + VE * si + SUBV * i + 13 + 4 * hh

    def ve_evh(si, i, k):
        return WEV + VE * si + SUBV * i + 11 + NIT + k + 1

    def ve_res(si, i, dn):
        return WEV + VE * si + SUBV * i + 11 + NIT + NKF + dn + 1

    SUBS = 2 + NIT
    SC = 4 * SUBS

    def sc_sq(si, i):
        return SC * si + SUBS * i + 1

    def sc_rs(si, i):
        return SC * si + SUBS * i + 2

    def sc_si(si, i, it):
        return SC * si + SUBS * i + 2 + it + 1

    SUBP = 8 + 2 * NIT + NKF + 2
    PE = 4 * SUBP

    def pe_tpx(si, i, c):
        return WEV + PE * si + SUBP * i + c + 1

    def pe_g(si, i, it):
        return WEV + PE * si + SUBP * i + 8 + 2 * it + 1

    def pe_u(si, i, it):
        return WEV + PE * si + SUBP * i + 8 + 2 * it + 2

    def pe_tph(si, i, k):
        return WEV + PE * si + SUBP * i + 8 + 2 * NIT + k + 1

    def pe_o(si, i, dn):
        return WEV + PE * si + SUBP * i + 8 + 2 * NIT + NKF + dn + 1

    tp_evac_ve = []
    for si in range(NT512):
        for i in range(4):
            for c in range(ND):
                tp_evac_ve.append(ve_evx(si, i, c))
            for k in range(NKF):
                tp_evac_ve.append(ve_evh(si, i, k))

    with p.block() as blk:

        def sync_f(q):
            q.dma_start(out=xs[0][:], in_=xacc[0:128, :]).then_inc(s["ldp0"], 16)
            q.dma_start(out=xs[1][:], in_=xacc[128:256, :]).then_inc(s["ldp1"], 16)
            q.dma_start(out=ident[:], in_=ident_d).then_inc(s["li"], 16)
            for c in range(ND):
                q.dma_start(out=wgqT[c][:], in_=wgq_d[:, c * 128:(c + 1) * 128],
                            transpose=True).then_inc(s["tw"], 16)
            for c in range(ND):
                q.dma_start(out=wuqT[c][:], in_=wuq_d[:, c * 128:(c + 1) * 128],
                            transpose=True).then_inc(s["tw"], 16)
            for k in range(NKF):
                q.dma_start(out=wdqT[k][:], in_=wdq_d[:, k * 128:(k + 1) * 128],
                            transpose=True).then_inc(s["tw"], 16)
            for si in range(NT512):
                for i in range(4):
                    g = si * 4 + i
                    if g >= 2:
                        q.wait_ge(s["ve"], ve_qx((g - 2) // 4, (g - 2) % 4))
                        q.dma_start(out=xs[g % 2][:],
                                    in_=xacc[g * 128:(g + 1) * 128, :],
                                    ).then_inc(s[f"ldp{g % 2}"], 16)
                    for dn in range(2):
                        gj = g * 2 + dn
                        if gj >= 2:
                            pj = gj - 2
                            q.wait_ge(s["ve"], ve_res(pj // 8, (pj % 8) // 2, pj % 2))
                        q.dma_start(
                            out=xr[gj % 2][:],
                            in_=xacc[g * 128:(g + 1) * 128, dn * 512:(dn + 1) * 512],
                        ).then_inc(s[f"lrp{gj % 2}"], 16)
                        if gj >= 2:
                            pj = gj - 2
                            q.dma_start(
                                out=xout[(pj // 2) * 128:(pj // 2 + 1) * 128,
                                         (pj % 2) * 512:(pj % 2 + 1) * 512],
                                in_=xo[pj % 2][:]).then_inc(s[f"stp{pj % 2}"], 16)
            for gj in range(NT128 * 2 - 2, NT128 * 2):
                q.wait_ge(s["ve"], ve_res(gj // 8, (gj % 8) // 2, gj % 2))
                q.dma_start(
                    out=xout[(gj // 2) * 128:(gj // 2 + 1) * 128,
                             (gj % 2) * 512:(gj % 2 + 1) * 512],
                    in_=xo[gj % 2][:]).then_inc(s[f"stp{gj % 2}"], 16)
            q.wait_ge(s["stp0"], 16 * NT128)
            q.wait_ge(s["stp1"], 16 * NT128)

        def vector_f(q):
            vsn = [0]

            def vinc(ins):
                ins.then_inc(s["vs"], 1)
                vsn[0] += 1
                return vsn[0]

            def vwait(n):
                if n > 0:
                    q.wait_ge(s["vs"], n)

            for si in range(NT512):
                for i in range(4):
                    g = si * 4 + i
                    q.wait_ge(s[f"ldp{g % 2}"], 16 * (g // 2 + 1))
                    a = vinc(q.reduce_sum(stt["s1"][i][:], xs[g % 2][:], AX.X))
                    vwait(a)
                    b = vinc(q.tensor_scalar(stt["mu"][i][:], stt["s1"][i][:],
                                             1.0 / D, None, ALU.mult))
                    q.wait_ge(s["sc"], sc_sq(si, i))
                    cviv = vinc(q.tensor_scalar(stt["v1"][i][:], stt["s2"][i][:],
                                                1.0 / D, 1e-5, ALU.mult, ALU.add))
                    vwait(b)
                    d = vinc(q.scalar_tensor_tensor(stt["m2"][i][:], stt["mu"][i][:],
                                                    -1.0, stt["mu"][i][:],
                                                    ALU.mult, ALU.mult))
                    vwait(d)
                    e = vinc(q.tensor_tensor(stt["var"][i][:], stt["v1"][i][:],
                                             stt["m2"][i][:], ALU.add))
                    vwait(e)
                    q.sem_inc(s["vv"], 1)
                    q.wait_ge(s["sc"], sc_rs(si, i))
                    f = vinc(q.scalar_tensor_tensor(stt["nm"][i][:], stt["mu"][i][:],
                                                    -1.0, stt["rs"][i][:],
                                                    ALU.mult, ALU.mult))
                    vwait(f)
                    gg = vinc(q.tensor_scalar(
                        xnf[:], xs[g % 2][:], stt["rs"][i].ap()[:, 0:1],
                        stt["nm"][i].ap()[:, 0:1], ALU.mult, ALU.add))
                    vwait(gg)
                    if g >= 2:
                        q.wait_ge(s["pe"], pe_tpx((g - 2) // 4, (g - 2) % 4, 7))
                    qemit(q, vinc, vwait, xnf.ap(), xq[g % 2].ap(), D)
                    q.sem_inc(s["ve"], 1)
                    for c in range(ND):
                        q.wait_ge(s["pe"], pe_tpx(si, i, c))
                        if c == 0 and g >= 2:
                            q.wait_ge(s["pe"], pe_u((g - 2) // 4, (g - 2) % 4, NIT - 1))
                        tot = g * (8 + NKF) + c
                        q.tensor_copy(xqT[c].ap()[:, (i % 2) * 128:(i % 2 + 1) * 128],
                                      ps_tp[tot % 2][:]).then_inc(s["ve"], 1)
                    for hh2 in range(2):
                        for it in range(3 * hh2, 3 * hh2 + 3):
                            c0, w = ITS[it]
                            q.wait_ge(s["sc"], sc_si(si, i, it))
                            pit = g * NIT + it
                            q.wait_ge(s["pe"], pe_u(si, i, it))
                            vwait(vsn[0])
                            hv = vinc(q.tensor_tensor(
                                hf.ap()[:, c0 - hh2 * I1H2:c0 - hh2 * I1H2 + w],
                                sgf.ap()[:, 0:w],
                                ps_u[pit % 2][:, 0:w], ALU.mult))
                            vwait(hv)
                            q.sem_inc(s["ve"], 1)
                        if hh2 == 0 and g >= 1:
                            q.wait_ge(s["pe"], pe_tph((g - 1) // 4, (g - 1) % 4, NKF - 1))
                        qemit(q, vinc, vwait, hf.ap()[:, 0:I1H2],
                              hq[0].ap()[:, hh2 * I1H2:(hh2 + 1) * I1H2], I1H2)
                        q.sem_inc(s["ve"], 1)
                    for k in range(NKF):
                        q.wait_ge(s["pe"], pe_tph(si, i, k))
                        if k == 0 and g >= 2:
                            q.wait_ge(s["pe"], pe_o((g - 2) // 4, (g - 2) % 4, 1))
                        tot = g * (8 + NKF) + 8 + k
                        q.tensor_copy(hqT[g % 2][k][:],
                                      ps_tp[tot % 2][:]).then_inc(s["ve"], 1)
                    for dn in range(2):
                        gj = g * 2 + dn
                        q.wait_ge(s["pe"], pe_o(si, i, dn))
                        q.wait_ge(s[f"lrp{gj % 2}"], 16 * (gj // 2 + 1))
                        if gj >= 2:
                            q.wait_ge(s[f"stp{gj % 2}"], 16 * ((gj - 2) // 2 + 1))
                        q.tensor_tensor(xo[gj % 2][:], xr[gj % 2][:],
                                        ps_o[dn % 2][:], ALU.add).then_inc(s["ve"], 1)

        def scalar_f(q):
            ssn = [0]

            def sinc(ins):
                ins.then_inc(s["ss"], 1)
                ssn[0] += 1
                return ssn[0]

            for si in range(NT512):
                for i in range(4):
                    g = si * 4 + i
                    q.wait_ge(s[f"ldp{g % 2}"], 16 * (g // 2 + 1))
                    prior = SC * si + SUBS * i
                    if prior > 0:
                        q.wait_ge(s["sc"], prior)
                    q.activation(xnf[:], xs[g % 2][:], AF.Square,
                                 accum_out=stt["s2"][i][:]).then_inc(s["sc"], 1)
                    q.wait_ge(s["vv"], g + 1)
                    n0 = sinc(q.activation(stt["ln"][i][:], stt["var"][i][:], AF.Ln))
                    q.wait_ge(s["ss"], n0)
                    q.activation(stt["rs"][i][:], stt["ln"][i][:], AF.Exp,
                                 scale=-0.5).then_inc(s["sc"], 1)
                    for it, (c0, w) in enumerate(ITS):
                        q.wait_ge(s["pe"], pe_g(si, i, it))
                        pit = g * NIT + it
                        if pit >= 1:
                            pp = pit - 1
                            q.wait_ge(s["ve"], ve_hf(pp // (4 * NIT),
                                                     (pp % (4 * NIT)) // NIT,
                                                     pp % NIT))
                        q.activation(sgf.ap()[:, 0:w], ps_g[pit % 2][:, 0:w],
                                     AF.Silu).then_inc(s["sc"], 1)

        def tensor_f(q):
            q.wait_ge(s["li"], 16)
            q.wait_ge(s["tw"], 16 * 2 * ND)
            for si in range(NT512):
                for i in range(4):
                    g = si * 4 + i
                    q.wait_ge(s["ve"], ve_qx(si, i))
                    for c in range(ND):
                        tot = g * (8 + NKF) + c
                        if tot >= 2:
                            q.wait_ge(s["ve"], tp_evac_ve[tot - 2])
                        q.transpose(ps_tp[tot % 2][:],
                                    xq[g % 2].ap()[:, c * 128:(c + 1) * 128],
                                    ident[:]).then_inc(s["pe"], 1)
                    for it, (c0, w) in enumerate(ITS):
                        pit = g * NIT + it
                        if pit >= 2:
                            pp = pit - 2
                            q.wait_ge(s["sc"], sc_si(pp // (4 * NIT),
                                                     (pp % (4 * NIT)) // NIT,
                                                     pp % NIT))
                        q.wait_ge(s["ve"], ve_evx(si, i, 7))
                        for c in range(ND):
                            ins = q.matmul(ps_g[pit % 2][:, 0:w],
                                           xqT[c].ap()[:, (i % 2) * 128:(i % 2 + 1) * 128],
                                           wgqT[c].ap()[:, c0:c0 + w],
                                           start=(c == 0), stop=(c == ND - 1))
                            if c == ND - 1:
                                ins.then_inc(s["pe"], 1)
                        if pit >= 2:
                            pp = pit - 2
                            q.wait_ge(s["ve"], ve_hf(pp // (4 * NIT),
                                                     (pp % (4 * NIT)) // NIT,
                                                     pp % NIT))
                        for c in range(ND):
                            ins = q.matmul(ps_u[pit % 2][:, 0:w],
                                           xqT[c].ap()[:, (i % 2) * 128:(i % 2 + 1) * 128],
                                           wuqT[c].ap()[:, c0:c0 + w],
                                           start=(c == 0), stop=(c == ND - 1))
                            if c == ND - 1:
                                ins.then_inc(s["pe"], 1)
                    for k in range(NKF):
                        if k < I1H2 // 128:
                            q.wait_ge(s["ve"], ve_qh(si, i, 0))
                        else:
                            q.wait_ge(s["ve"], ve_qh(si, i, 1))
                        tot = g * (8 + NKF) + 8 + k
                        q.wait_ge(s["ve"], tp_evac_ve[tot - 2])
                        q.transpose(ps_tp[tot % 2][:],
                                    hq[0].ap()[:, k * 128:(k + 1) * 128],
                                    ident[:]).then_inc(s["pe"], 1)
                    for dn in range(2):
                        if g == 0 and dn == 0:
                            q.wait_ge(s["tw"], NTW)
                        q.wait_ge(s["ve"], ve_evh(si, i, NKF - 1))
                        gj = g * 2 + dn
                        if gj >= 2:
                            pj = gj - 2
                            q.wait_ge(s["ve"], ve_res(pj // 8, (pj % 8) // 2, pj % 2))
                        for k in range(NKF):
                            ins = q.matmul(ps_o[dn % 2][:],
                                           hqT[g % 2][k][:],
                                           wdqT[k].ap()[:, dn * 512:(dn + 1) * 512],
                                           start=(k == 0), stop=(k == NKF - 1))
                            if k == NKF - 1:
                                ins.then_inc(s["pe"], 1)

        blk.sync(sync_f)
        blk.vector(vector_f)
        blk.scalar(scalar_f)
        blk.tensor(tensor_f)

    return nc


def build_attn(p: Prog, io=None):
    nc = p.nc
    if io is None:
        xin = nc.dram_tensor("xin", [T, D], F32, kind="ExternalInput").ap()
        wq_d = nc.dram_tensor("wq", [D, D], F32, kind="ExternalInput").ap()
        wk_d = nc.dram_tensor("wk", [D, D], F32, kind="ExternalInput").ap()
        wv_d = nc.dram_tensor("wv", [D, D], F32, kind="ExternalInput").ap()
        wo_d = nc.dram_tensor("wo", [D, D], F32, kind="ExternalInput").ap()
        bcol_d = nc.dram_tensor("bcol", [128, 2 * 8], F32, kind="ExternalInput").ap()
        bvb_d = nc.dram_tensor("bvb", [128, D], F32, kind="ExternalInput").ap()
        bob_d = nc.dram_tensor("bob", [128, D], F32, kind="ExternalInput").ap()
        cos_d = nc.dram_tensor("cos_t", [128, T], BF16, kind="ExternalInput").ap()
        sin_d = nc.dram_tensor("sin_t", [128, T], BF16, kind="ExternalInput").ap()
        msk_d = nc.dram_tensor("msk", [128, 4 * 512], BF16, kind="ExternalInput").ap()
        ident_d = nc.dram_tensor("ident", [128, 128], BF16, kind="ExternalInput").ap()
        negC_d = nc.dram_tensor("negC", [128, 1], F32, kind="ExternalInput").ap()
        ones_d = nc.dram_tensor("ones_r", [1, 64], F32, kind="ExternalInput").ap()
        xout = nc.dram_tensor("xout", [T, D], F32, kind="ExternalOutput").ap()
    else:
        (xin, wq_d, wk_d, wv_d, wo_d, bcol_d, bvb_d, bob_d,
         cos_d, sin_d, msk_d, ident_d, negC_d, ones_d, xout) = io

    ND = D // 128
    NQT = 8
    q_scr = nc.dram_tensor("q_scr", [D, T], BF16, kind="Internal").ap()
    k_scr = nc.dram_tensor("k_scr", [D, T], BF16, kind="Internal").ap()
    v_scr = nc.dram_tensor("v_scr", [T, NH * 65], BF16, kind="Internal").ap()

    wqT = [p.sb(f"wqT{c}", [128, D], BF16) for c in range(ND)]
    wkT = [p.sb(f"wkT{c}", [128, D], BF16) for c in range(ND)]
    wvT = [p.sb(f"wvT{c}", [128, D], BF16) for c in range(ND)]
    bcol = p.sb("bcol_s", [128, 2 * 8], F32)
    bvb = p.sb("bvb_s", [128, D], F32)
    bob = p.sb("bob_s", [128, D], F32)
    cos_t = p.sb("cos_sb", [128, T], BF16)
    sin_t = p.sb("sin_sb", [128, T], BF16)
    xs = [p.sb(f"xs{b}", [128, D], F32) for b in range(2)]
    xn = p.sb("xn", [128, D], BF16)
    xnT = [p.sb(f"xnT{c}", [128, 512], BF16) for c in range(ND)]
    qb = p.sb("qb", [128, 512], F32)
    mm1 = p.sb("mm1", [128, 512], BF16)
    mm2 = p.sb("mm2", [128, 512], BF16)
    qro = [p.sb(f"qro{b}", [128, 512], BF16) for b in range(4)]
    vst = [p.sb(f"vst{b}", [128, 520], BF16) for b in range(2)]
    stt = {nm: [p.sb(f"st_{nm}{i}", [128, 1], F32) for i in range(4)]
           for nm in ["s1", "s2", "mu", "v1", "m2", "var", "ln", "rs", "nm"]}
    scr = p.sb("scr", [128, D], BF16)
    ident = p.sb("identt", [128, 128], BF16)
    negC = p.sb("negC_s", [128, 1], F32)
    onesr = p.sb("onesr", [1, 64], F32)

    qtl = [[p.sb(f"qtl{b}_{c}", [128, 512], BF16) for c in range(ND)]
           for b in range(2)]
    ktl = [p.sb(f"ktl{b}", [128, 512], BF16) for b in range(2)]
    vtl = [p.sb(f"vtl{b}", [128, 130], BF16) for b in range(4)]
    pt = p.sb("pt", [128, 2048], BF16)
    pm = [p.sb(f"pm{b}", [128, 512], BF16) for b in range(4)]
    msk = p.sb("msk_sb", [128, 4 * 512], BF16)
    yall = [p.sb(f"yall{hp}", [128, 512], BF16) for hp in range(ND)]
    dden = [p.sb(f"dden{h}", [1, 512], F32) for h in range(2)]
    lnscr = p.sb("lnscr", [64, 512], F32)
    bcb = p.sb("bcb", [64, 1024], F32)
    woT = [p.sb(f"woT{c}", [128, D], BF16) for c in range(ND)]
    xr = [p.sb(f"xr{b}", [128, 512], F32) for b in range(2)]
    xo = [p.sb(f"xo{b}", [128, 512], F32) for b in range(2)]

    psA = p.ps("psA", [128, 2048], F32)
    psB = p.ps("psB", [128, 2048], F32)
    psB_bf = psB.ap().bitcast(BF16)

    def tp_view(tot):
        b = tot % 2
        return psB_bf[:, b * 1024:b * 1024 + 128]

    def qk_view(aidx):
        b = aidx % 2
        return psA.ap()[:, b * 512:(b + 1) * 512]

    s_view = psA.ap()
    y_view = [psB.ap()[0:65, 0:512], psB.ap()[0:65, 512:1024]]
    bc_view = psB.ap()[0:64, 1024:1536]
    o_view = psB.ap()[:, 1536:2048]

    s = {nm: p.sem(nm) for nm in [
        "wl", "li",
        "ldp0", "ldp1", "lq", "lk0", "lk1", "lk2", "lk3",
        "lv0", "lv1", "lv2", "lv3", "lrp0", "lrp1",
        "ve", "vv", "sc", "pe",
        "sq0", "sq1", "sk0", "sk1", "sv0", "sv1", "stp0", "stp1", "vs", "ss"]}

    VEA = 60

    def ve_ln(si, i):
        return VEA * si + 9 * i + 1

    def ve_ev(si, i, c):
        return VEA * si + 9 * i + 1 + c + 1

    def ve_qr(si, oc):
        return VEA * si + 36 + 2 * oc + 1

    def ve_kr(si, oc):
        return VEA * si + 36 + 2 * oc + 2

    def ve_vv(si, j):
        return VEA * si + 52 + j + 1

    VEA_TOT = VEA * NT512

    SCA = 8

    def sc_sq(si, i):
        return SCA * si + 2 * i + 1

    def sc_rs(si, i):
        return SCA * si + 2 * i + 2

    SCA_TOT = SCA * NT512

    PEA = 56

    def pe_tp(si, j):
        return PEA * si + j + 1

    def pe_q(si, oc):
        return PEA * si + 32 + 2 * oc + 1

    def pe_k(si, oc):
        return PEA * si + 32 + 2 * oc + 2

    def pe_v(si, j):
        return PEA * si + 48 + j + 1

    PEA_TOT = PEA * NT512

    def aidx_q(si, oc):
        return si * 24 + 2 * oc

    def aidx_k(si, oc):
        return si * 24 + 2 * oc + 1

    def aidx_v(si, j):
        return si * 24 + 16 + j

    def a_consumer_ve(aidx):
        si, r = aidx // 24, aidx % 24
        if r < 16:
            return ve_qr(si, r // 2) if r % 2 == 0 else ve_kr(si, r // 2)
        return ve_vv(si, r - 16)

    def NKB(qt):
        return 4 * (qt + 1)

    def NG(qt):
        return 2 * (qt + 1)

    def diag_idx(qt, kb):
        off = kb * 128 - qt * 512
        if off < 0:
            return None
        return off // 128

    peb, scb, veb = {}, {}, {}
    pe_c, sc_c, ve_c = [PEA_TOT], [SCA_TOT], [VEA_TOT]
    kb_list = []
    for qt in range(NQT):
        for hp in range(ND):
            for kb in range(NKB(qt)):
                kb_list.append((qt, hp, kb))
                pe_c[0] += 1
                peb[("s", qt, hp, kb)] = pe_c[0]
                sc_c[0] += 1
                scb[("e", qt, hp, kb)] = sc_c[0]
                nm = 2 if diag_idx(qt, kb) is not None else 0
                ve_c[0] += nm
                veb[("m", qt, hp, kb)] = ve_c[0]
            sc_c[0] += 1
            scb[("rcA", qt, hp)] = sc_c[0]
            sc_c[0] += 1
            scb[("rcB", qt, hp)] = sc_c[0]
            pe_c[0] += 1
            peb[("yA", qt, hp)] = pe_c[0]
            pe_c[0] += 1
            peb[("yB", qt, hp)] = pe_c[0]
            ve_c[0] += 1
            veb[("dA", qt, hp)] = ve_c[0]
            ve_c[0] += 1
            veb[("dB", qt, hp)] = ve_c[0]
            pe_c[0] += 1
            peb[("bcA", qt, hp)] = pe_c[0]
            ve_c[0] += 1
            veb[("bcbA", qt, hp)] = ve_c[0]
            pe_c[0] += 1
            peb[("bcB", qt, hp)] = pe_c[0]
            ve_c[0] += 1
            veb[("bcbB", qt, hp)] = ve_c[0]
            ve_c[0] += 1
            veb[("yA", qt, hp)] = ve_c[0]
            ve_c[0] += 1
            veb[("yB", qt, hp)] = ve_c[0]
        for j in range(8):
            pe_c[0] += 1
            peb[("o", qt, j)] = pe_c[0]
            ve_c[0] += 1
            veb[("bo", qt, j)] = ve_c[0]
            ve_c[0] += 1
            veb[("res", qt, j)] = ve_c[0]

    def kb_done_pe(qt, hp, kb):
        if kb + 2 < NKB(qt):
            return peb[("s", qt, hp, kb + 2)]
        return peb[("yB", qt, hp)]

    kcons = []
    for gk, (qt, hp, kb) in enumerate(kb_list):
        kcons.append(kb_done_pe(qt, hp, kb))
    kq_list = []
    kqcons = []
    for qt in range(NQT):
        for hp in range(ND):
            for kq in range(NKB(qt) // 4):
                kq_list.append((qt, hp, kq))
                last_kb = kq * 4 + 3
                kqcons.append(kb_done_pe(qt, hp, last_kb))

    NWL = 16 * (4 * ND + 8)

    with p.block() as blk:

        def gpsimd_f(q):
            for c in range(ND):
                q.dma_start(out=wqT[c][:], in_=wq_d[c * 128:(c + 1) * 128, :]).then_inc(s["wl"], 16)
            for c in range(ND):
                q.dma_start(out=wkT[c][:], in_=wk_d[c * 128:(c + 1) * 128, :]).then_inc(s["wl"], 16)
            for c in range(ND):
                q.dma_start(out=wvT[c][:], in_=wv_d[c * 128:(c + 1) * 128, :]).then_inc(s["wl"], 16)
            for c in range(ND):
                q.dma_start(out=woT[c][:], in_=wo_d[c * 128:(c + 1) * 128, :]).then_inc(s["wl"], 16)
            q.dma_start(out=bcol[:], in_=bcol_d).then_inc(s["wl"], 16)
            q.dma_start(out=bvb[:], in_=bvb_d).then_inc(s["wl"], 16)
            q.dma_start(out=bob[:], in_=bob_d).then_inc(s["wl"], 16)
            q.dma_start(out=cos_t[:], in_=cos_d).then_inc(s["wl"], 16)
            q.dma_start(out=sin_t[:], in_=sin_d).then_inc(s["wl"], 16)
            q.dma_start(out=msk[:], in_=msk_d).then_inc(s["wl"], 16)
            q.dma_start(out=negC[:], in_=negC_d).then_inc(s["wl"], 16)
            q.dma_start(out=onesr[:], in_=ones_d).then_inc(s["wl"], 16)
            q.wait_ge(s["sk0"], 16 * 32)
            q.wait_ge(s["sk1"], 16 * 32)
            q.wait_ge(s["sv0"], 16 * 32)
            q.wait_ge(s["sv1"], 16 * 32)
            gk = 0
            gkk = 0
            for qt in range(NQT):
                for hp in range(ND):
                    for kq in range(NKB(qt) // 4):
                        if gkk >= 2:
                            q.wait_ge(s["pe"], kqcons[gkk - 2])
                        q.dma_start(out=ktl[gkk % 2][:],
                                    in_=k_scr[hp * 128:(hp + 1) * 128,
                                              kq * 512:(kq + 1) * 512],
                                    ).then_inc(s[f"lk{gkk % 2}"], 16)
                        gkk += 1
                        for kb in range(kq * 4, kq * 4 + 4):
                            b4 = gk % 4
                            if gk >= 4:
                                q.wait_ge(s["pe"], kcons[gk - 4])
                            q.dma_start(out=vtl[b4][:],
                                        in_=v_scr[kb * 128:(kb + 1) * 128,
                                                  hp * 130:(hp + 1) * 130],
                                        ).then_inc(s[f"lv{b4}"], 16)
                            gk += 1

        def sync_f(q):
            q.dma_start(out=ident[:], in_=ident_d).then_inc(s["li"], 16)
            for si in range(NT512):
                for i in range(4):
                    g = si * 4 + i
                    if g >= 2:
                        q.wait_ge(s["ve"], ve_ln((g - 2) // 4, (g - 2) % 4))
                    q.dma_start(out=xs[g % 2][:],
                                in_=xin[g * 128:(g + 1) * 128, :]).then_inc(s[f"ldp{g % 2}"], 16)
                for oc in range(ND):
                    gq = si * ND + oc
                    q.wait_ge(s["ve"], ve_qr(si, oc))
                    q.dma_start(out=q_scr[oc * 128:(oc + 1) * 128, si * 512:(si + 1) * 512],
                                in_=qro[gq % 2][:]).then_inc(s[f"sq{gq % 2}"], 16)
                    q.wait_ge(s["ve"], ve_kr(si, oc))
                    q.dma_start(out=k_scr[oc * 128:(oc + 1) * 128, si * 512:(si + 1) * 512],
                                in_=qro[2 + gq % 2][:]).then_inc(s[f"sk{gq % 2}"], 16)
                for j in range(8):
                    gv = si * 8 + j
                    sub, ot = j // 2, j % 2
                    q.wait_ge(s["ve"], ve_vv(si, j))
                    q.dma_start(
                        out=v_scr[(si * 4 + sub) * 128:(si * 4 + sub + 1) * 128,
                                  ot * 520:(ot + 1) * 520],
                        in_=vst[gv % 2][:]).then_inc(s[f"sv{gv % 2}"], 16)
            for nm2 in ["sq0", "sq1", "sk0", "sk1", "sv0", "sv1"]:
                q.wait_ge(s[nm2], 16 * 32)
            for qt in range(NQT):
                for c in range(ND):
                    if qt >= 2:
                        q.wait_ge(s["ve"], veb[("res", qt - 2, 7)])
                    q.dma_start(out=qtl[qt % 2][c][:],
                                in_=q_scr[c * 128:(c + 1) * 128,
                                          qt * 512:(qt + 1) * 512]).then_inc(s["lq"], 16)
                for j in range(8):
                    gj = qt * 8 + j
                    qs, dn = j // 2, j % 2
                    if gj >= 2:
                        pj = gj - 2
                        q.wait_ge(s["ve"], veb[("res", pj // 8, pj % 8)])
                    q.dma_start(out=xr[gj % 2][:],
                                in_=xin[(qt * 4 + qs) * 128:(qt * 4 + qs + 1) * 128,
                                        dn * 512:(dn + 1) * 512],
                                ).then_inc(s[f"lrp{gj % 2}"], 16)
                    if gj >= 2:
                        pj = gj - 2
                        q.wait_ge(s["ve"], veb[("res", pj // 8, pj % 8)])
                        q.dma_start(
                            out=xout[(pj // 2) * 128:(pj // 2 + 1) * 128,
                                     (pj % 2) * 512:(pj % 2 + 1) * 512],
                            in_=xo[pj % 2][:]).then_inc(s[f"stp{pj % 2}"], 16)
            for gj in range(NQT * 8 - 2, NQT * 8):
                q.wait_ge(s["ve"], veb[("res", gj // 8, gj % 8)])
                q.dma_start(
                    out=xout[(gj // 2) * 128:(gj // 2 + 1) * 128,
                             (gj % 2) * 512:(gj % 2 + 1) * 512],
                    in_=xo[gj % 2][:]).then_inc(s[f"stp{gj % 2}"], 16)
            q.wait_ge(s["stp0"], 16 * NT128)
            q.wait_ge(s["stp1"], 16 * NT128)

        def vector_f(q):
            vsn = [0]

            def vinc(ins):
                ins.then_inc(s["vs"], 1)
                vsn[0] += 1
                return vsn[0]

            def vwait(n):
                if n > 0:
                    q.wait_ge(s["vs"], n)

            def emit_rope(psrc, bcol_col, si, dst):
                a = vinc(q.tensor_scalar(qb[:], psrc, bcol.ap()[:, bcol_col:bcol_col + 1],
                                         None, ALU.add))
                vwait(a)
                cs = cos_t.ap()[:, si * 512:(si + 1) * 512]
                sn = sin_t.ap()[:, si * 512:(si + 1) * 512]
                b = vinc(q.tensor_tensor(mm1[:], qb[:], cs, ALU.mult))
                last = b
                for h2 in range(2):
                    for hh in range(2):
                        p0 = h2 * 64 + hh * 32
                        psw = h2 * 64 + (1 - hh) * 32
                        last = vinc(q.tensor_tensor(mm2.ap()[p0:p0 + 32, :],
                                                    qb.ap()[psw:psw + 32, :],
                                                    sn[psw:psw + 32, :], ALU.mult))
                vwait(last)
                dd = vinc(q.tensor_tensor(dst, mm1[:], mm2[:], ALU.add))
                vwait(dd)

            for si in range(NT512):
                for i in range(4):
                    g = si * 4 + i
                    q.wait_ge(s[f"ldp{g % 2}"], 16 * (g // 2 + 1))
                    a = vinc(q.reduce_sum(stt["s1"][i][:], xs[g % 2][:], AX.X))
                    vwait(a)
                    b = vinc(q.tensor_scalar(stt["mu"][i][:], stt["s1"][i][:],
                                             1.0 / D, None, ALU.mult))
                    q.wait_ge(s["sc"], sc_sq(si, i))
                    cv = vinc(q.tensor_scalar(stt["v1"][i][:], stt["s2"][i][:],
                                              1.0 / D, 1e-5, ALU.mult, ALU.add))
                    vwait(b)
                    dv = vinc(q.scalar_tensor_tensor(stt["m2"][i][:], stt["mu"][i][:],
                                                     -1.0, stt["mu"][i][:],
                                                     ALU.mult, ALU.mult))
                    vwait(dv)
                    e = vinc(q.tensor_tensor(stt["var"][i][:], stt["v1"][i][:],
                                             stt["m2"][i][:], ALU.add))
                    vwait(e)
                    q.sem_inc(s["vv"], 1)
                    q.wait_ge(s["sc"], sc_rs(si, i))
                    f = vinc(q.scalar_tensor_tensor(stt["nm"][i][:], stt["mu"][i][:],
                                                    -1.0, stt["rs"][i][:],
                                                    ALU.mult, ALU.mult))
                    if g > 0:
                        pg = g - 1
                        q.wait_ge(s["pe"], pe_tp(pg // 4, (pg % 4) * 8 + 7))
                    vwait(f)
                    q.tensor_scalar(
                        xn[:], xs[g % 2][:], stt["rs"][i].ap()[:, 0:1],
                        stt["nm"][i].ap()[:, 0:1], ALU.mult, ALU.add,
                    ).then_inc(s["ve"], 1)
                    for c in range(ND):
                        jj = i * 8 + c
                        q.wait_ge(s["pe"], pe_tp(si, jj))
                        tot = si * 32 + jj
                        q.tensor_copy(xnT[c].ap()[:, i * 128:(i + 1) * 128],
                                      tp_view(tot)).then_inc(s["ve"], 1)
                for oc in range(ND):
                    gq = si * ND + oc
                    q.wait_ge(s["pe"], pe_q(si, oc))
                    if gq >= 2:
                        q.wait_ge(s[f"sq{gq % 2}"], 16 * ((gq - 2) // 2 + 1))
                    emit_rope(qk_view(aidx_q(si, oc)), oc, si, qro[gq % 2][:])
                    q.sem_inc(s["ve"], 1)
                    q.wait_ge(s["pe"], pe_k(si, oc))
                    if gq >= 2:
                        q.wait_ge(s[f"sk{gq % 2}"], 16 * ((gq - 2) // 2 + 1))
                    emit_rope(qk_view(aidx_k(si, oc)), 8 + oc, si, qro[2 + gq % 2][:])
                    q.sem_inc(s["ve"], 1)
                for j in range(8):
                    gv = si * 8 + j
                    sub, ot = j // 2, j % 2
                    q.wait_ge(s["pe"], pe_v(si, j))
                    if gv >= 2:
                        q.wait_ge(s[f"sv{gv % 2}"], 16 * ((gv - 2) // 2 + 1))
                    hv = vinc(q.tensor_tensor(
                        vst[gv % 2].ap().rearrange("p (h w) -> p h w", w=65)[:, :, 0:64],
                        qk_view(aidx_v(si, j)).rearrange("p (h w) -> p h w", w=64),
                        bvb.ap()[:, ot * 512:(ot + 1) * 512].rearrange("p (h w) -> p h w", w=64),
                        ALU.add))
                    vwait(hv)
                    q.memset(vst[gv % 2].ap().rearrange("p (h w) -> p h w", w=65)[:, :, 64:65],
                             1.0).then_inc(s["ve"], 1)
            for qt in range(NQT):
                for hp in range(ND):
                    for kb in range(NKB(qt)):
                        mi = diag_idx(qt, kb)
                        if mi is None:
                            continue
                        q.wait_ge(s["sc"], scb[("e", qt, hp, kb)])
                        for h in range(2):
                            q.tensor_tensor(
                                pm[(kb % 2) * 2 + h][:],
                                pt.ap()[:, ((kb % 2) * 2 + h) * 512:((kb % 2) * 2 + h + 1) * 512],
                                msk.ap()[:, mi * 512:(mi + 1) * 512],
                                ALU.mult).then_inc(s["ve"], 1)
                    q.wait_ge(s["pe"], peb[("yB", qt, hp)])
                    dv = vinc(q.tensor_copy(dden[0][:], y_view[0][64:65, :]))
                    vwait(dv)
                    q.sem_inc(s["ve"], 1)
                    dv = vinc(q.tensor_copy(dden[1][:], y_view[1][64:65, :]))
                    vwait(dv)
                    q.sem_inc(s["ve"], 1)
                    q.wait_ge(s["sc"], scb[("rcA", qt, hp)])
                    q.sem_inc(s["ve"], 1)
                    q.wait_ge(s["sc"], scb[("rcB", qt, hp)])
                    q.sem_inc(s["ve"], 1)
                    if qt >= 1:
                        q.wait_ge(s["pe"], peb[("o", qt - 1, 7)])
                    yv = vinc(q.tensor_tensor(yall[hp].ap()[0:64, :], y_view[0][0:64, :],
                                              bcb.ap()[:, 0:512], ALU.mult))
                    vwait(yv)
                    q.sem_inc(s["ve"], 1)
                    yv = vinc(q.tensor_tensor(yall[hp].ap()[64:128, :], y_view[1][0:64, :],
                                              bcb.ap()[:, 512:1024], ALU.mult))
                    vwait(yv)
                    q.sem_inc(s["ve"], 1)
                for j in range(8):
                    gj = qt * 8 + j
                    qs, dn = j // 2, j % 2
                    q.wait_ge(s["pe"], peb[("o", qt, j)])
                    if gj >= 2:
                        q.wait_ge(s[f"stp{gj % 2}"], 16 * ((gj - 2) // 2 + 1))
                    ov = vinc(q.tensor_tensor(xo[gj % 2][:], o_view,
                                              bob.ap()[:, dn * 512:(dn + 1) * 512],
                                              ALU.add))
                    vwait(ov)
                    q.sem_inc(s["ve"], 1)
                    q.wait_ge(s[f"lrp{gj % 2}"], 16 * (gj // 2 + 1))
                    o2 = vinc(q.tensor_tensor(xo[gj % 2][:], xo[gj % 2][:],
                                              xr[gj % 2][:], ALU.add))
                    vwait(o2)
                    q.sem_inc(s["ve"], 1)

        def scalar_f(q):
            ssn = [0]

            def sinc(ins):
                ins.then_inc(s["ss"], 1)
                ssn[0] += 1
                return ssn[0]

            for si in range(NT512):
                for i in range(4):
                    g = si * 4 + i
                    q.wait_ge(s[f"ldp{g % 2}"], 16 * (g // 2 + 1))
                    prior = SCA * si + 2 * i
                    if prior > 0:
                        q.wait_ge(s["sc"], prior)
                    q.activation(scr[:], xs[g % 2][:], AF.Square,
                                 accum_out=stt["s2"][i][:]).then_inc(s["sc"], 1)
                    q.wait_ge(s["vv"], g + 1)
                    n0 = sinc(q.activation(stt["ln"][i][:], stt["var"][i][:], AF.Ln))
                    q.wait_ge(s["ss"], n0)
                    q.activation(stt["rs"][i][:], stt["ln"][i][:], AF.Exp,
                                 scale=-0.5).then_inc(s["sc"], 1)
            for qt in range(NQT):
                for hp in range(ND):
                    for kb in range(NKB(qt)):
                        q.wait_ge(s["pe"], peb[("s", qt, hp, kb)])
                        h0 = (kb % 2) * 1024
                        q.activation(pt.ap()[:, h0:h0 + 1024],
                                     s_view[:, h0:h0 + 1024], AF.Exp, scale=0.125,
                                     bias=negC.ap()[:, 0:1]).then_inc(s["sc"], 1)
                    q.wait_ge(s["pe"], peb[("bcA", qt, hp)])
                    n1 = sinc(q.activation(lnscr.ap()[0:64, 0:512], bc_view, AF.Ln))
                    q.wait_ge(s["ss"], n1)
                    q.activation(bcb.ap()[:, 0:512], lnscr.ap()[0:64, 0:512],
                                 AF.Exp, scale=-1.0).then_inc(s["sc"], 1)
                    q.wait_ge(s["pe"], peb[("bcB", qt, hp)])
                    n1 = sinc(q.activation(lnscr.ap()[0:64, 0:512], bc_view, AF.Ln))
                    q.wait_ge(s["ss"], n1)
                    q.activation(bcb.ap()[:, 512:1024], lnscr.ap()[0:64, 0:512],
                                 AF.Exp, scale=-1.0).then_inc(s["sc"], 1)

        def tensor_f(q):
            q.wait_ge(s["li"], 16)
            q.wait_ge(s["wl"], NWL)
            for si in range(NT512):
                for i in range(4):
                    q.wait_ge(s["ve"], ve_ln(si, i))
                    for c in range(ND):
                        jj = i * 8 + c
                        tot = si * 32 + jj
                        if tot >= 2:
                            pj = tot - 2
                            q.wait_ge(s["ve"], ve_ev(pj // 32, (pj % 32) // 8, pj % 8))
                        q.transpose(tp_view(tot),
                                    xn.ap()[:, c * 128:(c + 1) * 128],
                                    ident[:]).then_inc(s["pe"], 1)
                q.wait_ge(s["ve"], VEA * si + 36)
                for oc in range(ND):
                    ai = aidx_q(si, oc)
                    if ai >= 2:
                        q.wait_ge(s["ve"], a_consumer_ve(ai - 2))
                    for c in range(ND):
                        ins = q.matmul(qk_view(ai),
                                       wqT[c].ap()[:, oc * 128:(oc + 1) * 128],
                                       xnT[c][:], start=(c == 0), stop=(c == ND - 1))
                        if c == ND - 1:
                            ins.then_inc(s["pe"], 1)
                    ai = aidx_k(si, oc)
                    if ai >= 2:
                        q.wait_ge(s["ve"], a_consumer_ve(ai - 2))
                    for c in range(ND):
                        ins = q.matmul(qk_view(ai),
                                       wkT[c].ap()[:, oc * 128:(oc + 1) * 128],
                                       xnT[c][:], start=(c == 0), stop=(c == ND - 1))
                        if c == ND - 1:
                            ins.then_inc(s["pe"], 1)
                for j in range(8):
                    sub, ot = j // 2, j % 2
                    ai = aidx_v(si, j)
                    q.wait_ge(s["ve"], a_consumer_ve(ai - 2))
                    for c in range(ND):
                        ins = q.matmul(qk_view(ai),
                                       xnT[c].ap()[:, sub * 128:(sub + 1) * 128],
                                       wvT[c].ap()[:, ot * 512:(ot + 1) * 512],
                                       start=(c == 0), stop=(c == ND - 1))
                        if c == ND - 1:
                            ins.then_inc(s["pe"], 1)
            q.wait_ge(s["ve"], VEA_TOT)
            gk = 0
            gkk = 0
            for qt in range(NQT):
                q.wait_ge(s["lq"], 16 * 8 * (qt + 1))
                for hp in range(ND):
                    base_gk = gk
                    base_gkk = gkk

                    def emit_scores(kb):
                        kqi = base_gkk + kb // 4
                        q.wait_ge(s[f"lk{kqi % 2}"], 16 * (kqi // 2 + 1))
                        if kb >= 2:
                            q.wait_ge(s["sc"], scb[("e", qt, hp, kb - 2)])
                        elif (qt, hp) != (0, 0):
                            php = (qt, hp - 1) if hp > 0 else (qt - 1, ND - 1)
                            q.wait_ge(s["sc"], scb[("e", php[0], php[1],
                                                    NKB(php[0]) - 2 + kb)])
                        h0 = (kb % 2) * 1024
                        kcol = (kb % 4) * 128
                        for h in range(2):
                            ins = q.matmul(
                                s_view[:, h0 + h * 512:h0 + (h + 1) * 512],
                                ktl[kqi % 2].ap()[h * 64:(h + 1) * 64,
                                                  kcol:kcol + 128],
                                qtl[qt % 2][hp].ap()[h * 64:(h + 1) * 64, :],
                                start=True, stop=True)
                            if h == 1:
                                ins.then_inc(s["pe"], 1)

                    emit_scores(0)
                    if NKB(qt) > 1:
                        emit_scores(1)
                    for kb in range(NKB(qt)):
                        b4v = (base_gk + kb) % 4
                        q.wait_ge(s[f"lv{b4v}"], 16 * ((base_gk + kb) // 4 + 1))
                        mi = diag_idx(qt, kb)
                        if mi is not None:
                            q.wait_ge(s["ve"], veb[("m", qt, hp, kb)])
                        else:
                            q.wait_ge(s["sc"], scb[("e", qt, hp, kb)])
                        if kb == 0 and (qt, hp) != (0, 0):
                            php = (qt, hp - 1) if hp > 0 else (qt - 1, ND - 1)
                            q.wait_ge(s["ve"], veb[("yB", *php)])
                        h0 = (kb % 2) * 1024
                        for h in range(2):
                            if mi is not None:
                                rhs = pm[(kb % 2) * 2 + h][:]
                            else:
                                rhs = pt.ap()[:, h0 + h * 512:h0 + (h + 1) * 512]
                            ins = q.matmul(
                                y_view[h],
                                vtl[b4v].ap()[:, h * 65:(h + 1) * 65],
                                rhs,
                                start=(kb == 0), stop=(kb == NKB(qt) - 1))
                            if kb == NKB(qt) - 1:
                                ins.then_inc(s["pe"], 1)
                        if kb + 2 < NKB(qt):
                            emit_scores(kb + 2)
                    gk += NKB(qt)
                    gkk += NKB(qt) // 4
                    q.wait_ge(s["ve"], veb[("dA", qt, hp)])
                    q.matmul(bc_view, onesr[:], dden[0][:],
                             start=True, stop=True).then_inc(s["pe"], 1)
                    q.wait_ge(s["ve"], veb[("bcbA", qt, hp)])
                    q.wait_ge(s["ve"], veb[("dB", qt, hp)])
                    q.matmul(bc_view, onesr[:], dden[1][:],
                             start=True, stop=True).then_inc(s["pe"], 1)
                q.wait_ge(s["ve"], veb[("yB", qt, ND - 1)])
                for j in range(8):
                    qs, dn = j // 2, j % 2
                    gj = qt * 8 + j
                    if gj >= 1:
                        pj = gj - 1
                        q.wait_ge(s["ve"], veb[("bo", pj // 8, pj % 8)])
                    for hp in range(ND):
                        ins = q.matmul(o_view,
                                       yall[hp].ap()[:, qs * 128:(qs + 1) * 128],
                                       woT[hp].ap()[:, dn * 512:(dn + 1) * 512],
                                       start=(hp == 0), stop=(hp == ND - 1))
                        if hp == ND - 1:
                            ins.then_inc(s["pe"], 1)

        blk.gpsimd(gpsimd_f)
        blk.sync(sync_f)
        blk.vector(vector_f)
        blk.scalar(scalar_f)
        blk.tensor(tensor_f)

    return nc


def attn_host_consts():
    hd = HD
    half = hd // 2
    theta = 1.0 / (10000.0 ** (np.arange(0, hd, 2, dtype=np.float64) / hd))
    pos = np.arange(1, T + 1, dtype=np.float64)
    ang = pos[:, None] * theta[None, :]
    ang2 = np.concatenate([ang, ang], axis=1)
    cos = np.cos(ang2).T.astype(BF)
    sin = np.sin(ang2).T
    sgn = np.ones((hd, 1)); sgn[:half] = -1.0
    sins = (sin * sgn)
    sins = np.concatenate([sins[half:], sins[:half]], axis=0).astype(BF)
    cos128 = np.concatenate([cos, cos], axis=0)
    sin128 = np.concatenate([sins, sins], axis=0)
    masks = np.zeros((128, 4 * 512), dtype=BF)
    for mi in range(4):
        dk = np.arange(128)[:, None]
        dq = np.arange(512)[None, :]
        masks[:, mi * 512:(mi + 1) * 512] = (dq >= dk + mi * 128).astype(BF)
    negc = np.full((128, 1), -12.0, dtype=np.float32)
    ones = np.ones((1, 64), dtype=np.float32)
    return dict(cos_t=cos128, sin_t=sin128, msk=masks, negC=negc, ones_r=ones)


NEC = E // 128


def mamba_scratch(nc):
    u_scr = nc.dram_tensor("u_scr", [E, T], BF16, kind="Internal").ap()
    z_scr = nc.dram_tensor("z_scr", [E, T], BF16, kind="Internal").ap()
    d_scr = nc.dram_tensor("d_scr", [E, T], BF16, kind="Internal").ap()
    b_scr = nc.dram_tensor("b_scr", [NST, T], BF16, kind="Internal").ap()
    y_scr = nc.dram_tensor("y_scr", [E, T], BF16, kind="Internal").ap()
    return u_scr, z_scr, d_scr, b_scr, y_scr


def build_mamba_a(p: Prog, io=None, fwprep=None):
    nc = p.nc
    if io is None:
        xin = nc.dram_tensor("xin", [T, D], F32, kind="ExternalInput").ap()
        ip_d = nc.dram_tensor("ipT", [D, 2 * E], F32, kind="ExternalInput").ap()
        cw_d = nc.dram_tensor("cwcol", [128, NEC * 4], F32, kind="ExternalInput").ap()
        xpb_d = nc.dram_tensor("xpTB", [E, NST], F32, kind="ExternalInput").ap()
        xpd_d = nc.dram_tensor("xpTd", [E, 1], F32, kind="ExternalInput").ap()
        dtw_d = nc.dram_tensor("dtw_row", [1, E], F32, kind="ExternalInput").ap()
        dtb_d = nc.dram_tensor("dtb_col", [128, NEC], F32, kind="ExternalInput").ap()
        one_d = nc.dram_tensor("one_col", [128, 1], F32, kind="ExternalInput").ap()
        ident_d = nc.dram_tensor("ident", [128, 128], BF16, kind="ExternalInput").ap()
        u_out = nc.dram_tensor("u_scr", [E, T], BF16, kind="ExternalOutput").ap()
        z_out = nc.dram_tensor("z_scr", [E, T], BF16, kind="ExternalOutput").ap()
        dl_out = nc.dram_tensor("d_scr", [E, T], BF16, kind="ExternalOutput").ap()
        b_out = nc.dram_tensor("b_scr", [NST, T], BF16, kind="ExternalOutput").ap()
    else:
        (xin, ip_d, cw_d, xpb_d, xpd_d, dtw_d, dtb_d, one_d, ident_d,
         u_out, z_out, dl_out, b_out) = io

    ND = D // 128
    ipT = [p.sb(f"ipT{c}", [128, 2 * E], BF16) for c in range(ND)]
    xpB = [p.sb(f"xpB{k}", [128, NST], BF16) for k in range(NEC)]
    xpd = [p.sb(f"xpd{k}", [128, 1], BF16) for k in range(NEC)]
    dtw = p.sb("dtw", [1, E], F32)
    dtb = p.sb("dtb", [128, NEC], F32)
    onec = p.sb("onec", [128, 1], F32)
    cw = p.sb("cw", [128, NEC * 4], F32)
    diag = [p.sb(f"diag{k}", [128, 4 * 128], BF16) for k in range(NEC)]
    xs = [p.sb(f"xs{b}", [128, D], F32) for b in range(2)]
    xn = p.sb("xn", [128, D], BF16)
    xnT = [p.sb(f"xnT{c}", [128, 512], BF16) for c in range(ND)]
    u_nat = [p.sb(f"u_nat{k}", [128, 515], BF16) for k in range(NEC)]
    u_act = [p.sb(f"u_act{k}", [128, 512], BF16) for k in range(NEC)]
    zb = [p.sb(f"zb{b}", [128, 512], BF16) for b in range(2)]
    db = [p.sb(f"db{b}", [128, 512], BF16) for b in range(2)]
    B17 = p.sb("B17", [NST, 512], BF16)
    dt_sb = p.sb("dt_sb", [1, 512], F32)
    r1s = p.sb("r1s", [128, 512], F32)
    l1s = p.sb("l1s", [128, 512], F32)
    a1s = p.sb("a1s", [128, 512], F32)
    e1s = p.sb("e1s", [128, 512], F32)
    stt = {nm: [p.sb(f"st_{nm}{i}", [128, 1], F32) for i in range(4)]
           for nm in ["s1", "s2", "mu", "v1", "m2", "var", "ln", "rs", "nm"]}
    scr = p.sb("scr", [128, D], BF16)
    ident = p.sb("identt", [128, 128], BF16)

    psP = p.ps("psP", [128, 2048], F32)
    psX = p.ps("psX", [128, 2048], F32)
    pool_v = [psP.ap()[:, b * 512:(b + 1) * 512] for b in range(4)]
    xpB_v = psX.ap()[0:NST, 0:512]
    xpd_v = psX.ap()[0:1, 512:1024]
    psX_bf = psX.ap().bitcast(BF16)
    tp_v = [psX_bf[:, 2048:2176], psX_bf[:, 3072:3200]]

    s = {nm: p.sem(nm) for nm in [
        "wl", "li", "ldp0", "ldp1", "ve", "vv", "sc", "pe",
        "su0", "su1", "sz0", "sz1", "sd0", "sd1", "sb0", "vs", "ss"]}

    fw = FWPrep(p, fwprep) if fwprep is not None else None

    peb, veb, scb = {}, {}, {}
    pe_c, ve_c, sc_c = [0], [0], [0]
    pool_seq = []
    for si in range(NT512):
        for i in range(4):
            for c in range(8):
                pe_c[0] += 1
                peb[("tp", si, i * 8 + c)] = pe_c[0]
        for ec in range(NEC):
            pe_c[0] += 1
            peb[("u", si, ec)] = pe_c[0]
            pool_seq.append(("u", si, ec))
            pe_c[0] += 1
            peb[("cv", si, ec)] = pe_c[0]
            pool_seq.append(("cv", si, ec))
        for ec in range(NEC):
            pe_c[0] += 1
            peb[("z", si, ec)] = pe_c[0]
            pool_seq.append(("z", si, ec))
        pe_c[0] += 1
        peb[("xpB", si)] = pe_c[0]
        pe_c[0] += 1
        peb[("xpd", si)] = pe_c[0]
        for ec in range(NEC):
            pe_c[0] += 1
            peb[("dl", si, ec)] = pe_c[0]
            pool_seq.append(("dl", si, ec))
    for si in range(NT512):
        for i in range(4):
            ve_c[0] += 1
            veb[("ln", si, i)] = ve_c[0]
            for c in range(8):
                ve_c[0] += 1
                veb[("ev", si, i * 8 + c)] = ve_c[0]
        for ec in range(NEC):
            ve_c[0] += 1
            veb[("halo", si, ec)] = ve_c[0]
            ve_c[0] += 1
            veb[("ue", si, ec)] = ve_c[0]
        for ec in range(NEC):
            ve_c[0] += 1
            veb[("ze", si, ec)] = ve_c[0]
        ve_c[0] += 1
        veb[("B17", si)] = ve_c[0]
        ve_c[0] += 1
        veb[("dt", si)] = ve_c[0]
        for ec in range(NEC):
            ve_c[0] += 1
            veb[("dle", si, ec)] = ve_c[0]
    for si in range(NT512):
        for i in range(4):
            sc_c[0] += 1
            scb[("sq", si, i)] = sc_c[0]
            sc_c[0] += 1
            scb[("rs", si, i)] = sc_c[0]
        for ec in range(NEC):
            sc_c[0] += 1
            scb[("silu", si, ec)] = sc_c[0]
        for ec in range(NEC):
            sc_c[0] += 1
            scb[("r1", si, ec)] = sc_c[0]

    def pool_consumer(kind, si, ec):
        if kind == "u":
            return ("ve", veb[("ue", si, ec)])
        if kind == "cv":
            return ("sc", scb[("silu", si, ec)])
        if kind == "z":
            return ("ve", veb[("ze", si, ec)])
        return ("sc", scb[("r1", si, ec)])

    pool_of = {}
    for idx, key in enumerate(pool_seq):
        pool_of[key] = idx

    def pool_wait(q, key):
        idx = pool_of[key]
        if idx >= 4:
            k2 = pool_seq[idx - 4]
            sem_nm, val = pool_consumer(*k2)
            q.wait_ge(s[sem_nm], val)
        return pool_v[idx % 4]

    with p.block() as blk:

        def gpsimd_f(q):
            for c in range(ND):
                q.dma_start(out=ipT[c][:], in_=ip_d[c * 128:(c + 1) * 128, :]).then_inc(s["wl"], 16)
            for k in range(NEC):
                q.dma_start(out=xpB[k][:], in_=xpb_d[k * 128:(k + 1) * 128, :]).then_inc(s["wl"], 16)
                q.dma_start(out=xpd[k][:], in_=xpd_d[k * 128:(k + 1) * 128, :]).then_inc(s["wl"], 16)
            q.dma_start(out=dtw[:], in_=dtw_d).then_inc(s["wl"], 16)
            q.dma_start(out=dtb[:], in_=dtb_d).then_inc(s["wl"], 16)
            q.dma_start(out=onec[:], in_=one_d).then_inc(s["wl"], 16)
            q.dma_start(out=cw[:], in_=cw_d).then_inc(s["wl"], 16)
            if fw is not None:
                fw.run_gp(q)
        NWL = 16 * (ND + 2 * NEC + 4)

        def sync_f(q):
            q.dma_start(out=ident[:], in_=ident_d).then_inc(s["li"], 16)
            for si in range(NT512):
                for i in range(4):
                    g = si * 4 + i
                    if g >= 2:
                        q.wait_ge(s["ve"], veb[("ln", (g - 2) // 4, (g - 2) % 4)])
                    q.dma_start(out=xs[g % 2][:],
                                in_=xin[g * 128:(g + 1) * 128, :]).then_inc(s[f"ldp{g % 2}"], 16)
                for ec in range(NEC):
                    gu = si * NEC + ec
                    q.wait_ge(s["sc"], scb[("silu", si, ec)])
                    q.dma_start(out=u_out[ec * 128:(ec + 1) * 128, si * 512:(si + 1) * 512],
                                in_=u_act[ec][:]).then_inc(s[f"su{gu % 2}"], 16)
                for ec in range(NEC):
                    gz = si * NEC + ec
                    q.wait_ge(s["ve"], veb[("ze", si, ec)])
                    q.dma_start(out=z_out[ec * 128:(ec + 1) * 128, si * 512:(si + 1) * 512],
                                in_=zb[gz % 2][:]).then_inc(s[f"sz{gz % 2}"], 16)
                q.wait_ge(s["ve"], veb[("B17", si)])
                q.dma_start(out=b_out[:, si * 512:(si + 1) * 512],
                            in_=B17[:]).then_inc(s["sb0"], 16)
                for ec in range(NEC):
                    gd = si * NEC + ec
                    q.wait_ge(s["ve"], veb[("dle", si, ec)])
                    q.dma_start(out=dl_out[ec * 128:(ec + 1) * 128, si * 512:(si + 1) * 512],
                                in_=db[gd % 2][:]).then_inc(s[f"sd{gd % 2}"], 16)
            q.wait_ge(s["su0"], 16 * 64)
            q.wait_ge(s["su1"], 16 * 64)
            q.wait_ge(s["sz0"], 16 * 64)
            q.wait_ge(s["sz1"], 16 * 64)
            q.wait_ge(s["sd0"], 16 * 64)
            q.wait_ge(s["sd1"], 16 * 64)
            q.wait_ge(s["sb0"], 16 * 8)

        def vector_f(q):
            vsn = [0]

            def vinc(ins):
                ins.then_inc(s["vs"], 1)
                vsn[0] += 1
                return vsn[0]

            def vwait(n):
                if n > 0:
                    q.wait_ge(s["vs"], n)

            q.wait_ge(s["wl"], NWL)
            q.wait_ge(s["li"], 16)
            dlast = 0
            for k in range(NEC):
                for j in range(4):
                    dlast = vinc(q.tensor_scalar(
                        diag[k].ap()[:, j * 128:(j + 1) * 128], ident[:],
                        cw.ap()[:, k * 4 + j:k * 4 + j + 1], None, ALU.mult))
            vwait(dlast)
            q.sem_inc(s["ve"], 0) if False else None
            for si in range(NT512):
                for i in range(4):
                    g = si * 4 + i
                    q.wait_ge(s[f"ldp{g % 2}"], 16 * (g // 2 + 1))
                    a = vinc(q.reduce_sum(stt["s1"][i][:], xs[g % 2][:], AX.X))
                    vwait(a)
                    b = vinc(q.tensor_scalar(stt["mu"][i][:], stt["s1"][i][:],
                                             1.0 / D, None, ALU.mult))
                    q.wait_ge(s["sc"], scb[("sq", si, i)])
                    cv = vinc(q.tensor_scalar(stt["v1"][i][:], stt["s2"][i][:],
                                              1.0 / D, 1e-5, ALU.mult, ALU.add))
                    vwait(b)
                    dv = vinc(q.scalar_tensor_tensor(stt["m2"][i][:], stt["mu"][i][:],
                                                     -1.0, stt["mu"][i][:],
                                                     ALU.mult, ALU.mult))
                    vwait(dv)
                    e = vinc(q.tensor_tensor(stt["var"][i][:], stt["v1"][i][:],
                                             stt["m2"][i][:], ALU.add))
                    vwait(e)
                    q.sem_inc(s["vv"], 1)
                    q.wait_ge(s["sc"], scb[("rs", si, i)])
                    f = vinc(q.scalar_tensor_tensor(stt["nm"][i][:], stt["mu"][i][:],
                                                    -1.0, stt["rs"][i][:],
                                                    ALU.mult, ALU.mult))
                    if g > 0:
                        pg = g - 1
                        q.wait_ge(s["pe"], peb[("tp", pg // 4, (pg % 4) * 8 + 7)])
                    vwait(f)
                    q.tensor_scalar(
                        xn[:], xs[g % 2][:], stt["rs"][i].ap()[:, 0:1],
                        stt["nm"][i].ap()[:, 0:1], ALU.mult, ALU.add,
                    ).then_inc(s["ve"], 1)
                    for c in range(8):
                        jj = i * 8 + c
                        q.wait_ge(s["pe"], peb[("tp", si, jj)])
                        q.tensor_copy(xnT[c].ap()[:, i * 128:(i + 1) * 128],
                                      tp_v[jj % 2]).then_inc(s["ve"], 1)
                for ec in range(NEC):
                    gu = si * NEC + ec
                    if si > 0:
                        q.wait_ge(s["pe"], peb[("cv", si - 1, ec)])
                        hc = vinc(q.tensor_copy(u_nat[ec].ap()[:, 0:3],
                                                u_nat[ec].ap()[:, 512:515]))
                        vwait(hc)
                        q.sem_inc(s["ve"], 1)
                    else:
                        hc = vinc(q.memset(u_nat[ec].ap()[:, 0:3], 0.0))
                        vwait(hc)
                        q.sem_inc(s["ve"], 1)
                    q.wait_ge(s["pe"], peb[("u", si, ec)])
                    ue = vinc(q.tensor_copy(u_nat[ec].ap()[:, 3:515],
                                            pool_v[pool_of[("u", si, ec)] % 4]))
                    vwait(ue)
                    q.sem_inc(s["ve"], 1)
                    if fw is not None:
                        fw.pump_v(q, vinc, vwait)
                for ec in range(NEC):
                    gz = si * NEC + ec
                    q.wait_ge(s["pe"], peb[("z", si, ec)])
                    if gz >= 2:
                        q.wait_ge(s[f"sz{gz % 2}"], 16 * ((gz - 2) // 2 + 1))
                    ze = vinc(q.tensor_copy(zb[gz % 2][:],
                                            pool_v[pool_of[("z", si, ec)] % 4]))
                    vwait(ze)
                    q.sem_inc(s["ve"], 1)
                    if fw is not None:
                        fw.pump_v(q, vinc, vwait)
                q.wait_ge(s["pe"], peb[("xpB", si)])
                if si >= 1:
                    q.wait_ge(s["sb0"], 16 * si)
                b17 = vinc(q.tensor_copy(B17[:], xpB_v))
                vwait(b17)
                q.sem_inc(s["ve"], 1)
                q.wait_ge(s["pe"], peb[("xpd", si)])
                dts = vinc(q.tensor_copy(dt_sb[:], xpd_v))
                vwait(dts)
                q.sem_inc(s["ve"], 1)
                for ec in range(NEC):
                    gd = si * NEC + ec
                    q.wait_ge(s["sc"], scb[("r1", si, ec)])
                    if gd >= 2:
                        q.wait_ge(s[f"sd{gd % 2}"], 16 * ((gd - 2) // 2 + 1))
                    de = vinc(q.tensor_tensor(db[gd % 2][:], r1s[:], l1s[:], ALU.add))
                    vwait(de)
                    q.sem_inc(s["ve"], 1)
                    if fw is not None:
                        fw.pump_v(q, vinc, vwait)
            if fw is not None:
                fw.drain_v(q, vinc, vwait)

        def scalar_f(q):
            ssn = [0]

            def sinc(ins):
                ins.then_inc(s["ss"], 1)
                ssn[0] += 1
                return ssn[0]

            for si in range(NT512):
                for i in range(4):
                    g = si * 4 + i
                    q.wait_ge(s[f"ldp{g % 2}"], 16 * (g // 2 + 1))
                    prior = scb[("sq", si, i)] - 1
                    if prior > 0:
                        q.wait_ge(s["sc"], prior)
                    q.activation(scr[:], xs[g % 2][:], AF.Square,
                                 accum_out=stt["s2"][i][:]).then_inc(s["sc"], 1)
                    q.wait_ge(s["vv"], g + 1)
                    n0 = sinc(q.activation(stt["ln"][i][:], stt["var"][i][:], AF.Ln))
                    q.wait_ge(s["ss"], n0)
                    q.activation(stt["rs"][i][:], stt["ln"][i][:], AF.Exp,
                                 scale=-0.5).then_inc(s["sc"], 1)
                for ec in range(NEC):
                    gu = si * NEC + ec
                    q.wait_ge(s["pe"], peb[("cv", si, ec)])
                    if si >= 1:
                        q.wait_ge(s["pe"], peb[("xpd", si - 1)])
                    if gu >= 2:
                        q.wait_ge(s[f"su{gu % 2}"], 16 * ((gu - 2) // 2 + 1))
                    q.activation(u_act[ec][:],
                                 pool_v[pool_of[("cv", si, ec)] % 4],
                                 AF.Silu).then_inc(s["sc"], 1)
                for ec in range(NEC):
                    q.wait_ge(s["pe"], peb[("dl", si, ec)])
                    pv = pool_v[pool_of[("dl", si, ec)] % 4]
                    bb = dtb.ap()[:, ec:ec + 1]
                    if (si, ec) != (0, 0):
                        pec = ec - 1 if ec > 0 else NEC - 1
                        psi = si if ec > 0 else si - 1
                        q.wait_ge(s["ve"], veb[("dle", psi, pec)])
                    n1 = sinc(q.activation(a1s[:], pv, AF.Abs, bias=bb))
                    q.wait_ge(s["ss"], n1)
                    n2 = sinc(q.activation(e1s[:], a1s[:], AF.Exp, scale=-1.0))
                    q.wait_ge(s["ss"], n2)
                    n3 = sinc(q.activation(l1s[:], e1s[:], AF.Ln,
                                           bias=onec.ap()[:, 0:1]))
                    q.wait_ge(s["ss"], n3)
                    q.activation(r1s[:], pv, AF.Relu, bias=bb).then_inc(s["sc"], 1)

        def tensor_f(q):
            q.wait_ge(s["li"], 16)
            q.wait_ge(s["wl"], NWL)
            q.wait_ge(s["vs"], 64)
            for si in range(NT512):
                for i in range(4):
                    q.wait_ge(s["ve"], veb[("ln", si, i)])
                    for c in range(8):
                        jj = i * 8 + c
                        tot = si * 32 + jj
                        if tot >= 2:
                            pj = tot - 2
                            q.wait_ge(s["ve"], veb[("ev", pj // 32, pj % 32)])
                        q.transpose(tp_v[jj % 2],
                                    xn.ap()[:, c * 128:(c + 1) * 128],
                                    ident[:]).then_inc(s["pe"], 1)
                q.wait_ge(s["ve"], veb[("ev", si, 31)])
                for ec in range(NEC):
                    pv = pool_wait(q, ("u", si, ec))
                    for c in range(ND):
                        ins = q.matmul(pv, ipT[c].ap()[:, ec * 128:(ec + 1) * 128],
                                       xnT[c][:], start=(c == 0), stop=(c == ND - 1))
                        if c == ND - 1:
                            ins.then_inc(s["pe"], 1)
                    pv2 = pool_wait(q, ("cv", si, ec))
                    q.wait_ge(s["ve"], veb[("ue", si, ec)])
                    for j in range(4):
                        ins = q.matmul(pv2, diag[ec].ap()[:, j * 128:(j + 1) * 128],
                                       u_nat[ec].ap()[:, j:j + 512],
                                       start=(j == 0), stop=(j == 3))
                        if j == 3:
                            ins.then_inc(s["pe"], 1)
                for ec in range(NEC):
                    pv = pool_wait(q, ("z", si, ec))
                    for c in range(ND):
                        ins = q.matmul(pv, ipT[c].ap()[:, E + ec * 128:E + (ec + 1) * 128],
                                       xnT[c][:], start=(c == 0), stop=(c == ND - 1))
                        if c == ND - 1:
                            ins.then_inc(s["pe"], 1)
                if si >= 1:
                    q.wait_ge(s["ve"], veb[("B17", si - 1)])
                for k in range(NEC):
                    q.wait_ge(s["sc"], scb[("silu", si, k)])
                    ins = q.matmul(xpB_v, xpB[k][:], u_act[k][:],
                                   start=(k == 0), stop=(k == NEC - 1))
                    if k == NEC - 1:
                        ins.then_inc(s["pe"], 1)
                if si >= 1:
                    q.wait_ge(s["ve"], veb[("dt", si - 1)])
                for k in range(NEC):
                    ins = q.matmul(xpd_v, xpd[k][:], u_act[k][:],
                                   start=(k == 0), stop=(k == NEC - 1))
                    if k == NEC - 1:
                        ins.then_inc(s["pe"], 1)
                q.wait_ge(s["ve"], veb[("dt", si)])
                for ec in range(NEC):
                    pv = pool_wait(q, ("dl", si, ec))
                    q.matmul(pv, dtw.ap()[0:1, ec * 128:(ec + 1) * 128],
                             dt_sb[:], start=True, stop=True).then_inc(s["pe"], 1)

        blk.gpsimd(gpsimd_f)
        blk.sync(sync_f)
        blk.vector(vector_f)
        blk.scalar(scalar_f)
        blk.tensor(tensor_f)

    return nc


def build_mamba_b(p: Prog, io=None):
    nc = p.nc
    if io is None:
        u_in = nc.dram_tensor("u_scr", [E, T], BF16, kind="ExternalInput").ap()
        z_in = nc.dram_tensor("z_scr", [E, T], BF16, kind="ExternalInput").ap()
        d_in = nc.dram_tensor("d_scr", [E, T], BF16, kind="ExternalInput").ap()
        b_in = nc.dram_tensor("b_scr", [NST, T], BF16, kind="ExternalInput").ap()
        ac_d = nc.dram_tensor("a_cols", [E, NST], F32, kind="ExternalInput").ap()
        dc_d = nc.dram_tensor("d_cols", [128, NEC], F32, kind="ExternalInput").ap()
        y_out = nc.dram_tensor("y_scr", [E, T], BF16, kind="ExternalOutput").ap()
    else:
        u_in, z_in, d_in, b_in, ac_d, dc_d, y_out = io

    acolr = [p.sb(f"acolr{k}", [128, NST], F32) for k in range(NEC)]
    acol = [p.sb(f"acol{k}", [128, NST], F32) for k in range(NEC)]
    dcol = p.sb("dcol", [128, NEC], F32)
    dch = [p.sb(f"dch{b}", [128, T], BF16) for b in range(2)]
    uch = [p.sb(f"uch{b}", [128, T], BF16) for b in range(2)]
    zch = [p.sb(f"zch{b}", [128, T], BF16) for b in range(2)]
    brep = [p.sb(f"brep{b}", [128, T], BF16) for b in range(2)]
    at = [p.sb(f"at{b}", [128, T], BF16) for b in range(2)]
    dbt = [p.sb(f"dbt{b}", [128, T], BF16) for b in range(2)]
    ht = p.sb("ht", [128, T], BF16)
    prod = p.sb("prod", [128, T], BF16)
    yacc = p.sb("yacc", [128, T], BF16)
    t1g = p.sb("t1g", [128, T], BF16)
    szg = p.sb("szg", [128, T], BF16)
    ygo = [p.sb(f"ygo{b}", [128, T], BF16) for b in range(2)]

    s = {nm: p.sem(nm) for nm in [
        "wl", "ld0", "ld1", "lu0", "lu1", "lz0", "lz1", "lb0", "lb1",
        "ve", "sc", "sy0", "sy1", "vs", "ss"]}

    with p.block() as blk:
        VEB = NEC * 0 + 16 * 3 + 2

        def ve_db(ec, n):
            return VEB * ec + 3 * n + 1

        def ve_scan(ec, n):
            return VEB * ec + 3 * n + 2

        def ve_acc(ec, n):
            return VEB * ec + 3 * n + 3

        def ve_t1(ec):
            return VEB * ec + 48 + 1

        def ve_yg(ec):
            return VEB * ec + 48 + 2

        SCB = 17

        def sc_a(ec, n):
            return SCB * ec + n + 1

        def sc_sz(ec):
            return SCB * ec + 17

        def gpsimd_f(q):
            for k in range(NEC):
                q.dma_start(out=acolr[k][:],
                            in_=ac_d[k * 128:(k + 1) * 128, :]).then_inc(s["wl"], 16)
            q.dma_start(out=dcol[:], in_=dc_d).then_inc(s["wl"], 16)
        NWL = 16 * (NEC + 1)

        def sync_f(q):
            for ec in range(NEC):
                if ec >= 2:
                    q.wait_ge(s["ve"], ve_acc(ec - 2, NST - 1))
                q.dma_start(out=dch[ec % 2][:],
                            in_=d_in[ec * 128:(ec + 1) * 128, :]).then_inc(s[f"ld{ec % 2}"], 16)
                if ec >= 2:
                    q.wait_ge(s["ve"], ve_t1(ec - 2))
                q.dma_start(out=uch[ec % 2][:],
                            in_=u_in[ec * 128:(ec + 1) * 128, :]).then_inc(s[f"lu{ec % 2}"], 16)
                if ec >= 2:
                    q.wait_ge(s["ve"], ve_yg(ec - 2))
                q.dma_start(out=zch[ec % 2][:],
                            in_=z_in[ec * 128:(ec + 1) * 128, :]).then_inc(s[f"lz{ec % 2}"], 16)
                for n in range(NST):
                    g = ec * NST + n
                    if g >= 2:
                        pg = g - 2
                        q.wait_ge(s["ve"], ve_acc(pg // NST, pg % NST))
                    q.dma_start(
                        out=brep[g % 2][:],
                        in_=b_in[n:n + 1, :].broadcast_to([128, T]),
                    ).then_inc(s[f"lb{g % 2}"], 16)
                q.wait_ge(s["ve"], ve_yg(ec))
                q.dma_start(out=y_out[ec * 128:(ec + 1) * 128, :],
                            in_=ygo[ec % 2][:]).then_inc(s[f"sy{ec % 2}"], 16)
            q.wait_ge(s["sy0"], 16 * 8)
            q.wait_ge(s["sy1"], 16 * 8)

        def vector_f(q):
            vsn = [0]

            def vinc(ins):
                ins.then_inc(s["vs"], 1)
                vsn[0] += 1
                return vsn[0]

            def vwait(n):
                if n > 0:
                    q.wait_ge(s["vs"], n)

            for ec in range(NEC):
                for n in range(NST):
                    g = ec * NST + n
                    q.wait_ge(s[f"lb{g % 2}"], 16 * (g // 2 + 1))
                    q.wait_ge(s[f"ld{ec % 2}"], 16 * (ec // 2 + 1))
                    db_ = vinc(q.tensor_tensor(dbt[g % 2][:], dch[ec % 2][:],
                                               brep[g % 2][:], ALU.mult))
                    ins = q.tensor_tensor(dbt[g % 2][:], dch[ec % 2][:],
                                          brep[g % 2][:], ALU.mult) if False else None
                    vwait(db_)
                    q.sem_inc(s["ve"], 1)
                    q.wait_ge(s["sc"], sc_a(ec, n))
                    sc_ = vinc(q.tensor_tensor_scan(ht[:], at[g % 2][:], dbt[g % 2][:],
                                                    0.0, ALU.mult, ALU.add))
                    vwait(sc_)
                    q.sem_inc(s["ve"], 1)
                    if n == 0:
                        pr = vinc(q.tensor_tensor(yacc[:], ht[:], brep[g % 2][:],
                                                  ALU.mult))
                    else:
                        pr = vinc(q.tensor_tensor(prod[:], ht[:], brep[g % 2][:],
                                                  ALU.mult))
                        vwait(pr)
                        pr = vinc(q.tensor_tensor(yacc[:], yacc[:], prod[:], ALU.add))
                    vwait(pr)
                    q.sem_inc(s["ve"], 1)
                q.wait_ge(s[f"lu{ec % 2}"], 16 * (ec // 2 + 1))
                t1_ = vinc(q.scalar_tensor_tensor(t1g[:], uch[ec % 2][:],
                                                  dcol.ap()[:, ec:ec + 1],
                                                  yacc[:], ALU.mult, ALU.add))
                vwait(t1_)
                q.sem_inc(s["ve"], 1)
                q.wait_ge(s["sc"], sc_sz(ec))
                if ec >= 2:
                    q.wait_ge(s[f"sy{ec % 2}"], 16 * ((ec - 2) // 2 + 1))
                yg_ = vinc(q.tensor_tensor(ygo[ec % 2][:], t1g[:], szg[:], ALU.mult))
                vwait(yg_)
                q.sem_inc(s["ve"], 1)

        def scalar_f(q):
            ssn = [0]

            def sinc(ins):
                ins.then_inc(s["ss"], 1)
                ssn[0] += 1
                return ssn[0]

            q.wait_ge(s["wl"], NWL)
            for k in range(NEC):
                n1 = sinc(q.activation(acol[k][:], acolr[k][:], AF.Exp))
                q.wait_ge(s["ss"], n1)
                n2 = sinc(q.activation(acol[k][:], acol[k][:], AF.Copy, scale=-1.0))
                q.wait_ge(s["ss"], n2)
            for ec in range(NEC):
                for n in range(NST):
                    g = ec * NST + n
                    q.wait_ge(s[f"ld{ec % 2}"], 16 * (ec // 2 + 1))
                    if g >= 2:
                        pg = g - 2
                        q.wait_ge(s["ve"], ve_scan(pg // NST, pg % NST))
                    q.activation(at[g % 2][:], dch[ec % 2][:], AF.Exp,
                                 scale=acol[ec].ap()[:, n:n + 1]).then_inc(s["sc"], 1)
                q.wait_ge(s[f"lz{ec % 2}"], 16 * (ec // 2 + 1))
                if ec >= 1:
                    q.wait_ge(s["ve"], ve_yg(ec - 1))
                q.activation(szg[:], zch[ec % 2][:], AF.Silu).then_inc(s["sc"], 1)

        blk.gpsimd(gpsimd_f)
        blk.sync(sync_f)
        blk.vector(vector_f)
        blk.scalar(scalar_f)

    return nc


def build_mamba_c(p: Prog, io=None):
    nc = p.nc
    if io is None:
        xin = nc.dram_tensor("xin", [T, D], F32, kind="ExternalInput").ap()
        y_in = nc.dram_tensor("y_scr", [E, T], BF16, kind="ExternalInput").ap()
        op_d = nc.dram_tensor("opT", [E, D], F32, kind="ExternalInput").ap()
        ob_d = nc.dram_tensor("ob_b", [128, D], F32, kind="ExternalInput").ap()
        xout = nc.dram_tensor("xout", [T, D], F32, kind="ExternalOutput").ap()
    else:
        xin, y_in, op_d, ob_d, xout = io

    opT = [p.sb(f"opT{k}", [128, D], BF16) for k in range(NEC)]
    ob = p.sb("ob", [128, D], F32)
    ygt = [[p.sb(f"ygt{b}_{k}", [128, 512], BF16) for k in range(NEC)]
           for b in range(2)]
    xr = [p.sb(f"xr{b}", [128, 512], F32) for b in range(2)]
    xo = [p.sb(f"xo{b}", [128, 512], F32) for b in range(2)]
    ps_o = [p.ps(f"ps_o{b}", [128, 512], F32) for b in range(2)]

    s = {nm: p.sem(nm) for nm in [
        "wl", "ly", "lrp0", "lrp1", "ve", "pe", "stp0", "stp1", "vs"]}

    def ve_res(gj):
        return gj + 1

    def pe_o(gj):
        return gj + 1

    with p.block() as blk:

        def gpsimd_f(q):
            for k in range(NEC):
                q.dma_start(out=opT[k][:],
                            in_=op_d[k * 128:(k + 1) * 128, :]).then_inc(s["wl"], 16)
            q.dma_start(out=ob[:], in_=ob_d).then_inc(s["wl"], 16)
        NWL = 16 * (NEC + 1)

        def sync_f(q):
            for si in range(NT512):
                for k in range(NEC):
                    if si >= 2:
                        q.wait_ge(s["pe"], pe_o((si - 2) * 8 + 7))
                    q.dma_start(out=ygt[si % 2][k][:],
                                in_=y_in[k * 128:(k + 1) * 128,
                                         si * 512:(si + 1) * 512]).then_inc(s["ly"], 16)
                for j in range(8):
                    gj = si * 8 + j
                    qs, dn = j // 2, j % 2
                    if gj >= 2:
                        pj = gj - 2
                        q.wait_ge(s["ve"], ve_res(pj))
                    q.dma_start(out=xr[gj % 2][:],
                                in_=xin[(si * 4 + qs) * 128:(si * 4 + qs + 1) * 128,
                                        dn * 512:(dn + 1) * 512],
                                ).then_inc(s[f"lrp{gj % 2}"], 16)
                    if gj >= 2:
                        pj = gj - 2
                        q.dma_start(
                            out=xout[(pj // 2) * 128:(pj // 2 + 1) * 128,
                                     (pj % 2) * 512:(pj % 2 + 1) * 512],
                            in_=xo[pj % 2][:]).then_inc(s[f"stp{pj % 2}"], 16)
            for gj in range(NT512 * 8 - 2, NT512 * 8):
                q.wait_ge(s["ve"], ve_res(gj))
                q.dma_start(
                    out=xout[(gj // 2) * 128:(gj // 2 + 1) * 128,
                             (gj % 2) * 512:(gj % 2 + 1) * 512],
                    in_=xo[gj % 2][:]).then_inc(s[f"stp{gj % 2}"], 16)
            q.wait_ge(s["stp0"], 16 * NT128)
            q.wait_ge(s["stp1"], 16 * NT128)

        def vector_f(q):
            vsn = [0]

            def vinc(ins):
                ins.then_inc(s["vs"], 1)
                vsn[0] += 1
                return vsn[0]

            def vwait(n):
                if n > 0:
                    q.wait_ge(s["vs"], n)

            for si in range(NT512):
                for j in range(8):
                    gj = si * 8 + j
                    qs, dn = j // 2, j % 2
                    q.wait_ge(s["pe"], pe_o(gj))
                    if gj >= 2:
                        q.wait_ge(s[f"stp{gj % 2}"], 16 * ((gj - 2) // 2 + 1))
                    ov = vinc(q.tensor_tensor(xo[gj % 2][:], ps_o[gj % 2][:],
                                              ob.ap()[:, dn * 512:(dn + 1) * 512],
                                              ALU.add))
                    vwait(ov)
                    q.sem_inc(s["ve"], 0) if False else None
                    q.wait_ge(s[f"lrp{gj % 2}"], 16 * (gj // 2 + 1))
                    o2 = vinc(q.tensor_tensor(xo[gj % 2][:], xo[gj % 2][:],
                                              xr[gj % 2][:], ALU.add))
                    vwait(o2)
                    q.sem_inc(s["ve"], 1)

        def tensor_f(q):
            q.wait_ge(s["wl"], NWL)
            for si in range(NT512):
                q.wait_ge(s["ly"], 16 * NEC * (si + 1))
                for j in range(8):
                    gj = si * 8 + j
                    qs, dn = j // 2, j % 2
                    if gj >= 2:
                        q.wait_ge(s["ve"], ve_res(gj - 2))
                    for k in range(NEC):
                        ins = q.matmul(ps_o[gj % 2][:],
                                       ygt[si % 2][k].ap()[:, qs * 128:(qs + 1) * 128],
                                       opT[k].ap()[:, dn * 512:(dn + 1) * 512],
                                       start=(k == 0), stop=(k == NEC - 1))
                        if k == NEC - 1:
                            ins.then_inc(s["pe"], 1)

        blk.gpsimd(gpsimd_f)
        blk.sync(sync_f)
        blk.vector(vector_f)
        blk.tensor(tensor_f)

    return nc


def _u16(d):
    return {k: (v.view(np.uint16) if v.dtype == BF else np.ascontiguousarray(v))
            for k, v in d.items()}


def build_fused(p: Prog):
    nc = p.nc
    dt = nc.dram_tensor
    x0 = dt("x0", [T, D], F32, kind="ExternalInput").ap()
    ident_d = dt("ident", [128, 128], BF16, kind="ExternalInput").ap()
    ip_d = dt("ipT", [D, 2 * E], F32, kind="ExternalInput").ap()
    cw_d = dt("cwcol", [128, NEC * 4], F32, kind="ExternalInput").ap()
    xpb_d = dt("xpTB", [E, NST], F32, kind="ExternalInput").ap()
    xpd_d = dt("xpTd", [E, 1], F32, kind="ExternalInput").ap()
    dtw_d = dt("dtw_row", [1, E], F32, kind="ExternalInput").ap()
    dtb_d = dt("dtb_col", [128, NEC], F32, kind="ExternalInput").ap()
    one_d = dt("one_col", [128, 1], F32, kind="ExternalInput").ap()
    ac_d = dt("a_cols", [E, NST], F32, kind="ExternalInput").ap()
    dc_d = dt("d_cols", [128, NEC], F32, kind="ExternalInput").ap()
    op_d = dt("opT", [E, D], F32, kind="ExternalInput").ap()
    ob_d = dt("ob_b", [128, D], F32, kind="ExternalInput").ap()
    wg0_d = dt("wg0", [D, H0], I32, kind="ExternalInput").ap()
    wu0_d = dt("wu0", [D, H0], I32, kind="ExternalInput").ap()
    wd0_d = dt("wd0", [H0, D], I32, kind="ExternalInput").ap()
    wq_d = dt("wq", [D, D], F32, kind="ExternalInput").ap()
    wk_d = dt("wk", [D, D], F32, kind="ExternalInput").ap()
    wv_d = dt("wv", [D, D], F32, kind="ExternalInput").ap()
    wo_d = dt("wo", [D, D], F32, kind="ExternalInput").ap()
    bcol_d = dt("bcol", [128, 16], F32, kind="ExternalInput").ap()
    bvb_d = dt("bvb", [128, D], F32, kind="ExternalInput").ap()
    bob_d = dt("bob", [128, D], F32, kind="ExternalInput").ap()
    cos_d = dt("cos_t", [128, T], BF16, kind="ExternalInput").ap()
    sin_d = dt("sin_t", [128, T], BF16, kind="ExternalInput").ap()
    msk_d = dt("msk", [128, 4 * 512], BF16, kind="ExternalInput").ap()
    negC_d = dt("negC", [128, 1], F32, kind="ExternalInput").ap()
    ones_d = dt("ones_r", [1, 64], F32, kind="ExternalInput").ap()
    wg1_d = dt("wg1", [I1, D], F32, kind="ExternalInput").ap()
    wu1_d = dt("wu1", [I1, D], F32, kind="ExternalInput").ap()
    wd1_d = dt("wd1", [D, I1], F32, kind="ExternalInput").ap()
    out_d = dt("out", [T, D], F32, kind="ExternalOutput").ap()
    u_scr, z_scr, d_scr, b_scr, y_scr = mamba_scratch(nc)
    x1 = dt("x1_scr", [T, D], F32, kind="Internal").ap()
    x2 = dt("x2_scr", [T, D], F32, kind="Internal").ap()
    x3 = dt("x3_scr", [T, D], F32, kind="Internal").ap()
    wgq1 = dt("wgq1_scr", [I1, D], BF16, kind="Internal").ap()
    wuq1 = dt("wuq1_scr", [I1, D], BF16, kind="Internal").ap()
    wdq1 = dt("wdq1_scr", [D, I1], BF16, kind="Internal").ap()

    p.push("ma")
    build_mamba_a(p, io=(x0, ip_d, cw_d, xpb_d, xpd_d, dtw_d, dtb_d, one_d,
                         ident_d, u_scr, z_scr, d_scr, b_scr),
                   fwprep=(wg1_d, wu1_d, wd1_d, wgq1, wuq1, wdq1))
    p.pop()
    p.push("mb")
    build_mamba_b(p, io=(u_scr, z_scr, d_scr, b_scr, ac_d, dc_d, y_scr))
    p.pop()
    p.push("mc")
    build_mamba_c(p, io=(x0, y_scr, op_d, ob_d, x1))
    p.pop()
    p.push("f0")
    build_ffn0(p, io=(x1, wg0_d, wu0_d, wd0_d, ident_d, x2))
    p.pop()
    p.push("at")
    build_attn(p, io=(x2, wq_d, wk_d, wv_d, wo_d, bcol_d, bvb_d, bob_d,
                      cos_d, sin_d, msk_d, ident_d, negC_d, ones_d, x3))
    p.pop()
    p.push("fa")
    build_ffn1(p, io=(x3, wgq1, wuq1, wdq1, ident_d, out_d))
    p.pop()
    return nc


def kernel_fused(**inputs):
    B = 8
    cores = list(range(B))
    hp = _host_prep(**inputs)
    p = Prog()
    nc = build_fused(p)
    maps = []
    for b in range(B):
        m = dict(hp["shared"])
        m["x0"] = hp["x"][b]
        maps.append(m)
    res = run_bass_kernel_spmd(nc, maps, core_ids=cores).results
    return np.stack([res[b]["out"] for b in range(B)], axis=0)


def _host_prep(L0_attn_x, L0_attn_in_proj_w, L0_attn_conv_w, L0_attn_x_proj_w,
               L0_attn_dt_w, L0_attn_dt_b, L0_attn_A_param, L0_attn_D_param,
               L0_attn_out_w, L0_attn_out_b, L0_ffn_w_gate, L0_ffn_w_up,
               L0_ffn_w_down, L1_attn_Wq, L1_attn_bq, L1_attn_Wk, L1_attn_bk,
               L1_attn_Wv, L1_attn_bv, L1_attn_Wo, L1_attn_bo, L1_ffn_w_gate,
               L1_ffn_w_up, L1_ffn_w_down):
    x = np.ascontiguousarray(np.asarray(L0_attn_x, dtype=np.float32))
    ident = np.eye(128).astype(BF)
    cwcol = np.zeros((128, NEC * 4), np.float32)
    cwa = np.asarray(L0_attn_conv_w)
    for k in range(NEC):
        for j in range(4):
            cwcol[:, k * 4 + j] = cwa[k * 128:(k + 1) * 128, 0, j]
    xpw = np.asarray(L0_attn_x_proj_w)
    dtb_col = np.zeros((128, NEC), np.float32)
    dtba = np.asarray(L0_attn_dt_b)
    for k in range(NEC):
        dtb_col[:, k] = dtba[k * 128:(k + 1) * 128]
    d_cols = np.zeros((128, NEC), np.float32)
    dpa = np.asarray(L0_attn_D_param)
    for k in range(NEC):
        d_cols[:, k] = dpa[k * 128:(k + 1) * 128]
    bq, bk = np.asarray(L1_attn_bq), np.asarray(L1_attn_bk)
    bcol = np.zeros((128, 16), np.float32)
    for c in range(8):
        bcol[:, c] = bq[c * 128:(c + 1) * 128]
        bcol[:, 8 + c] = bk[c * 128:(c + 1) * 128]
    shared = dict(
        ident=ident,
        ipT=np.ascontiguousarray(np.asarray(L0_attn_in_proj_w).T),
        cwcol=cwcol,
        xpTB=np.ascontiguousarray(xpw[1:].T),
        xpTd=np.ascontiguousarray(xpw[0:1].T),
        dtw_row=np.ascontiguousarray(np.asarray(L0_attn_dt_w).T),
        dtb_col=dtb_col,
        one_col=np.ones((128, 1), np.float32),
        a_cols=np.ascontiguousarray(np.asarray(L0_attn_A_param)),
        d_cols=d_cols,
        opT=np.ascontiguousarray(np.asarray(L0_attn_out_w).T),
        ob_b=np.ascontiguousarray(np.broadcast_to(np.asarray(L0_attn_out_b), (128, D))),
        wg0=np.ascontiguousarray(np.asarray(L0_ffn_w_gate).T),
        wu0=np.ascontiguousarray(np.asarray(L0_ffn_w_up).T),
        wd0=np.ascontiguousarray(np.asarray(L0_ffn_w_down).T),
        wq=np.ascontiguousarray(np.asarray(L1_attn_Wq).T),
        wk=np.ascontiguousarray(np.asarray(L1_attn_Wk).T),
        wv=np.ascontiguousarray(np.asarray(L1_attn_Wv).T),
        wo=np.ascontiguousarray(np.asarray(L1_attn_Wo).T),
        bcol=bcol,
        bvb=np.ascontiguousarray(np.broadcast_to(np.asarray(L1_attn_bv), (128, D))),
        bob=np.ascontiguousarray(np.broadcast_to(np.asarray(L1_attn_bo), (128, D))),
        wg1=np.ascontiguousarray(np.asarray(L1_ffn_w_gate)),
        wu1=np.ascontiguousarray(np.asarray(L1_ffn_w_up)),
        wd1=np.ascontiguousarray(np.asarray(L1_ffn_w_down)),
        **attn_host_consts(),
    )
    shared = _u16(shared)
    return dict(x=x, shared=shared)


def kernel(**inputs):
    return kernel_fused(**inputs)


# version 21
# speedup vs baseline: 1.1815x; 1.0043x over previous
import math
from contextlib import ExitStack

import numpy as np
import ml_dtypes
import concourse.bass as bass
import concourse.mybir as mybir
from concourse.bass_utils import run_bass_kernel_spmd

F32 = mybir.dt.float32
BF16 = mybir.dt.bfloat16
I32 = mybir.dt.int32
AF = mybir.ActivationFunctionType
ALU = mybir.AluOpType
AX = mybir.AxisListType
BF = ml_dtypes.bfloat16

T = 4096
D = 1024
E = 2048
NST = 16
H0 = 3072
I1 = 2816
NH, HD = 16, 64
NT128 = T // 128
NT512 = T // 512


class _IP:

    def __init__(self, ins, pend):
        self._ins = ins
        self._pend = pend

    def then_inc(self, sem, k):
        self._pend[sem.num] = self._pend.get(sem.num, 0) + k
        self._ins.then_inc(sem, k)
        return self

    def annotate(self, a):
        self._ins.annotate(a)
        return self

    def __getattr__(self, name):
        return getattr(self._ins, name)


class _QP:

    def __init__(self, q, base, pend):
        self._q = q
        self._base = base
        self._pend = pend

    def wait_ge(self, sem, val):
        return self._q.wait_ge(sem, val + self._base.get(sem.num, 0))

    def sem_inc(self, sem, k):
        self._pend[sem.num] = self._pend.get(sem.num, 0) + k
        return self._q.sem_inc(sem, k)

    def __getattr__(self, name):
        attr = getattr(self._q, name)
        if not callable(attr):
            return attr
        pend = self._pend

        def call(*a, **kw):
            r = attr(*a, **kw)
            if r is not None and hasattr(r, "then_inc"):
                return _IP(r, pend)
            return r
        return call


class _BlockWrap:
    def __init__(self, p):
        self._p = p
        self._blk = None

    def __enter__(self):
        self._blk = self._p.nc.Block()
        self._blk.__enter__()
        return self

    def __exit__(self, *a):
        r = self._blk.__exit__(*a)
        for k, v in self._p._pend.items():
            self._p._base[k] = self._p._base.get(k, 0) + v
        self._p._pend.clear()
        return r

    def _mk(self, engname):
        def reg(f):
            getattr(self._blk, engname)(
                lambda q: f(_QP(q, self._p._base, self._p._pend)))
        return reg

    def __getattr__(self, name):
        if name in ("sync", "vector", "scalar", "tensor", "gpsimd"):
            return self._mk(name)
        return getattr(self._blk, name)


class Prog:
    def __init__(self):
        self.nc = bass.Bass("TRN2", target_bir_lowering=False)
        self.stack = ExitStack()
        self._names = set()
        self._scopes = []
        self._tag = ""
        self._base = {}
        self._pend = {}

    def _st(self):
        return self._scopes[-1] if self._scopes else self.stack

    def push(self, tag):
        self._scopes.append(ExitStack())
        self._tag = tag + "_"

    def pop(self):
        self._scopes.pop().close()
        self._tag = ""

    def sb(self, name, shape, dtype):
        name = self._tag + name
        assert name not in self._names, name
        self._names.add(name)
        return self._st().enter_context(self.nc.sbuf_tensor(name, shape, dtype))

    def ps(self, name, shape, dtype):
        name = self._tag + name
        assert name not in self._names, name
        self._names.add(name)
        return self._st().enter_context(self.nc.psum_tensor(name, shape, dtype))

    def sem(self, name):
        name = self._tag + name
        assert name not in self._names, name
        self._names.add(name)
        return self._st().enter_context(self.nc.semaphore(name))

    def block(self):
        return _BlockWrap(self)

    def close(self):
        self.stack.close()


class WeightLoad:

    def __init__(self, tag, dram_ap, O, IN, dest):
        self.tag = tag
        self.dram = dram_ap
        self.O, self.IN = O, IN
        self.dest = dest
        self.nj = IN // 128

    def emit_loads(self, q, sem):
        for j in range(self.nj):
            q.dma_start(
                out=self.dest[j][:],
                in_=self.dram[j * 128:(j + 1) * 128, :],
            ).then_inc(sem, 16)

    def full(self):
        return 16 * self.nj


def build_ffn0(p: Prog, io=None):
    nc = p.nc
    if io is None:
        xin = nc.dram_tensor("xin", [T, D], F32, kind="ExternalInput").ap()
        wg_d = nc.dram_tensor("wg", [D, H0], I32, kind="ExternalInput").ap()
        wu_d = nc.dram_tensor("wu", [D, H0], I32, kind="ExternalInput").ap()
        wd_d = nc.dram_tensor("wd", [H0, D], I32, kind="ExternalInput").ap()
        ident_d = nc.dram_tensor("ident", [128, 128], BF16, kind="ExternalInput").ap()
        xout = nc.dram_tensor("xout", [T, D], F32, kind="ExternalOutput").ap()
    else:
        xin, wg_d, wu_d, wd_d, ident_d, xout = io

    NKH = H0 // 128
    ND = D // 128

    wgT = [p.sb(f"wgT{c}", [128, H0], BF16) for c in range(ND)]
    wuT = [p.sb(f"wuT{c}", [128, H0], BF16) for c in range(ND)]
    wdT = [p.sb(f"wdT{k}", [128, D], BF16) for k in range(NKH)]

    xs = [p.sb(f"xs{b}", [128, D], F32) for b in range(2)]
    xr = [p.sb(f"xr{b}", [128, 512], F32) for b in range(2)]
    xo = [p.sb(f"xo{b}", [128, 512], F32) for b in range(2)]
    xn = p.sb("xn", [128, D], BF16)
    xnT = [p.sb(f"xnT{c}", [128, 512], BF16) for c in range(ND)]
    hsb = [p.sb(f"hsb{k}", [128, 512], BF16) for k in range(NKH)]
    sgb = [p.sb(f"sgb{b}", [128, 512], BF16) for b in range(2)]
    scr = p.sb("scr", [128, D], BF16)
    ident = p.sb("identt", [128, 128], BF16)

    stt = {nm: [p.sb(f"st_{nm}{i}", [128, 1], F32) for i in range(4)]
           for nm in ["s1", "s2", "mu", "v1", "m2", "var", "ln", "rs", "rs2", "nm"]}

    ps_tp = [p.ps(f"ps_tp{b}", [128, 128], BF16) for b in range(2)]
    ps_g = [p.ps(f"ps_g{b}", [128, 512], F32) for b in range(2)]
    ps_u = [p.ps(f"ps_u{b}", [128, 512], F32) for b in range(2)]
    ps_o = [p.ps(f"ps_o{b}", [128, 512], F32) for b in range(2)]

    s = {nm: p.sem(nm) for nm in ["wlg", "wlu", "wld", "li",
                                  "ldp0", "ldp1", "lrp0", "lrp1",
                                  "ve", "vv", "sc", "pe", "stp0", "stp1", "vs", "ss"]}

    wg_p = WeightLoad("wg", wg_d, H0, D, wgT)
    wu_p = WeightLoad("wu", wu_d, H0, D, wuT)
    wd_p = WeightLoad("wd", wd_d, D, H0, wdT)

    INV127 = 1.0 / 127.0

    VE = 68

    def ve_ln(si, i):
        return VE * si + 9 * i + 1

    def ve_ev(si, j):
        i, c = j // 8, j % 8
        return VE * si + 9 * i + 2 + c

    def ve_h(si, k):
        return VE * si + 36 + k + 1

    def ve_res(si, j):
        return VE * si + 60 + j + 1

    SC = 32

    def sc_sq(si, i):
        return SC * si + 2 * i + 1

    def sc_rs(si, i):
        return SC * si + 2 * i + 2

    def sc_si(si, k):
        return SC * si + 8 + k + 1

    PE = 88

    def pe_tp(si, j):
        return PE * si + j + 1

    def pe_g(si, k):
        return PE * si + 32 + 2 * k + 1

    def pe_u(si, k):
        return PE * si + 32 + 2 * k + 2

    def pe_o(si, j):
        return PE * si + 80 + j + 1

    def ld_x(g):
        return (f"ldp{g % 2}", 16 * (g // 2 + 1))

    def ld_r(cnt):
        return (f"lrp{cnt % 2}", 16 * (cnt // 2 + 1))

    with p.block() as blk:

        def gpsimd_f(q):
            wg_p.emit_loads(q, s["wlg"])
            wu_p.emit_loads(q, s["wlu"])
            wd_p.emit_loads(q, s["wld"])

        def scalarq_f(q):
            ssn = [0]

            def sinc(ins):
                ins.then_inc(s["ss"], 1)
                ssn[0] += 1
                return ssn[0]

            for si in range(NT512):
                for i in range(4):
                    g = si * 4 + i
                    nm_, v_ = ld_x(g)
                    q.wait_ge(s[nm_], v_)
                    prior = SC * si + 2 * i
                    if prior > 0:
                        q.wait_ge(s["sc"], prior)
                    q.activation(scr[:], xs[g % 2][:], AF.Square,
                                 accum_out=stt["s2"][i][:]).then_inc(s["sc"], 1)
                    q.wait_ge(s["vv"], g + 1)
                    n0 = sinc(q.activation(stt["ln"][i][:], stt["var"][i][:], AF.Ln))
                    q.wait_ge(s["ss"], n0)
                    q.activation(stt["rs"][i][:], stt["ln"][i][:], AF.Exp,
                                 scale=-0.5).then_inc(s["sc"], 1)
                for k in range(NKH):
                    q.wait_ge(s["pe"], pe_g(si, k))
                    if k >= 2 or si > 0:
                        kk, ss_ = (k - 2, si) if k >= 2 else (22 + k, si - 1)
                        q.wait_ge(s["ve"], ve_h(ss_, kk))
                    q.activation(sgb[k % 2][:], ps_g[k % 2][:],
                                 AF.Silu).then_inc(s["sc"], 1)

        def sync_f(q):
            q.dma_start(out=ident[:], in_=ident_d).then_inc(s["li"], 16)
            for si in range(NT512):
                for i in range(4):
                    g = si * 4 + i
                    if g >= 2:
                        q.wait_ge(s["ve"], ve_ln((g - 2) // 4, (g - 2) % 4))
                    q.dma_start(
                        out=xs[g % 2][:],
                        in_=xin[g * 128:(g + 1) * 128, :],
                    ).then_inc(s[f"ldp{g % 2}"], 16)
                def emit_store(jj):
                    sub_, dn_ = jj // 2, jj % 2
                    gj_ = si * 8 + jj
                    q.wait_ge(s["ve"], ve_res(si, jj))
                    q.dma_start(
                        out=xout[(si * 4 + sub_) * 128:(si * 4 + sub_ + 1) * 128,
                                 dn_ * 512:(dn_ + 1) * 512],
                        in_=xo[jj % 2][:],
                    ).then_inc(s[f"stp{gj_ % 2}"], 16)

                for j in range(8):
                    sub, dn = j // 2, j % 2
                    gj = si * 8 + j
                    if gj >= 2:
                        pj = gj - 2
                        q.wait_ge(s["ve"], ve_res(pj // 8, pj % 8))
                    q.dma_start(
                        out=xr[j % 2][:],
                        in_=xin[(si * 4 + sub) * 128:(si * 4 + sub + 1) * 128,
                                dn * 512:(dn + 1) * 512],
                    ).then_inc(s[f"lrp{gj % 2}"], 16)
                    if j >= 2:
                        emit_store(j - 2)
                emit_store(6)
                emit_store(7)
            q.wait_ge(s["stp0"], 16 * NT512 * 4)
            q.wait_ge(s["stp1"], 16 * NT512 * 4)

        def vector_f(q):
            vsn = [0]

            def vinc(ins):
                ins.then_inc(s["vs"], 1)
                vsn[0] += 1
                return vsn[0]

            def vwait(n):
                if n > 0:
                    q.wait_ge(s["vs"], n)

            for si in range(NT512):
                ev = {}
                for i in range(4):
                    g = si * 4 + i
                    nm_, v_ = ld_x(g)
                    q.wait_ge(s[nm_], v_)
                    ev[("s1", i)] = vinc(q.reduce_sum(stt["s1"][i][:],
                                                      xs[g % 2][:], AX.X))
                    vwait(ev[("s1", i)])
                    ev[("mu", i)] = vinc(q.tensor_scalar(
                        stt["mu"][i][:], stt["s1"][i][:], 1.0 / D, None, ALU.mult))
                    q.wait_ge(s["sc"], sc_sq(si, i))
                    ev[("v1", i)] = vinc(q.tensor_scalar(
                        stt["v1"][i][:], stt["s2"][i][:], 1.0 / D, 1e-5,
                        ALU.mult, ALU.add))
                    vwait(ev[("mu", i)])
                    ev[("m2", i)] = vinc(q.scalar_tensor_tensor(
                        stt["m2"][i][:], stt["mu"][i][:], -1.0, stt["mu"][i][:],
                        ALU.mult, ALU.mult))
                    vwait(ev[("m2", i)])
                    ins = q.tensor_tensor(stt["var"][i][:], stt["v1"][i][:],
                                          stt["m2"][i][:], ALU.add)
                    ev[("var", i)] = vinc(ins)
                    vwait(ev[("var", i)])
                    q.sem_inc(s["vv"], 1)
                    q.wait_ge(s["sc"], sc_rs(si, i))
                    ev[("nm0", i)] = vinc(q.tensor_scalar(
                        stt["nm"][i][:], stt["rs"][i][:], INV127, None, ALU.mult))
                    vwait(ev[("nm0", i)])
                    ev[("nm", i)] = vinc(q.scalar_tensor_tensor(
                        stt["nm"][i][:], stt["mu"][i][:], -1.0, stt["nm"][i][:],
                        ALU.mult, ALU.mult))
                    ev[("rs2", i)] = vinc(q.tensor_scalar(
                        stt["rs2"][i][:], stt["rs"][i][:], INV127, None, ALU.mult))
                    if g > 0:
                        pg = g - 1
                        q.wait_ge(s["pe"], pe_tp(pg // 4, (pg % 4) * 8 + 7))
                    vwait(max(ev[("rs2", i)], ev[("nm", i)]))
                    q.tensor_scalar(
                        xn[:], xs[g % 2][:], stt["rs2"][i].ap()[:, 0:1],
                        stt["nm"][i].ap()[:, 0:1], ALU.mult, ALU.add,
                    ).then_inc(s["ve"], 1)
                    for c in range(8):
                        j = i * 8 + c
                        q.wait_ge(s["pe"], pe_tp(si, j))
                        q.tensor_copy(xnT[c].ap()[:, i * 128:(i + 1) * 128],
                                      ps_tp[j % 2][:]).then_inc(s["ve"], 1)
                for k in range(NKH):
                    q.wait_ge(s["sc"], sc_si(si, k))
                    q.wait_ge(s["pe"], pe_u(si, k))
                    q.scalar_tensor_tensor(hsb[k][:], sgb[k % 2][:], INV127,
                                           ps_u[k % 2][:], ALU.mult,
                                           ALU.mult).then_inc(s["ve"], 1)
                for j in range(8):
                    gj = si * 8 + j
                    q.wait_ge(s["pe"], pe_o(si, j))
                    nm_, v_ = ld_r(gj)
                    q.wait_ge(s[nm_], v_)
                    if gj >= 2:
                        q.wait_ge(s[f"stp{gj % 2}"], 16 * ((gj - 2) // 2 + 1))
                    q.tensor_tensor(xo[j % 2][:], xr[j % 2][:], ps_o[j % 2][:],
                                    ALU.add).then_inc(s["ve"], 1)

        def tensor_f(q):
            q.wait_ge(s["li"], 16)
            for si in range(NT512):
                for j in range(32):
                    i, c = j // 8, j % 8
                    q.wait_ge(s["ve"], ve_ln(si, i))
                    gj = si * 32 + j
                    if gj >= 2:
                        pj = gj - 2
                        q.wait_ge(s["ve"], ve_ev(pj // 32, pj % 32))
                    q.transpose(ps_tp[j % 2][:],
                                xn.ap()[:, c * 128:(c + 1) * 128],
                                ident[:]).then_inc(s["pe"], 1)
                for k in range(NKH):
                    if si == 0 and k == 0:
                        q.wait_ge(s["wlg"], wg_p.full())
                        q.wait_ge(s["wlu"], wu_p.full())
                    if k == 0:
                        q.wait_ge(s["ve"], VE * si + 36)
                    if k >= 2 or si > 0:
                        kk, ss = (k - 2, si) if k >= 2 else (22 + k, si - 1)
                        q.wait_ge(s["sc"], sc_si(ss, kk))
                    for c in range(ND):
                        ins = q.matmul(ps_g[k % 2][:],
                                       wgT[c].ap()[:, k * 128:(k + 1) * 128],
                                       xnT[c][:], start=(c == 0),
                                       stop=(c == ND - 1))
                        if c == ND - 1:
                            ins.then_inc(s["pe"], 1)
                    if k >= 2 or si > 0:
                        kk, ss = (k - 2, si) if k >= 2 else (22 + k, si - 1)
                        q.wait_ge(s["ve"], ve_h(ss, kk))
                    for c in range(ND):
                        ins = q.matmul(ps_u[k % 2][:],
                                       wuT[c].ap()[:, k * 128:(k + 1) * 128],
                                       xnT[c][:], start=(c == 0),
                                       stop=(c == ND - 1))
                        if c == ND - 1:
                            ins.then_inc(s["pe"], 1)
                for j in range(8):
                    sub, dn = j // 2, j % 2
                    if si == 0 and j == 0:
                        q.wait_ge(s["wld"], wd_p.full())
                    q.wait_ge(s["ve"], ve_h(si, NKH - 1))
                    if j >= 2 or si > 0:
                        jj, ss = (j - 2, si) if j >= 2 else (6 + j, si - 1)
                        q.wait_ge(s["ve"], ve_res(ss, jj))
                    for k in range(NKH):
                        ins = q.matmul(ps_o[j % 2][:],
                                       hsb[k].ap()[:, sub * 128:(sub + 1) * 128],
                                       wdT[k].ap()[:, dn * 512:(dn + 1) * 512],
                                       start=(k == 0), stop=(k == NKH - 1))
                        if k == NKH - 1:
                            ins.then_inc(s["pe"], 1)

        blk.gpsimd(gpsimd_f)
        blk.scalar(scalarq_f)
        blk.sync(sync_f)
        blk.vector(vector_f)
        blk.tensor(tensor_f)

    return nc


def host_ident():
    return np.eye(128).astype(BF).view(np.uint16)


def ffn0_ref_np(x):
    raise NotImplementedError


I16 = mybir.dt.int16
I1H = I1 // 2
NKI = I1H // 128
M05 = float(3 * 2 ** 21)


class QuantEmitter:

    def __init__(self, p: Prog, maxw):
        nb = maxw // 16
        self.m = p.sb("qt_m", [128, nb], F32)
        self.sc6 = p.sb("qt_sc6", [128, nb], F32)
        self.rinv = p.sb("qt_rinv", [128, nb], F32)
        self.sb_ = p.sb("qt_sb", [128, nb], BF16)
        self.t = p.sb("qt_t", [128, maxw], F32)
        self.r05 = p.sb("qt_r05", [128, maxw], BF16)
        self.vka = p.sb("qt_vka", [128, maxw], I16)
        self.vkb = p.sb("qt_vkb", [128, maxw], I16)
        self.A = p.sb("qt_A", [128, maxw], BF16)
        self.B = p.sb("qt_B", [128, maxw], BF16)
        self.u = p.sb("qt_u", [128, maxw], BF16)
        self.qq = p.sb("qt_qq", [128, maxw], BF16)

    def emit(self, q, vinc, vwait, src_ap, dst_ap, W, vcur=None):
        nb = W // 16
        if vcur is not None:
            vwait(vcur())
        ev = {}
        ev["m"] = vinc(q.tensor_reduce(self.m.ap()[:, 0:nb],
                                       src_ap.rearrange("p (a b) -> p a b", b=16),
                                       AX.X, ALU.max, apply_absolute_value=True))
        vwait(ev["m"])
        ev["sc6"] = vinc(q.tensor_scalar(self.sc6.ap()[:, 0:nb], self.m.ap()[:, 0:nb],
                                         1.0 / 6.0, 1e-12, ALU.mult, ALU.max))
        vwait(ev["sc6"])
        ev["rinv"] = vinc(q.reciprocal(self.rinv.ap()[:, 0:nb], self.sc6.ap()[:, 0:nb]))
        ev["sb"] = vinc(q.tensor_scalar(self.sb_.ap()[:, 0:nb], self.sc6.ap()[:, 0:nb],
                                        1.0, None, ALU.mult))
        vwait(ev["rinv"])
        ev["t"] = vinc(q.tensor_tensor(
            self.t.ap()[:, 0:W].rearrange("p (a b) -> p a b", b=16),
            src_ap.rearrange("p (a b) -> p a b", b=16),
            self.rinv.ap()[:, 0:nb].unsqueeze(2).broadcast_to([128, nb, 16]),
            ALU.mult))
        vwait(ev["t"])
        ev["r05"] = vinc(q.tensor_scalar(self.r05.ap()[:, 0:W], self.t.ap()[:, 0:W],
                                         M05, -M05, ALU.add, ALU.add))
        ev["vka"] = vinc(q.tensor_scalar(self.vka.ap()[:, 0:W],
                                         self.t.ap()[:, 0:W].bitcast(I16)[:, 1::2],
                                         0x20, None, ALU.add))
        vwait(ev["vka"])
        ev["vkb"] = vinc(q.tensor_scalar(self.vkb.ap()[:, 0:W], self.vka.ap()[:, 0:W],
                                         -64, None, ALU.bitwise_and))
        vwait(ev["r05"])
        ev["A"] = vinc(q.tensor_scalar(self.A.ap()[:, 0:W], self.r05.ap()[:, 0:W],
                                       -2.0, 2.0, ALU.max, ALU.min))
        vwait(ev["vkb"])
        ev["B"] = vinc(q.tensor_scalar(self.B.ap()[:, 0:W],
                                       self.vkb.ap()[:, 0:W].bitcast(BF16),
                                       -2.0, 2.0, ALU.max, ALU.min))
        vwait(ev["B"])
        ev["u"] = vinc(q.tensor_tensor(self.u.ap()[:, 0:W],
                                       self.vkb.ap()[:, 0:W].bitcast(BF16),
                                       self.B.ap()[:, 0:W], ALU.subtract))
        vwait(max(ev["A"], ev["u"]))
        ev["qq"] = vinc(q.tensor_tensor(self.qq.ap()[:, 0:W], self.A.ap()[:, 0:W],
                                        self.u.ap()[:, 0:W], ALU.add))
        vwait(max(ev["qq"], ev["sb"]))
        ins = q.tensor_tensor(
            dst_ap.rearrange("p (a b) -> p a b", b=16),
            self.qq.ap()[:, 0:W].rearrange("p (a b) -> p a b", b=16),
            self.sb_.ap()[:, 0:nb].unsqueeze(2).broadcast_to([128, nb, 16]),
            ALU.mult)
        c = vinc(ins)
        vwait(c)
        return ins, c



class FWPrep:

    def __init__(self, p, fwprep):
        wg_raw, wu_raw, wd_raw, wgq_sp, wuq_sp, wdq_sp = fwprep
        FQW = 704
        self.fwn = [p.sb(f"fwn{b}", [128, FQW], F32) for b in range(2)]
        self.fwqn = [p.sb(f"fwqn{b}", [128, FQW], BF16) for b in range(2)]
        self.qe = QuantEmitter(p, FQW)
        self.sl = [p.sem("fwl0"), p.sem("fwl1")]
        self.sq = p.sem("fwq")
        self.ss = p.sem("fws")
        pieces = []
        for raw, sp in ((wg_raw, wgq_sp), (wu_raw, wuq_sp)):
            for k in range(I1 // 128):
                for (c0, w) in ((0, 704), (704, 320)):
                    pieces.append((raw[k * 128:(k + 1) * 128, c0:c0 + w],
                                   sp[k * 128:(k + 1) * 128, c0:c0 + w], w))
        for k in range(D // 128):
            for j4 in range(4):
                c0 = j4 * 704
                pieces.append((wd_raw[k * 128:(k + 1) * 128, c0:c0 + 704],
                               wdq_sp[k * 128:(k + 1) * 128, c0:c0 + 704], 704))
        self.pieces = pieces
        self.NFP = len(pieces)
        self.vg = [0]

    def run_gp(self, q):
        for g in range(self.NFP):
            if g >= 2:
                q.wait_ge(self.sq, g - 1)
            src, dst, w = self.pieces[g]
            q.dma_start(out=self.fwn[g % 2].ap()[:, 0:w],
                        in_=src).then_inc(self.sl[g % 2], 16)
            if g >= 2:
                ps_, pd_, pw_ = self.pieces[g - 2]
                q.dma_start(out=pd_,
                            in_=self.fwqn[(g - 2) % 2].ap()[:, 0:pw_]
                            ).then_inc(self.ss, 16)
        q.wait_ge(self.sq, self.NFP)
        for g in (self.NFP - 2, self.NFP - 1):
            src, dst, w = self.pieces[g]
            q.dma_start(out=dst,
                        in_=self.fwqn[g % 2].ap()[:, 0:w]).then_inc(self.ss, 16)
        q.wait_ge(self.ss, 16 * self.NFP)

    def pump_v(self, q, vinc, vwait):
        if self.vg[0] >= self.NFP:
            return
        g = self.vg[0]
        q.wait_ge(self.sl[g % 2], 16 * (g // 2 + 1))
        if g >= 2:
            q.wait_ge(self.ss, 16 * (g - 1))
        src, dst, w = self.pieces[g]
        self.qe.emit(q, vinc, vwait, self.fwn[g % 2].ap()[:, 0:w],
                     self.fwqn[g % 2].ap()[:, 0:w], w, vcur=None)
        q.sem_inc(self.sq, 1)
        self.vg[0] += 1

    def drain_v(self, q, vinc, vwait):
        while self.vg[0] < self.NFP:
            self.pump_v(q, vinc, vwait)


def build_ffn1(p: Prog, half=None, io=None):
    nc = p.nc
    if io is None:
        xacc = nc.dram_tensor("xacc", [T, D], F32, kind="ExternalInput").ap()
        wgq_d = nc.dram_tensor("wgq", [I1, D], BF16, kind="ExternalInput").ap()
        wuq_d = nc.dram_tensor("wuq", [I1, D], BF16, kind="ExternalInput").ap()
        wdq_d = nc.dram_tensor("wdq", [D, I1], BF16, kind="ExternalInput").ap()
        ident_d = nc.dram_tensor("ident", [128, 128], BF16, kind="ExternalInput").ap()
        xout = nc.dram_tensor("xout", [T, D], F32, kind="ExternalOutput").ap()
    else:
        xacc, wgq_d, wuq_d, wdq_d, ident_d, xout = io

    ND = D // 128
    NKF = I1 // 128
    ITS = [(0, 512), (512, 512), (1024, 384), (1408, 512), (1920, 512), (2432, 384)]
    NIT = len(ITS)
    I1H2 = I1 // 2

    wgqT = [p.sb(f"wgqT{c}", [128, I1], BF16) for c in range(ND)]
    wuqT = [p.sb(f"wuqT{c}", [128, I1], BF16) for c in range(ND)]
    wdqT = [p.sb(f"wdqT{k}", [128, D], BF16) for k in range(NKF)]

    QW = 704
    qe = QuantEmitter(p, QW)

    def qemit(q, vinc, vwait, src_ap, dst_ap, cols):
        c0 = 0
        while c0 < cols:
            w = min(QW, cols - c0)
            qe.emit(q, vinc, vwait, src_ap[:, c0:c0 + w],
                    dst_ap[:, c0:c0 + w], w, vcur=None)
            c0 += w

    xs = [p.sb(f"xs{b}", [128, D], F32) for b in range(2)]
    xnf = p.sb("xnf", [128, D], F32)
    xq = [p.sb(f"xq{b}", [128, D], BF16) for b in range(2)]
    xqT = [p.sb(f"xqT{c}", [128, 256], BF16) for c in range(ND)]
    sgf = p.sb("sgf", [128, 512], F32)
    hf = p.sb("hf", [128, I1H2], F32)
    hq = [p.sb("hq0", [128, I1], BF16)] * 2
    hqT = [[p.sb(f"hqT{b}_{k}", [128, 128], BF16) for k in range(NKF)]
           for b in range(2)]
    xr = [p.sb(f"xr{b}", [128, 512], F32) for b in range(2)]
    xo = [p.sb(f"xo{b}", [128, 512], F32) for b in range(2)]
    ident = p.sb("identt", [128, 128], BF16)
    stt = {nm: [p.sb(f"st_{nm}{i}", [128, 1], F32) for i in range(4)]
           for nm in ["s1", "s2", "mu", "v1", "m2", "var", "ln", "rs", "nm"]}

    ps_tp = [p.ps(f"ps_tp{b}", [128, 128], BF16) for b in range(2)]
    ps_g = [p.ps(f"ps_g{b}", [128, 512], F32) for b in range(2)]
    ps_u = [p.ps(f"ps_u{b}", [128, 512], F32) for b in range(2)]
    ps_o = [p.ps(f"ps_o{b}", [128, 512], F32) for b in range(2)]

    s = {nm: p.sem(nm) for nm in [
        "tw", "li",
        "ldp0", "ldp1", "lrp0", "lrp1",
        "ve", "vv", "sc", "pe", "stp0", "stp1", "vs", "ss"]}

    WEV = 0
    NTW = 16 * (2 * ND + NKF)

    SUBV = 1 + 8 + NIT + 2 + NKF + 2
    VE = 4 * SUBV

    def ve_qx(si, i):
        return WEV + VE * si + SUBV * i + 1

    def ve_evx(si, i, c):
        return WEV + VE * si + SUBV * i + 1 + c + 1

    def ve_hf(si, i, it):
        return WEV + VE * si + SUBV * i + 9 + it + (1 if it < 3 else 2)

    def ve_qh(si, i, hh):
        return WEV + VE * si + SUBV * i + 13 + 4 * hh

    def ve_evh(si, i, k):
        return WEV + VE * si + SUBV * i + 11 + NIT + k + 1

    def ve_res(si, i, dn):
        return WEV + VE * si + SUBV * i + 11 + NIT + NKF + dn + 1

    SUBS = 2 + NIT
    SC = 4 * SUBS

    def sc_sq(si, i):
        return SC * si + SUBS * i + 1

    def sc_rs(si, i):
        return SC * si + SUBS * i + 2

    def sc_si(si, i, it):
        return SC * si + SUBS * i + 2 + it + 1

    SUBP = 8 + 2 * NIT + NKF + 2
    PE = 4 * SUBP

    def pe_tpx(si, i, c):
        return WEV + PE * si + SUBP * i + c + 1

    def pe_g(si, i, it):
        return WEV + PE * si + SUBP * i + 8 + 2 * it + 1

    def pe_u(si, i, it):
        return WEV + PE * si + SUBP * i + 8 + 2 * it + 2

    def pe_tph(si, i, k):
        return WEV + PE * si + SUBP * i + 8 + 2 * NIT + k + 1

    def pe_o(si, i, dn):
        return WEV + PE * si + SUBP * i + 8 + 2 * NIT + NKF + dn + 1

    tp_evac_ve = []
    for si in range(NT512):
        for i in range(4):
            for c in range(ND):
                tp_evac_ve.append(ve_evx(si, i, c))
            for k in range(NKF):
                tp_evac_ve.append(ve_evh(si, i, k))

    with p.block() as blk:

        def sync_f(q):
            q.dma_start(out=xs[0][:], in_=xacc[0:128, :]).then_inc(s["ldp0"], 16)
            q.dma_start(out=xs[1][:], in_=xacc[128:256, :]).then_inc(s["ldp1"], 16)
            for dn in range(2):
                q.dma_start(out=xr[dn][:],
                            in_=xacc[0:128, dn * 512:(dn + 1) * 512],
                            ).then_inc(s[f"lrp{dn}"], 16)
            q.dma_start(out=ident[:], in_=ident_d).then_inc(s["li"], 16)
            for c in range(ND):
                q.dma_start(out=wgqT[c][:], in_=wgq_d[:, c * 128:(c + 1) * 128],
                            transpose=True).then_inc(s["tw"], 16)
            for c in range(ND):
                q.dma_start(out=wuqT[c][:], in_=wuq_d[:, c * 128:(c + 1) * 128],
                            transpose=True).then_inc(s["tw"], 16)
            for k in range(NKF):
                q.dma_start(out=wdqT[k][:], in_=wdq_d[:, k * 128:(k + 1) * 128],
                            transpose=True).then_inc(s["tw"], 16)
            for si in range(NT512):
                for i in range(4):
                    g = si * 4 + i
                    if g >= 2:
                        q.wait_ge(s["ve"], ve_qx((g - 2) // 4, (g - 2) % 4))
                        q.dma_start(out=xs[g % 2][:],
                                    in_=xacc[g * 128:(g + 1) * 128, :],
                                    ).then_inc(s[f"ldp{g % 2}"], 16)
                    for dn in range(2):
                        gj = g * 2 + dn
                        if gj >= 2:
                            pj = gj - 2
                            q.wait_ge(s["ve"], ve_res(pj // 8, (pj % 8) // 2, pj % 2))
                            q.dma_start(
                                out=xr[gj % 2][:],
                                in_=xacc[g * 128:(g + 1) * 128,
                                         dn * 512:(dn + 1) * 512],
                            ).then_inc(s[f"lrp{gj % 2}"], 16)
                            q.dma_start(
                                out=xout[(pj // 2) * 128:(pj // 2 + 1) * 128,
                                         (pj % 2) * 512:(pj % 2 + 1) * 512],
                                in_=xo[pj % 2][:]).then_inc(s[f"stp{pj % 2}"], 16)
            for gj in range(NT128 * 2 - 2, NT128 * 2):
                q.wait_ge(s["ve"], ve_res(gj // 8, (gj % 8) // 2, gj % 2))
                q.dma_start(
                    out=xout[(gj // 2) * 128:(gj // 2 + 1) * 128,
                             (gj % 2) * 512:(gj % 2 + 1) * 512],
                    in_=xo[gj % 2][:]).then_inc(s[f"stp{gj % 2}"], 16)
            q.wait_ge(s["stp0"], 16 * NT128)
            q.wait_ge(s["stp1"], 16 * NT128)

        def vector_f(q):
            vsn = [0]

            def vinc(ins):
                ins.then_inc(s["vs"], 1)
                vsn[0] += 1
                return vsn[0]

            def vwait(n):
                if n > 0:
                    q.wait_ge(s["vs"], n)

            for si in range(NT512):
                for i in range(4):
                    g = si * 4 + i
                    q.wait_ge(s[f"ldp{g % 2}"], 16 * (g // 2 + 1))
                    a = vinc(q.reduce_sum(stt["s1"][i][:], xs[g % 2][:], AX.X))
                    vwait(a)
                    b = vinc(q.tensor_scalar(stt["mu"][i][:], stt["s1"][i][:],
                                             1.0 / D, None, ALU.mult))
                    q.wait_ge(s["sc"], sc_sq(si, i))
                    cviv = vinc(q.tensor_scalar(stt["v1"][i][:], stt["s2"][i][:],
                                                1.0 / D, 1e-5, ALU.mult, ALU.add))
                    vwait(b)
                    d = vinc(q.scalar_tensor_tensor(stt["m2"][i][:], stt["mu"][i][:],
                                                    -1.0, stt["mu"][i][:],
                                                    ALU.mult, ALU.mult))
                    vwait(d)
                    e = vinc(q.tensor_tensor(stt["var"][i][:], stt["v1"][i][:],
                                             stt["m2"][i][:], ALU.add))
                    vwait(e)
                    q.sem_inc(s["vv"], 1)
                    q.wait_ge(s["sc"], sc_rs(si, i))
                    f = vinc(q.scalar_tensor_tensor(stt["nm"][i][:], stt["mu"][i][:],
                                                    -1.0, stt["rs"][i][:],
                                                    ALU.mult, ALU.mult))
                    vwait(f)
                    gg = vinc(q.tensor_scalar(
                        xnf[:], xs[g % 2][:], stt["rs"][i].ap()[:, 0:1],
                        stt["nm"][i].ap()[:, 0:1], ALU.mult, ALU.add))
                    vwait(gg)
                    if g >= 2:
                        q.wait_ge(s["pe"], pe_tpx((g - 2) // 4, (g - 2) % 4, 7))
                    qemit(q, vinc, vwait, xnf.ap(), xq[g % 2].ap(), D)
                    q.sem_inc(s["ve"], 1)
                    for c in range(ND):
                        q.wait_ge(s["pe"], pe_tpx(si, i, c))
                        if c == 0 and g >= 2:
                            q.wait_ge(s["pe"], pe_u((g - 2) // 4, (g - 2) % 4, NIT - 1))
                        tot = g * (8 + NKF) + c
                        q.tensor_copy(xqT[c].ap()[:, (i % 2) * 128:(i % 2 + 1) * 128],
                                      ps_tp[tot % 2][:]).then_inc(s["ve"], 1)
                    for hh2 in range(2):
                        for it in range(3 * hh2, 3 * hh2 + 3):
                            c0, w = ITS[it]
                            q.wait_ge(s["sc"], sc_si(si, i, it))
                            pit = g * NIT + it
                            q.wait_ge(s["pe"], pe_u(si, i, it))
                            vwait(vsn[0])
                            hv = vinc(q.tensor_tensor(
                                hf.ap()[:, c0 - hh2 * I1H2:c0 - hh2 * I1H2 + w],
                                sgf.ap()[:, 0:w],
                                ps_u[pit % 2][:, 0:w], ALU.mult))
                            vwait(hv)
                            q.sem_inc(s["ve"], 1)
                        if hh2 == 0 and g >= 1:
                            q.wait_ge(s["pe"], pe_tph((g - 1) // 4, (g - 1) % 4, NKF - 1))
                        qemit(q, vinc, vwait, hf.ap()[:, 0:I1H2],
                              hq[0].ap()[:, hh2 * I1H2:(hh2 + 1) * I1H2], I1H2)
                        q.sem_inc(s["ve"], 1)
                    for k in range(NKF):
                        q.wait_ge(s["pe"], pe_tph(si, i, k))
                        if k == 0 and g >= 2:
                            q.wait_ge(s["pe"], pe_o((g - 2) // 4, (g - 2) % 4, 1))
                        tot = g * (8 + NKF) + 8 + k
                        q.tensor_copy(hqT[g % 2][k][:],
                                      ps_tp[tot % 2][:]).then_inc(s["ve"], 1)
                    for dn in range(2):
                        gj = g * 2 + dn
                        q.wait_ge(s["pe"], pe_o(si, i, dn))
                        q.wait_ge(s[f"lrp{gj % 2}"], 16 * (gj // 2 + 1))
                        if gj >= 2:
                            q.wait_ge(s[f"stp{gj % 2}"], 16 * ((gj - 2) // 2 + 1))
                        q.tensor_tensor(xo[gj % 2][:], xr[gj % 2][:],
                                        ps_o[dn % 2][:], ALU.add).then_inc(s["ve"], 1)

        def scalar_f(q):
            ssn = [0]

            def sinc(ins):
                ins.then_inc(s["ss"], 1)
                ssn[0] += 1
                return ssn[0]

            for si in range(NT512):
                for i in range(4):
                    g = si * 4 + i
                    q.wait_ge(s[f"ldp{g % 2}"], 16 * (g // 2 + 1))
                    prior = SC * si + SUBS * i
                    if prior > 0:
                        q.wait_ge(s["sc"], prior)
                    q.activation(xnf[:], xs[g % 2][:], AF.Square,
                                 accum_out=stt["s2"][i][:]).then_inc(s["sc"], 1)
                    q.wait_ge(s["vv"], g + 1)
                    n0 = sinc(q.activation(stt["ln"][i][:], stt["var"][i][:], AF.Ln))
                    q.wait_ge(s["ss"], n0)
                    q.activation(stt["rs"][i][:], stt["ln"][i][:], AF.Exp,
                                 scale=-0.5).then_inc(s["sc"], 1)
                    for it, (c0, w) in enumerate(ITS):
                        q.wait_ge(s["pe"], pe_g(si, i, it))
                        pit = g * NIT + it
                        if pit >= 1:
                            pp = pit - 1
                            q.wait_ge(s["ve"], ve_hf(pp // (4 * NIT),
                                                     (pp % (4 * NIT)) // NIT,
                                                     pp % NIT))
                        q.activation(sgf.ap()[:, 0:w], ps_g[pit % 2][:, 0:w],
                                     AF.Silu).then_inc(s["sc"], 1)

        def tensor_f(q):
            q.wait_ge(s["li"], 16)
            q.wait_ge(s["tw"], 16 * 2 * ND)
            for si in range(NT512):
                for i in range(4):
                    g = si * 4 + i
                    q.wait_ge(s["ve"], ve_qx(si, i))
                    for c in range(ND):
                        tot = g * (8 + NKF) + c
                        if tot >= 2:
                            q.wait_ge(s["ve"], tp_evac_ve[tot - 2])
                        q.transpose(ps_tp[tot % 2][:],
                                    xq[g % 2].ap()[:, c * 128:(c + 1) * 128],
                                    ident[:]).then_inc(s["pe"], 1)
                    for it, (c0, w) in enumerate(ITS):
                        pit = g * NIT + it
                        if pit >= 2:
                            pp = pit - 2
                            q.wait_ge(s["sc"], sc_si(pp // (4 * NIT),
                                                     (pp % (4 * NIT)) // NIT,
                                                     pp % NIT))
                        q.wait_ge(s["ve"], ve_evx(si, i, 7))
                        for c in range(ND):
                            ins = q.matmul(ps_g[pit % 2][:, 0:w],
                                           xqT[c].ap()[:, (i % 2) * 128:(i % 2 + 1) * 128],
                                           wgqT[c].ap()[:, c0:c0 + w],
                                           start=(c == 0), stop=(c == ND - 1))
                            if c == ND - 1:
                                ins.then_inc(s["pe"], 1)
                        if pit >= 2:
                            pp = pit - 2
                            q.wait_ge(s["ve"], ve_hf(pp // (4 * NIT),
                                                     (pp % (4 * NIT)) // NIT,
                                                     pp % NIT))
                        for c in range(ND):
                            ins = q.matmul(ps_u[pit % 2][:, 0:w],
                                           xqT[c].ap()[:, (i % 2) * 128:(i % 2 + 1) * 128],
                                           wuqT[c].ap()[:, c0:c0 + w],
                                           start=(c == 0), stop=(c == ND - 1))
                            if c == ND - 1:
                                ins.then_inc(s["pe"], 1)
                    for k in range(NKF):
                        if k < I1H2 // 128:
                            q.wait_ge(s["ve"], ve_qh(si, i, 0))
                        else:
                            q.wait_ge(s["ve"], ve_qh(si, i, 1))
                        tot = g * (8 + NKF) + 8 + k
                        q.wait_ge(s["ve"], tp_evac_ve[tot - 2])
                        q.transpose(ps_tp[tot % 2][:],
                                    hq[0].ap()[:, k * 128:(k + 1) * 128],
                                    ident[:]).then_inc(s["pe"], 1)
                    for dn in range(2):
                        if g == 0 and dn == 0:
                            q.wait_ge(s["tw"], NTW)
                        q.wait_ge(s["ve"], ve_evh(si, i, NKF - 1))
                        gj = g * 2 + dn
                        if gj >= 2:
                            pj = gj - 2
                            q.wait_ge(s["ve"], ve_res(pj // 8, (pj % 8) // 2, pj % 2))
                        for k in range(NKF):
                            ins = q.matmul(ps_o[dn % 2][:],
                                           hqT[g % 2][k][:],
                                           wdqT[k].ap()[:, dn * 512:(dn + 1) * 512],
                                           start=(k == 0), stop=(k == NKF - 1))
                            if k == NKF - 1:
                                ins.then_inc(s["pe"], 1)

        blk.sync(sync_f)
        blk.vector(vector_f)
        blk.scalar(scalar_f)
        blk.tensor(tensor_f)

    return nc


def build_attn(p: Prog, io=None):
    nc = p.nc
    if io is None:
        xin = nc.dram_tensor("xin", [T, D], F32, kind="ExternalInput").ap()
        wq_d = nc.dram_tensor("wq", [D, D], F32, kind="ExternalInput").ap()
        wk_d = nc.dram_tensor("wk", [D, D], F32, kind="ExternalInput").ap()
        wv_d = nc.dram_tensor("wv", [D, D], F32, kind="ExternalInput").ap()
        wo_d = nc.dram_tensor("wo", [D, D], F32, kind="ExternalInput").ap()
        bcol_d = nc.dram_tensor("bcol", [128, 2 * 8], F32, kind="ExternalInput").ap()
        bvb_d = nc.dram_tensor("bvb", [128, D], F32, kind="ExternalInput").ap()
        bob_d = nc.dram_tensor("bob", [128, D], F32, kind="ExternalInput").ap()
        cos_d = nc.dram_tensor("cos_t", [128, T], BF16, kind="ExternalInput").ap()
        sin_d = nc.dram_tensor("sin_t", [128, T], BF16, kind="ExternalInput").ap()
        msk_d = nc.dram_tensor("msk", [128, 4 * 512], BF16, kind="ExternalInput").ap()
        ident_d = nc.dram_tensor("ident", [128, 128], BF16, kind="ExternalInput").ap()
        negC_d = nc.dram_tensor("negC", [128, 1], F32, kind="ExternalInput").ap()
        ones_d = nc.dram_tensor("ones_r", [1, 64], F32, kind="ExternalInput").ap()
        xout = nc.dram_tensor("xout", [T, D], F32, kind="ExternalOutput").ap()
    else:
        (xin, wq_d, wk_d, wv_d, wo_d, bcol_d, bvb_d, bob_d,
         cos_d, sin_d, msk_d, ident_d, negC_d, ones_d, xout) = io

    ND = D // 128
    NQT = 8
    q_scr = nc.dram_tensor("q_scr", [D, T], BF16, kind="Internal").ap()
    k_scr = nc.dram_tensor("k_scr", [D, T], BF16, kind="Internal").ap()
    v_scr = nc.dram_tensor("v_scr", [T, NH * 65], BF16, kind="Internal").ap()

    wqT = [p.sb(f"wqT{c}", [128, D], BF16) for c in range(ND)]
    wkT = [p.sb(f"wkT{c}", [128, D], BF16) for c in range(ND)]
    wvT = [p.sb(f"wvT{c}", [128, D], BF16) for c in range(ND)]
    bcol = p.sb("bcol_s", [128, 2 * 8], F32)
    bvb = p.sb("bvb_s", [128, D], F32)
    bob = p.sb("bob_s", [128, D], F32)
    cos_t = p.sb("cos_sb", [128, T], BF16)
    sin_t = p.sb("sin_sb", [128, T], BF16)
    xs = [p.sb(f"xs{b}", [128, D], F32) for b in range(2)]
    xn = p.sb("xn", [128, D], BF16)
    xnT = [p.sb(f"xnT{c}", [128, 512], BF16) for c in range(ND)]
    qb = p.sb("qb", [128, 512], F32)
    mm1 = p.sb("mm1", [128, 512], BF16)
    mm2 = p.sb("mm2", [128, 512], BF16)
    qro = [p.sb(f"qro{b}", [128, 512], BF16) for b in range(4)]
    vst = [p.sb(f"vst{b}", [128, 520], BF16) for b in range(2)]
    stt = {nm: [p.sb(f"st_{nm}{i}", [128, 1], F32) for i in range(4)]
           for nm in ["s1", "s2", "mu", "v1", "m2", "var", "ln", "rs", "nm"]}
    scr = p.sb("scr", [128, D], BF16)
    ident = p.sb("identt", [128, 128], BF16)
    negC = p.sb("negC_s", [128, 1], F32)
    onesr = p.sb("onesr", [1, 64], F32)

    qtl = [[p.sb(f"qtl{b}_{c}", [128, 512], BF16) for c in range(ND)]
           for b in range(2)]
    ktl = [p.sb(f"ktl{b}", [128, 512], BF16) for b in range(2)]
    vtl = [p.sb(f"vtl{b}", [128, 130], BF16) for b in range(4)]
    pt = p.sb("pt", [128, 2048], BF16)
    pm = [p.sb(f"pm{b}", [128, 512], BF16) for b in range(4)]
    msk = p.sb("msk_sb", [128, 4 * 512], BF16)
    yall = [p.sb(f"yall{hp}", [128, 512], BF16) for hp in range(ND)]
    dden = [p.sb(f"dden{h}", [1, 512], F32) for h in range(2)]
    lnscr = p.sb("lnscr", [64, 512], F32)
    bcb = p.sb("bcb", [64, 1024], F32)
    woT = [p.sb(f"woT{c}", [128, D], BF16) for c in range(ND)]
    xr = [p.sb(f"xr{b}", [128, 512], F32) for b in range(2)]
    xo = [p.sb(f"xo{b}", [128, 512], F32) for b in range(2)]

    psA = p.ps("psA", [128, 2048], F32)
    psB = p.ps("psB", [128, 2048], F32)
    psB_bf = psB.ap().bitcast(BF16)

    def tp_view(tot):
        b = tot % 2
        return psB_bf[:, b * 1024:b * 1024 + 128]

    def qk_view(aidx):
        b = aidx % 2
        return psA.ap()[:, b * 512:(b + 1) * 512]

    s_view = psA.ap()
    y_view = [psB.ap()[0:65, 0:512], psB.ap()[0:65, 512:1024]]
    bc_view = psB.ap()[0:64, 1024:1536]
    o_view = psB.ap()[:, 1536:2048]

    s = {nm: p.sem(nm) for nm in [
        "wl", "li",
        "ldp0", "ldp1", "lq", "lk0", "lk1", "lk2", "lk3",
        "lv0", "lv1", "lv2", "lv3", "lrp0", "lrp1",
        "ve", "vv", "sc", "pe",
        "sq0", "sq1", "sk0", "sk1", "sv0", "sv1", "stp0", "stp1", "vs", "ss"]}

    VEA = 60

    def ve_ln(si, i):
        return VEA * si + 9 * i + 1

    def ve_ev(si, i, c):
        return VEA * si + 9 * i + 1 + c + 1

    def ve_qr(si, oc):
        return VEA * si + 36 + 2 * oc + 1

    def ve_kr(si, oc):
        return VEA * si + 36 + 2 * oc + 2

    def ve_vv(si, j):
        return VEA * si + 52 + j + 1

    VEA_TOT = VEA * NT512

    SCA = 8

    def sc_sq(si, i):
        return SCA * si + 2 * i + 1

    def sc_rs(si, i):
        return SCA * si + 2 * i + 2

    SCA_TOT = SCA * NT512

    PEA = 56

    def pe_tp(si, j):
        return PEA * si + j + 1

    def pe_q(si, oc):
        return PEA * si + 32 + 2 * oc + 1

    def pe_k(si, oc):
        return PEA * si + 32 + 2 * oc + 2

    def pe_v(si, j):
        return PEA * si + 48 + j + 1

    PEA_TOT = PEA * NT512

    def aidx_q(si, oc):
        return si * 24 + 2 * oc

    def aidx_k(si, oc):
        return si * 24 + 2 * oc + 1

    def aidx_v(si, j):
        return si * 24 + 16 + j

    def a_consumer_ve(aidx):
        si, r = aidx // 24, aidx % 24
        if r < 16:
            return ve_qr(si, r // 2) if r % 2 == 0 else ve_kr(si, r // 2)
        return ve_vv(si, r - 16)

    def NKB(qt):
        return 4 * (qt + 1)

    def NG(qt):
        return 2 * (qt + 1)

    def diag_idx(qt, kb):
        off = kb * 128 - qt * 512
        if off < 0:
            return None
        return off // 128

    peb, scb, veb = {}, {}, {}
    pe_c, sc_c, ve_c = [PEA_TOT], [SCA_TOT], [VEA_TOT]
    kb_list = []
    for qt in range(NQT):
        for hp in range(ND):
            for kb in range(NKB(qt)):
                kb_list.append((qt, hp, kb))
                pe_c[0] += 1
                peb[("s", qt, hp, kb)] = pe_c[0]
                sc_c[0] += 1
                scb[("e", qt, hp, kb)] = sc_c[0]
                nm = 2 if diag_idx(qt, kb) is not None else 0
                ve_c[0] += nm
                veb[("m", qt, hp, kb)] = ve_c[0]
            sc_c[0] += 1
            scb[("rcA", qt, hp)] = sc_c[0]
            sc_c[0] += 1
            scb[("rcB", qt, hp)] = sc_c[0]
            pe_c[0] += 1
            peb[("yA", qt, hp)] = pe_c[0]
            pe_c[0] += 1
            peb[("yB", qt, hp)] = pe_c[0]
            ve_c[0] += 1
            veb[("dA", qt, hp)] = ve_c[0]
            ve_c[0] += 1
            veb[("dB", qt, hp)] = ve_c[0]
            pe_c[0] += 1
            peb[("bcA", qt, hp)] = pe_c[0]
            ve_c[0] += 1
            veb[("bcbA", qt, hp)] = ve_c[0]
            pe_c[0] += 1
            peb[("bcB", qt, hp)] = pe_c[0]
            ve_c[0] += 1
            veb[("bcbB", qt, hp)] = ve_c[0]
            ve_c[0] += 1
            veb[("yA", qt, hp)] = ve_c[0]
            ve_c[0] += 1
            veb[("yB", qt, hp)] = ve_c[0]
        for j in range(8):
            pe_c[0] += 1
            peb[("o", qt, j)] = pe_c[0]
            ve_c[0] += 1
            veb[("bo", qt, j)] = ve_c[0]
            ve_c[0] += 1
            veb[("res", qt, j)] = ve_c[0]

    def kb_done_pe(qt, hp, kb):
        if kb + 2 < NKB(qt):
            return peb[("s", qt, hp, kb + 2)]
        return peb[("yB", qt, hp)]

    kcons = []
    for gk, (qt, hp, kb) in enumerate(kb_list):
        kcons.append(kb_done_pe(qt, hp, kb))
    kq_list = []
    kqcons = []
    for qt in range(NQT):
        for hp in range(ND):
            for kq in range(NKB(qt) // 4):
                kq_list.append((qt, hp, kq))
                last_kb = kq * 4 + 3
                kqcons.append(kb_done_pe(qt, hp, last_kb))

    NWL = 16 * (4 * ND + 8)

    with p.block() as blk:

        def gpsimd_f(q):
            for c in range(ND):
                q.dma_start(out=wqT[c][:], in_=wq_d[c * 128:(c + 1) * 128, :]).then_inc(s["wl"], 16)
            for c in range(ND):
                q.dma_start(out=wkT[c][:], in_=wk_d[c * 128:(c + 1) * 128, :]).then_inc(s["wl"], 16)
            for c in range(ND):
                q.dma_start(out=wvT[c][:], in_=wv_d[c * 128:(c + 1) * 128, :]).then_inc(s["wl"], 16)
            for c in range(ND):
                q.dma_start(out=woT[c][:], in_=wo_d[c * 128:(c + 1) * 128, :]).then_inc(s["wl"], 16)
            q.dma_start(out=bcol[:], in_=bcol_d).then_inc(s["wl"], 16)
            q.dma_start(out=bvb[:], in_=bvb_d).then_inc(s["wl"], 16)
            q.dma_start(out=bob[:], in_=bob_d).then_inc(s["wl"], 16)
            q.dma_start(out=cos_t[:], in_=cos_d).then_inc(s["wl"], 16)
            q.dma_start(out=sin_t[:], in_=sin_d).then_inc(s["wl"], 16)
            q.dma_start(out=msk[:], in_=msk_d).then_inc(s["wl"], 16)
            q.dma_start(out=negC[:], in_=negC_d).then_inc(s["wl"], 16)
            q.dma_start(out=onesr[:], in_=ones_d).then_inc(s["wl"], 16)
            q.wait_ge(s["sk0"], 16 * 32)
            q.wait_ge(s["sk1"], 16 * 32)
            q.wait_ge(s["sv0"], 16 * 32)
            q.wait_ge(s["sv1"], 16 * 32)
            gk = 0
            gkk = 0
            for qt in range(NQT):
                for hp in range(ND):
                    for kq in range(NKB(qt) // 4):
                        if gkk >= 2:
                            q.wait_ge(s["pe"], kqcons[gkk - 2])
                        q.dma_start(out=ktl[gkk % 2][:],
                                    in_=k_scr[hp * 128:(hp + 1) * 128,
                                              kq * 512:(kq + 1) * 512],
                                    ).then_inc(s[f"lk{gkk % 2}"], 16)
                        gkk += 1
                        for kb in range(kq * 4, kq * 4 + 4):
                            b4 = gk % 4
                            if gk >= 4:
                                q.wait_ge(s["pe"], kcons[gk - 4])
                            q.dma_start(out=vtl[b4][:],
                                        in_=v_scr[kb * 128:(kb + 1) * 128,
                                                  hp * 130:(hp + 1) * 130],
                                        ).then_inc(s[f"lv{b4}"], 16)
                            gk += 1

        def sync_f(q):
            q.dma_start(out=ident[:], in_=ident_d).then_inc(s["li"], 16)
            for si in range(NT512):
                for i in range(4):
                    g = si * 4 + i
                    if g >= 2:
                        q.wait_ge(s["ve"], ve_ln((g - 2) // 4, (g - 2) % 4))
                    q.dma_start(out=xs[g % 2][:],
                                in_=xin[g * 128:(g + 1) * 128, :]).then_inc(s[f"ldp{g % 2}"], 16)
                for oc in range(ND):
                    gq = si * ND + oc
                    q.wait_ge(s["ve"], ve_qr(si, oc))
                    q.dma_start(out=q_scr[oc * 128:(oc + 1) * 128, si * 512:(si + 1) * 512],
                                in_=qro[gq % 2][:]).then_inc(s[f"sq{gq % 2}"], 16)
                    q.wait_ge(s["ve"], ve_kr(si, oc))
                    q.dma_start(out=k_scr[oc * 128:(oc + 1) * 128, si * 512:(si + 1) * 512],
                                in_=qro[2 + gq % 2][:]).then_inc(s[f"sk{gq % 2}"], 16)
                for j in range(8):
                    gv = si * 8 + j
                    sub, ot = j // 2, j % 2
                    q.wait_ge(s["ve"], ve_vv(si, j))
                    q.dma_start(
                        out=v_scr[(si * 4 + sub) * 128:(si * 4 + sub + 1) * 128,
                                  ot * 520:(ot + 1) * 520],
                        in_=vst[gv % 2][:]).then_inc(s[f"sv{gv % 2}"], 16)
            for nm2 in ["sq0", "sq1", "sk0", "sk1", "sv0", "sv1"]:
                q.wait_ge(s[nm2], 16 * 32)
            for qt in range(NQT):
                for c in range(ND):
                    if qt >= 2:
                        q.wait_ge(s["ve"], veb[("res", qt - 2, 7)])
                    q.dma_start(out=qtl[qt % 2][c][:],
                                in_=q_scr[c * 128:(c + 1) * 128,
                                          qt * 512:(qt + 1) * 512]).then_inc(s["lq"], 16)
                for j in range(8):
                    gj = qt * 8 + j
                    qs, dn = j // 2, j % 2
                    if gj >= 2:
                        pj = gj - 2
                        q.wait_ge(s["ve"], veb[("res", pj // 8, pj % 8)])
                    q.dma_start(out=xr[gj % 2][:],
                                in_=xin[(qt * 4 + qs) * 128:(qt * 4 + qs + 1) * 128,
                                        dn * 512:(dn + 1) * 512],
                                ).then_inc(s[f"lrp{gj % 2}"], 16)
                    if gj >= 2:
                        pj = gj - 2
                        q.wait_ge(s["ve"], veb[("res", pj // 8, pj % 8)])
                        q.dma_start(
                            out=xout[(pj // 2) * 128:(pj // 2 + 1) * 128,
                                     (pj % 2) * 512:(pj % 2 + 1) * 512],
                            in_=xo[pj % 2][:]).then_inc(s[f"stp{pj % 2}"], 16)
            for gj in range(NQT * 8 - 2, NQT * 8):
                q.wait_ge(s["ve"], veb[("res", gj // 8, gj % 8)])
                q.dma_start(
                    out=xout[(gj // 2) * 128:(gj // 2 + 1) * 128,
                             (gj % 2) * 512:(gj % 2 + 1) * 512],
                    in_=xo[gj % 2][:]).then_inc(s[f"stp{gj % 2}"], 16)
            q.wait_ge(s["stp0"], 16 * NT128)
            q.wait_ge(s["stp1"], 16 * NT128)

        def vector_f(q):
            vsn = [0]

            def vinc(ins):
                ins.then_inc(s["vs"], 1)
                vsn[0] += 1
                return vsn[0]

            def vwait(n):
                if n > 0:
                    q.wait_ge(s["vs"], n)

            def emit_rope(psrc, bcol_col, si, dst):
                a = vinc(q.tensor_scalar(qb[:], psrc, bcol.ap()[:, bcol_col:bcol_col + 1],
                                         None, ALU.add))
                vwait(a)
                cs = cos_t.ap()[:, si * 512:(si + 1) * 512]
                sn = sin_t.ap()[:, si * 512:(si + 1) * 512]
                b = vinc(q.tensor_tensor(mm1[:], qb[:], cs, ALU.mult))
                last = b
                for h2 in range(2):
                    for hh in range(2):
                        p0 = h2 * 64 + hh * 32
                        psw = h2 * 64 + (1 - hh) * 32
                        last = vinc(q.tensor_tensor(mm2.ap()[p0:p0 + 32, :],
                                                    qb.ap()[psw:psw + 32, :],
                                                    sn[psw:psw + 32, :], ALU.mult))
                vwait(last)
                dd = vinc(q.tensor_tensor(dst, mm1[:], mm2[:], ALU.add))
                vwait(dd)

            for si in range(NT512):
                for i in range(4):
                    g = si * 4 + i
                    q.wait_ge(s[f"ldp{g % 2}"], 16 * (g // 2 + 1))
                    a = vinc(q.reduce_sum(stt["s1"][i][:], xs[g % 2][:], AX.X))
                    vwait(a)
                    b = vinc(q.tensor_scalar(stt["mu"][i][:], stt["s1"][i][:],
                                             1.0 / D, None, ALU.mult))
                    q.wait_ge(s["sc"], sc_sq(si, i))
                    cv = vinc(q.tensor_scalar(stt["v1"][i][:], stt["s2"][i][:],
                                              1.0 / D, 1e-5, ALU.mult, ALU.add))
                    vwait(b)
                    dv = vinc(q.scalar_tensor_tensor(stt["m2"][i][:], stt["mu"][i][:],
                                                     -1.0, stt["mu"][i][:],
                                                     ALU.mult, ALU.mult))
                    vwait(dv)
                    e = vinc(q.tensor_tensor(stt["var"][i][:], stt["v1"][i][:],
                                             stt["m2"][i][:], ALU.add))
                    vwait(e)
                    q.sem_inc(s["vv"], 1)
                    q.wait_ge(s["sc"], sc_rs(si, i))
                    f = vinc(q.scalar_tensor_tensor(stt["nm"][i][:], stt["mu"][i][:],
                                                    -1.0, stt["rs"][i][:],
                                                    ALU.mult, ALU.mult))
                    if g > 0:
                        pg = g - 1
                        q.wait_ge(s["pe"], pe_tp(pg // 4, (pg % 4) * 8 + 7))
                    vwait(f)
                    q.tensor_scalar(
                        xn[:], xs[g % 2][:], stt["rs"][i].ap()[:, 0:1],
                        stt["nm"][i].ap()[:, 0:1], ALU.mult, ALU.add,
                    ).then_inc(s["ve"], 1)
                    for c in range(ND):
                        jj = i * 8 + c
                        q.wait_ge(s["pe"], pe_tp(si, jj))
                        tot = si * 32 + jj
                        q.tensor_copy(xnT[c].ap()[:, i * 128:(i + 1) * 128],
                                      tp_view(tot)).then_inc(s["ve"], 1)
                for oc in range(ND):
                    gq = si * ND + oc
                    q.wait_ge(s["pe"], pe_q(si, oc))
                    if gq >= 2:
                        q.wait_ge(s[f"sq{gq % 2}"], 16 * ((gq - 2) // 2 + 1))
                    emit_rope(qk_view(aidx_q(si, oc)), oc, si, qro[gq % 2][:])
                    q.sem_inc(s["ve"], 1)
                    q.wait_ge(s["pe"], pe_k(si, oc))
                    if gq >= 2:
                        q.wait_ge(s[f"sk{gq % 2}"], 16 * ((gq - 2) // 2 + 1))
                    emit_rope(qk_view(aidx_k(si, oc)), 8 + oc, si, qro[2 + gq % 2][:])
                    q.sem_inc(s["ve"], 1)
                for j in range(8):
                    gv = si * 8 + j
                    sub, ot = j // 2, j % 2
                    q.wait_ge(s["pe"], pe_v(si, j))
                    if gv >= 2:
                        q.wait_ge(s[f"sv{gv % 2}"], 16 * ((gv - 2) // 2 + 1))
                    hv = vinc(q.tensor_tensor(
                        vst[gv % 2].ap().rearrange("p (h w) -> p h w", w=65)[:, :, 0:64],
                        qk_view(aidx_v(si, j)).rearrange("p (h w) -> p h w", w=64),
                        bvb.ap()[:, ot * 512:(ot + 1) * 512].rearrange("p (h w) -> p h w", w=64),
                        ALU.add))
                    vwait(hv)
                    q.memset(vst[gv % 2].ap().rearrange("p (h w) -> p h w", w=65)[:, :, 64:65],
                             1.0).then_inc(s["ve"], 1)
            for qt in range(NQT):
                for hp in range(ND):
                    for kb in range(NKB(qt)):
                        mi = diag_idx(qt, kb)
                        if mi is None:
                            continue
                        q.wait_ge(s["sc"], scb[("e", qt, hp, kb)])
                        for h in range(2):
                            q.tensor_tensor(
                                pm[(kb % 2) * 2 + h][:],
                                pt.ap()[:, ((kb % 2) * 2 + h) * 512:((kb % 2) * 2 + h + 1) * 512],
                                msk.ap()[:, mi * 512:(mi + 1) * 512],
                                ALU.mult).then_inc(s["ve"], 1)
                    q.wait_ge(s["pe"], peb[("yB", qt, hp)])
                    dv = vinc(q.tensor_copy(dden[0][:], y_view[0][64:65, :]))
                    vwait(dv)
                    q.sem_inc(s["ve"], 1)
                    dv = vinc(q.tensor_copy(dden[1][:], y_view[1][64:65, :]))
                    vwait(dv)
                    q.sem_inc(s["ve"], 1)
                    q.wait_ge(s["sc"], scb[("rcA", qt, hp)])
                    q.sem_inc(s["ve"], 1)
                    q.wait_ge(s["sc"], scb[("rcB", qt, hp)])
                    q.sem_inc(s["ve"], 1)
                    if qt >= 1:
                        q.wait_ge(s["pe"], peb[("o", qt - 1, 7)])
                    yv = vinc(q.tensor_tensor(yall[hp].ap()[0:64, :], y_view[0][0:64, :],
                                              bcb.ap()[:, 0:512], ALU.mult))
                    vwait(yv)
                    q.sem_inc(s["ve"], 1)
                    yv = vinc(q.tensor_tensor(yall[hp].ap()[64:128, :], y_view[1][0:64, :],
                                              bcb.ap()[:, 512:1024], ALU.mult))
                    vwait(yv)
                    q.sem_inc(s["ve"], 1)
                for j in range(8):
                    gj = qt * 8 + j
                    qs, dn = j // 2, j % 2
                    q.wait_ge(s["pe"], peb[("o", qt, j)])
                    if gj >= 2:
                        q.wait_ge(s[f"stp{gj % 2}"], 16 * ((gj - 2) // 2 + 1))
                    ov = vinc(q.tensor_tensor(xo[gj % 2][:], o_view,
                                              bob.ap()[:, dn * 512:(dn + 1) * 512],
                                              ALU.add))
                    vwait(ov)
                    q.sem_inc(s["ve"], 1)
                    q.wait_ge(s[f"lrp{gj % 2}"], 16 * (gj // 2 + 1))
                    o2 = vinc(q.tensor_tensor(xo[gj % 2][:], xo[gj % 2][:],
                                              xr[gj % 2][:], ALU.add))
                    vwait(o2)
                    q.sem_inc(s["ve"], 1)

        def scalar_f(q):
            ssn = [0]

            def sinc(ins):
                ins.then_inc(s["ss"], 1)
                ssn[0] += 1
                return ssn[0]

            for si in range(NT512):
                for i in range(4):
                    g = si * 4 + i
                    q.wait_ge(s[f"ldp{g % 2}"], 16 * (g // 2 + 1))
                    prior = SCA * si + 2 * i
                    if prior > 0:
                        q.wait_ge(s["sc"], prior)
                    q.activation(scr[:], xs[g % 2][:], AF.Square,
                                 accum_out=stt["s2"][i][:]).then_inc(s["sc"], 1)
                    q.wait_ge(s["vv"], g + 1)
                    n0 = sinc(q.activation(stt["ln"][i][:], stt["var"][i][:], AF.Ln))
                    q.wait_ge(s["ss"], n0)
                    q.activation(stt["rs"][i][:], stt["ln"][i][:], AF.Exp,
                                 scale=-0.5).then_inc(s["sc"], 1)
            for qt in range(NQT):
                for hp in range(ND):
                    for kb in range(NKB(qt)):
                        q.wait_ge(s["pe"], peb[("s", qt, hp, kb)])
                        h0 = (kb % 2) * 1024
                        q.activation(pt.ap()[:, h0:h0 + 1024],
                                     s_view[:, h0:h0 + 1024], AF.Exp, scale=0.125,
                                     bias=negC.ap()[:, 0:1]).then_inc(s["sc"], 1)
                    q.wait_ge(s["pe"], peb[("bcA", qt, hp)])
                    n1 = sinc(q.activation(lnscr.ap()[0:64, 0:512], bc_view, AF.Ln))
                    q.wait_ge(s["ss"], n1)
                    q.activation(bcb.ap()[:, 0:512], lnscr.ap()[0:64, 0:512],
                                 AF.Exp, scale=-1.0).then_inc(s["sc"], 1)
                    q.wait_ge(s["pe"], peb[("bcB", qt, hp)])
                    n1 = sinc(q.activation(lnscr.ap()[0:64, 0:512], bc_view, AF.Ln))
                    q.wait_ge(s["ss"], n1)
                    q.activation(bcb.ap()[:, 512:1024], lnscr.ap()[0:64, 0:512],
                                 AF.Exp, scale=-1.0).then_inc(s["sc"], 1)

        def tensor_f(q):
            q.wait_ge(s["li"], 16)
            q.wait_ge(s["wl"], NWL)
            for si in range(NT512):
                for i in range(4):
                    q.wait_ge(s["ve"], ve_ln(si, i))
                    for c in range(ND):
                        jj = i * 8 + c
                        tot = si * 32 + jj
                        if tot >= 2:
                            pj = tot - 2
                            q.wait_ge(s["ve"], ve_ev(pj // 32, (pj % 32) // 8, pj % 8))
                        q.transpose(tp_view(tot),
                                    xn.ap()[:, c * 128:(c + 1) * 128],
                                    ident[:]).then_inc(s["pe"], 1)
                q.wait_ge(s["ve"], VEA * si + 36)
                for oc in range(ND):
                    ai = aidx_q(si, oc)
                    if ai >= 2:
                        q.wait_ge(s["ve"], a_consumer_ve(ai - 2))
                    for c in range(ND):
                        ins = q.matmul(qk_view(ai),
                                       wqT[c].ap()[:, oc * 128:(oc + 1) * 128],
                                       xnT[c][:], start=(c == 0), stop=(c == ND - 1))
                        if c == ND - 1:
                            ins.then_inc(s["pe"], 1)
                    ai = aidx_k(si, oc)
                    if ai >= 2:
                        q.wait_ge(s["ve"], a_consumer_ve(ai - 2))
                    for c in range(ND):
                        ins = q.matmul(qk_view(ai),
                                       wkT[c].ap()[:, oc * 128:(oc + 1) * 128],
                                       xnT[c][:], start=(c == 0), stop=(c == ND - 1))
                        if c == ND - 1:
                            ins.then_inc(s["pe"], 1)
                for j in range(8):
                    sub, ot = j // 2, j % 2
                    ai = aidx_v(si, j)
                    q.wait_ge(s["ve"], a_consumer_ve(ai - 2))
                    for c in range(ND):
                        ins = q.matmul(qk_view(ai),
                                       xnT[c].ap()[:, sub * 128:(sub + 1) * 128],
                                       wvT[c].ap()[:, ot * 512:(ot + 1) * 512],
                                       start=(c == 0), stop=(c == ND - 1))
                        if c == ND - 1:
                            ins.then_inc(s["pe"], 1)
            q.wait_ge(s["ve"], VEA_TOT)
            gk = 0
            gkk = 0
            for qt in range(NQT):
                q.wait_ge(s["lq"], 16 * 8 * (qt + 1))
                for hp in range(ND):
                    base_gk = gk
                    base_gkk = gkk

                    def emit_scores(kb):
                        kqi = base_gkk + kb // 4
                        q.wait_ge(s[f"lk{kqi % 2}"], 16 * (kqi // 2 + 1))
                        if kb >= 2:
                            q.wait_ge(s["sc"], scb[("e", qt, hp, kb - 2)])
                        elif (qt, hp) != (0, 0):
                            php = (qt, hp - 1) if hp > 0 else (qt - 1, ND - 1)
                            q.wait_ge(s["sc"], scb[("e", php[0], php[1],
                                                    NKB(php[0]) - 2 + kb)])
                        h0 = (kb % 2) * 1024
                        kcol = (kb % 4) * 128
                        for h in range(2):
                            ins = q.matmul(
                                s_view[:, h0 + h * 512:h0 + (h + 1) * 512],
                                ktl[kqi % 2].ap()[h * 64:(h + 1) * 64,
                                                  kcol:kcol + 128],
                                qtl[qt % 2][hp].ap()[h * 64:(h + 1) * 64, :],
                                start=True, stop=True)
                            if h == 1:
                                ins.then_inc(s["pe"], 1)

                    emit_scores(0)
                    if NKB(qt) > 1:
                        emit_scores(1)
                    for kb in range(NKB(qt)):
                        b4v = (base_gk + kb) % 4
                        q.wait_ge(s[f"lv{b4v}"], 16 * ((base_gk + kb) // 4 + 1))
                        mi = diag_idx(qt, kb)
                        if mi is not None:
                            q.wait_ge(s["ve"], veb[("m", qt, hp, kb)])
                        else:
                            q.wait_ge(s["sc"], scb[("e", qt, hp, kb)])
                        if kb == 0 and (qt, hp) != (0, 0):
                            php = (qt, hp - 1) if hp > 0 else (qt - 1, ND - 1)
                            q.wait_ge(s["ve"], veb[("yB", *php)])
                        h0 = (kb % 2) * 1024
                        for h in range(2):
                            if mi is not None:
                                rhs = pm[(kb % 2) * 2 + h][:]
                            else:
                                rhs = pt.ap()[:, h0 + h * 512:h0 + (h + 1) * 512]
                            ins = q.matmul(
                                y_view[h],
                                vtl[b4v].ap()[:, h * 65:(h + 1) * 65],
                                rhs,
                                start=(kb == 0), stop=(kb == NKB(qt) - 1))
                            if kb == NKB(qt) - 1:
                                ins.then_inc(s["pe"], 1)
                        if kb + 2 < NKB(qt):
                            emit_scores(kb + 2)
                    gk += NKB(qt)
                    gkk += NKB(qt) // 4
                    q.wait_ge(s["ve"], veb[("dA", qt, hp)])
                    q.matmul(bc_view, onesr[:], dden[0][:],
                             start=True, stop=True).then_inc(s["pe"], 1)
                    q.wait_ge(s["ve"], veb[("bcbA", qt, hp)])
                    q.wait_ge(s["ve"], veb[("dB", qt, hp)])
                    q.matmul(bc_view, onesr[:], dden[1][:],
                             start=True, stop=True).then_inc(s["pe"], 1)
                q.wait_ge(s["ve"], veb[("yB", qt, ND - 1)])
                for j in range(8):
                    qs, dn = j // 2, j % 2
                    gj = qt * 8 + j
                    if gj >= 1:
                        pj = gj - 1
                        q.wait_ge(s["ve"], veb[("bo", pj // 8, pj % 8)])
                    for hp in range(ND):
                        ins = q.matmul(o_view,
                                       yall[hp].ap()[:, qs * 128:(qs + 1) * 128],
                                       woT[hp].ap()[:, dn * 512:(dn + 1) * 512],
                                       start=(hp == 0), stop=(hp == ND - 1))
                        if hp == ND - 1:
                            ins.then_inc(s["pe"], 1)

        blk.gpsimd(gpsimd_f)
        blk.sync(sync_f)
        blk.vector(vector_f)
        blk.scalar(scalar_f)
        blk.tensor(tensor_f)

    return nc


def attn_host_consts():
    hd = HD
    half = hd // 2
    theta = 1.0 / (10000.0 ** (np.arange(0, hd, 2, dtype=np.float64) / hd))
    pos = np.arange(1, T + 1, dtype=np.float64)
    ang = pos[:, None] * theta[None, :]
    ang2 = np.concatenate([ang, ang], axis=1)
    cos = np.cos(ang2).T.astype(BF)
    sin = np.sin(ang2).T
    sgn = np.ones((hd, 1)); sgn[:half] = -1.0
    sins = (sin * sgn)
    sins = np.concatenate([sins[half:], sins[:half]], axis=0).astype(BF)
    cos128 = np.concatenate([cos, cos], axis=0)
    sin128 = np.concatenate([sins, sins], axis=0)
    masks = np.zeros((128, 4 * 512), dtype=BF)
    for mi in range(4):
        dk = np.arange(128)[:, None]
        dq = np.arange(512)[None, :]
        masks[:, mi * 512:(mi + 1) * 512] = (dq >= dk + mi * 128).astype(BF)
    negc = np.full((128, 1), -12.0, dtype=np.float32)
    ones = np.ones((1, 64), dtype=np.float32)
    return dict(cos_t=cos128, sin_t=sin128, msk=masks, negC=negc, ones_r=ones)


NEC = E // 128


def mamba_scratch(nc):
    u_scr = nc.dram_tensor("u_scr", [E, T], BF16, kind="Internal").ap()
    z_scr = nc.dram_tensor("z_scr", [E, T], BF16, kind="Internal").ap()
    d_scr = nc.dram_tensor("d_scr", [E, T], BF16, kind="Internal").ap()
    b_scr = nc.dram_tensor("b_scr", [NST, T], F32, kind="Internal").ap()
    y_scr = nc.dram_tensor("y_scr", [E, T], BF16, kind="Internal").ap()
    return u_scr, z_scr, d_scr, b_scr, y_scr


def build_mamba_a(p: Prog, io=None, fwprep=None):
    nc = p.nc
    if io is None:
        xin = nc.dram_tensor("xin", [T, D], F32, kind="ExternalInput").ap()
        ip_d = nc.dram_tensor("ipT", [D, 2 * E], F32, kind="ExternalInput").ap()
        cw_d = nc.dram_tensor("cwcol", [128, NEC * 4], F32, kind="ExternalInput").ap()
        xpb_d = nc.dram_tensor("xpTB", [E, NST], F32, kind="ExternalInput").ap()
        xpd_d = nc.dram_tensor("xpTd", [E, 1], F32, kind="ExternalInput").ap()
        dtw_d = nc.dram_tensor("dtw_row", [1, E], F32, kind="ExternalInput").ap()
        dtb_d = nc.dram_tensor("dtb_col", [128, NEC], F32, kind="ExternalInput").ap()
        one_d = nc.dram_tensor("one_col", [128, 1], F32, kind="ExternalInput").ap()
        ident_d = nc.dram_tensor("ident", [128, 128], BF16, kind="ExternalInput").ap()
        u_out = nc.dram_tensor("u_scr", [E, T], BF16, kind="ExternalOutput").ap()
        z_out = nc.dram_tensor("z_scr", [E, T], BF16, kind="ExternalOutput").ap()
        dl_out = nc.dram_tensor("d_scr", [E, T], BF16, kind="ExternalOutput").ap()
        b_out = nc.dram_tensor("b_scr", [NST, T], F32, kind="ExternalOutput").ap()
    else:
        (xin, ip_d, cw_d, xpb_d, xpd_d, dtw_d, dtb_d, one_d, ident_d,
         u_out, z_out, dl_out, b_out) = io

    ND = D // 128
    ipT = [p.sb(f"ipT{c}", [128, 2 * E], BF16) for c in range(ND)]
    xpB = [p.sb(f"xpB{k}", [128, NST], BF16) for k in range(NEC)]
    xpd = [p.sb(f"xpd{k}", [128, 1], BF16) for k in range(NEC)]
    dtw = p.sb("dtw", [1, E], F32)
    dtb = p.sb("dtb", [128, NEC], F32)
    onec = p.sb("onec", [128, 1], F32)
    cw = p.sb("cw", [128, NEC * 4], F32)
    diag = [p.sb(f"diag{k}", [128, 4 * 128], BF16) for k in range(NEC)]
    xs = [p.sb(f"xs{b}", [128, D], F32) for b in range(2)]
    xn = p.sb("xn", [128, D], BF16)
    xnT = [p.sb(f"xnT{c}", [128, 512], BF16) for c in range(ND)]
    u_nat = [p.sb(f"u_nat{k}", [128, 515], BF16) for k in range(NEC)]
    u_act = [p.sb(f"u_act{k}", [128, 512], BF16) for k in range(NEC)]
    zb = [p.sb(f"zb{b}", [128, 512], BF16) for b in range(2)]
    db = [p.sb(f"db{b}", [128, 512], BF16) for b in range(2)]
    B17 = p.sb("B17", [NST, 512], F32)
    dt_sb = p.sb("dt_sb", [1, 512], F32)
    r1s = p.sb("r1s", [128, 512], F32)
    l1s = p.sb("l1s", [128, 512], F32)
    a1s = p.sb("a1s", [128, 512], F32)
    e1s = p.sb("e1s", [128, 512], F32)
    stt = {nm: [p.sb(f"st_{nm}{i}", [128, 1], F32) for i in range(4)]
           for nm in ["s1", "s2", "mu", "v1", "m2", "var", "ln", "rs", "nm"]}
    scr = p.sb("scr", [128, D], BF16)
    ident = p.sb("identt", [128, 128], BF16)

    psP = p.ps("psP", [128, 2048], F32)
    psX = p.ps("psX", [128, 2048], F32)
    pool_v = [psP.ap()[:, b * 512:(b + 1) * 512] for b in range(4)]
    xpB_v = psX.ap()[0:NST, 0:512]
    xpd_v = psX.ap()[0:1, 512:1024]
    psX_bf = psX.ap().bitcast(BF16)
    tp_v = [psX_bf[:, 2048:2176], psX_bf[:, 3072:3200]]

    s = {nm: p.sem(nm) for nm in [
        "wl", "li", "ldp0", "ldp1", "ve", "vv", "sc", "pe",
        "su0", "su1", "sz0", "sz1", "sd0", "sd1", "sb0", "vs", "ss"]}

    fw = FWPrep(p, fwprep) if fwprep is not None else None

    peb, veb, scb = {}, {}, {}
    pe_c, ve_c, sc_c = [0], [0], [0]
    pool_seq = []
    for si in range(NT512):
        for i in range(4):
            for c in range(8):
                pe_c[0] += 1
                peb[("tp", si, i * 8 + c)] = pe_c[0]
        for ec in range(NEC):
            pe_c[0] += 1
            peb[("u", si, ec)] = pe_c[0]
            pool_seq.append(("u", si, ec))
            pe_c[0] += 1
            peb[("cv", si, ec)] = pe_c[0]
            pool_seq.append(("cv", si, ec))
        for ec in range(NEC):
            pe_c[0] += 1
            peb[("z", si, ec)] = pe_c[0]
            pool_seq.append(("z", si, ec))
        pe_c[0] += 1
        peb[("xpB", si)] = pe_c[0]
        pe_c[0] += 1
        peb[("xpd", si)] = pe_c[0]
        for ec in range(NEC):
            pe_c[0] += 1
            peb[("dl", si, ec)] = pe_c[0]
            pool_seq.append(("dl", si, ec))
    for si in range(NT512):
        for i in range(4):
            ve_c[0] += 1
            veb[("ln", si, i)] = ve_c[0]
            for c in range(8):
                ve_c[0] += 1
                veb[("ev", si, i * 8 + c)] = ve_c[0]
        for ec in range(NEC):
            ve_c[0] += 1
            veb[("halo", si, ec)] = ve_c[0]
            ve_c[0] += 1
            veb[("ue", si, ec)] = ve_c[0]
        for ec in range(NEC):
            ve_c[0] += 1
            veb[("ze", si, ec)] = ve_c[0]
        ve_c[0] += 1
        veb[("B17", si)] = ve_c[0]
        ve_c[0] += 1
        veb[("dt", si)] = ve_c[0]
        for ec in range(NEC):
            ve_c[0] += 1
            veb[("dle", si, ec)] = ve_c[0]
    for si in range(NT512):
        for i in range(4):
            sc_c[0] += 1
            scb[("sq", si, i)] = sc_c[0]
            sc_c[0] += 1
            scb[("rs", si, i)] = sc_c[0]
        for ec in range(NEC):
            sc_c[0] += 1
            scb[("silu", si, ec)] = sc_c[0]
        for ec in range(NEC):
            sc_c[0] += 1
            scb[("r1", si, ec)] = sc_c[0]

    def pool_consumer(kind, si, ec):
        if kind == "u":
            return ("ve", veb[("ue", si, ec)])
        if kind == "cv":
            return ("sc", scb[("silu", si, ec)])
        if kind == "z":
            return ("ve", veb[("ze", si, ec)])
        return ("sc", scb[("r1", si, ec)])

    pool_of = {}
    for idx, key in enumerate(pool_seq):
        pool_of[key] = idx

    def pool_wait(q, key):
        idx = pool_of[key]
        if idx >= 4:
            k2 = pool_seq[idx - 4]
            sem_nm, val = pool_consumer(*k2)
            q.wait_ge(s[sem_nm], val)
        return pool_v[idx % 4]

    with p.block() as blk:

        def gpsimd_f(q):
            for c in range(ND):
                q.dma_start(out=ipT[c][:], in_=ip_d[c * 128:(c + 1) * 128, :]).then_inc(s["wl"], 16)
            for k in range(NEC):
                q.dma_start(out=xpB[k][:], in_=xpb_d[k * 128:(k + 1) * 128, :]).then_inc(s["wl"], 16)
                q.dma_start(out=xpd[k][:], in_=xpd_d[k * 128:(k + 1) * 128, :]).then_inc(s["wl"], 16)
            q.dma_start(out=dtw[:], in_=dtw_d).then_inc(s["wl"], 16)
            q.dma_start(out=dtb[:], in_=dtb_d).then_inc(s["wl"], 16)
            q.dma_start(out=onec[:], in_=one_d).then_inc(s["wl"], 16)
            q.dma_start(out=cw[:], in_=cw_d).then_inc(s["wl"], 16)
            if fw is not None:
                fw.run_gp(q)
        NWL = 16 * (ND + 2 * NEC + 4)

        def sync_f(q):
            q.dma_start(out=ident[:], in_=ident_d).then_inc(s["li"], 16)
            for si in range(NT512):
                for i in range(4):
                    g = si * 4 + i
                    if g >= 2:
                        q.wait_ge(s["ve"], veb[("ln", (g - 2) // 4, (g - 2) % 4)])
                    q.dma_start(out=xs[g % 2][:],
                                in_=xin[g * 128:(g + 1) * 128, :]).then_inc(s[f"ldp{g % 2}"], 16)
                for ec in range(NEC):
                    gu = si * NEC + ec
                    q.wait_ge(s["sc"], scb[("silu", si, ec)])
                    q.dma_start(out=u_out[ec * 128:(ec + 1) * 128, si * 512:(si + 1) * 512],
                                in_=u_act[ec][:]).then_inc(s[f"su{gu % 2}"], 16)
                for ec in range(NEC):
                    gz = si * NEC + ec
                    q.wait_ge(s["ve"], veb[("ze", si, ec)])
                    q.dma_start(out=z_out[ec * 128:(ec + 1) * 128, si * 512:(si + 1) * 512],
                                in_=zb[gz % 2][:]).then_inc(s[f"sz{gz % 2}"], 16)
                q.wait_ge(s["ve"], veb[("B17", si)])
                q.dma_start(out=b_out[:, si * 512:(si + 1) * 512],
                            in_=B17[:]).then_inc(s["sb0"], 16)
                for ec in range(NEC):
                    gd = si * NEC + ec
                    q.wait_ge(s["ve"], veb[("dle", si, ec)])
                    q.dma_start(out=dl_out[ec * 128:(ec + 1) * 128, si * 512:(si + 1) * 512],
                                in_=db[gd % 2][:]).then_inc(s[f"sd{gd % 2}"], 16)
            q.wait_ge(s["su0"], 16 * 64)
            q.wait_ge(s["su1"], 16 * 64)
            q.wait_ge(s["sz0"], 16 * 64)
            q.wait_ge(s["sz1"], 16 * 64)
            q.wait_ge(s["sd0"], 16 * 64)
            q.wait_ge(s["sd1"], 16 * 64)
            q.wait_ge(s["sb0"], 16 * 8)

        def vector_f(q):
            vsn = [0]

            def vinc(ins):
                ins.then_inc(s["vs"], 1)
                vsn[0] += 1
                return vsn[0]

            def vwait(n):
                if n > 0:
                    q.wait_ge(s["vs"], n)

            q.wait_ge(s["wl"], NWL)
            q.wait_ge(s["li"], 16)
            dlast = 0
            for k in range(NEC):
                for j in range(4):
                    dlast = vinc(q.tensor_scalar(
                        diag[k].ap()[:, j * 128:(j + 1) * 128], ident[:],
                        cw.ap()[:, k * 4 + j:k * 4 + j + 1], None, ALU.mult))
            vwait(dlast)
            q.sem_inc(s["ve"], 0) if False else None
            for si in range(NT512):
                for i in range(4):
                    g = si * 4 + i
                    q.wait_ge(s[f"ldp{g % 2}"], 16 * (g // 2 + 1))
                    a = vinc(q.reduce_sum(stt["s1"][i][:], xs[g % 2][:], AX.X))
                    vwait(a)
                    b = vinc(q.tensor_scalar(stt["mu"][i][:], stt["s1"][i][:],
                                             1.0 / D, None, ALU.mult))
                    q.wait_ge(s["sc"], scb[("sq", si, i)])
                    cv = vinc(q.tensor_scalar(stt["v1"][i][:], stt["s2"][i][:],
                                              1.0 / D, 1e-5, ALU.mult, ALU.add))
                    vwait(b)
                    dv = vinc(q.scalar_tensor_tensor(stt["m2"][i][:], stt["mu"][i][:],
                                                     -1.0, stt["mu"][i][:],
                                                     ALU.mult, ALU.mult))
                    vwait(dv)
                    e = vinc(q.tensor_tensor(stt["var"][i][:], stt["v1"][i][:],
                                             stt["m2"][i][:], ALU.add))
                    vwait(e)
                    q.sem_inc(s["vv"], 1)
                    q.wait_ge(s["sc"], scb[("rs", si, i)])
                    f = vinc(q.scalar_tensor_tensor(stt["nm"][i][:], stt["mu"][i][:],
                                                    -1.0, stt["rs"][i][:],
                                                    ALU.mult, ALU.mult))
                    if g > 0:
                        pg = g - 1
                        q.wait_ge(s["pe"], peb[("tp", pg // 4, (pg % 4) * 8 + 7)])
                    vwait(f)
                    q.tensor_scalar(
                        xn[:], xs[g % 2][:], stt["rs"][i].ap()[:, 0:1],
                        stt["nm"][i].ap()[:, 0:1], ALU.mult, ALU.add,
                    ).then_inc(s["ve"], 1)
                    for c in range(8):
                        jj = i * 8 + c
                        q.wait_ge(s["pe"], peb[("tp", si, jj)])
                        q.tensor_copy(xnT[c].ap()[:, i * 128:(i + 1) * 128],
                                      tp_v[jj % 2]).then_inc(s["ve"], 1)
                for ec in range(NEC):
                    gu = si * NEC + ec
                    if si > 0:
                        q.wait_ge(s["pe"], peb[("cv", si - 1, ec)])
                        hc = vinc(q.tensor_copy(u_nat[ec].ap()[:, 0:3],
                                                u_nat[ec].ap()[:, 512:515]))
                        vwait(hc)
                        q.sem_inc(s["ve"], 1)
                    else:
                        hc = vinc(q.memset(u_nat[ec].ap()[:, 0:3], 0.0))
                        vwait(hc)
                        q.sem_inc(s["ve"], 1)
                    q.wait_ge(s["pe"], peb[("u", si, ec)])
                    ue = vinc(q.tensor_copy(u_nat[ec].ap()[:, 3:515],
                                            pool_v[pool_of[("u", si, ec)] % 4]))
                    vwait(ue)
                    q.sem_inc(s["ve"], 1)
                    if fw is not None:
                        fw.pump_v(q, vinc, vwait)
                for ec in range(NEC):
                    gz = si * NEC + ec
                    q.wait_ge(s["pe"], peb[("z", si, ec)])
                    if gz >= 2:
                        q.wait_ge(s[f"sz{gz % 2}"], 16 * ((gz - 2) // 2 + 1))
                    ze = vinc(q.tensor_copy(zb[gz % 2][:],
                                            pool_v[pool_of[("z", si, ec)] % 4]))
                    vwait(ze)
                    q.sem_inc(s["ve"], 1)
                    if fw is not None:
                        fw.pump_v(q, vinc, vwait)
                q.wait_ge(s["pe"], peb[("xpB", si)])
                if si >= 1:
                    q.wait_ge(s["sb0"], 16 * si)
                b17 = vinc(q.tensor_copy(B17[:], xpB_v))
                vwait(b17)
                q.sem_inc(s["ve"], 1)
                q.wait_ge(s["pe"], peb[("xpd", si)])
                dts = vinc(q.tensor_copy(dt_sb[:], xpd_v))
                vwait(dts)
                q.sem_inc(s["ve"], 1)
                for ec in range(NEC):
                    gd = si * NEC + ec
                    q.wait_ge(s["sc"], scb[("r1", si, ec)])
                    if gd >= 2:
                        q.wait_ge(s[f"sd{gd % 2}"], 16 * ((gd - 2) // 2 + 1))
                    de = vinc(q.tensor_tensor(db[gd % 2][:], r1s[:], l1s[:], ALU.add))
                    vwait(de)
                    q.sem_inc(s["ve"], 1)
                    if fw is not None:
                        fw.pump_v(q, vinc, vwait)
            if fw is not None:
                fw.drain_v(q, vinc, vwait)

        def scalar_f(q):
            ssn = [0]

            def sinc(ins):
                ins.then_inc(s["ss"], 1)
                ssn[0] += 1
                return ssn[0]

            for si in range(NT512):
                for i in range(4):
                    g = si * 4 + i
                    q.wait_ge(s[f"ldp{g % 2}"], 16 * (g // 2 + 1))
                    prior = scb[("sq", si, i)] - 1
                    if prior > 0:
                        q.wait_ge(s["sc"], prior)
                    q.activation(scr[:], xs[g % 2][:], AF.Square,
                                 accum_out=stt["s2"][i][:]).then_inc(s["sc"], 1)
                    q.wait_ge(s["vv"], g + 1)
                    n0 = sinc(q.activation(stt["ln"][i][:], stt["var"][i][:], AF.Ln))
                    q.wait_ge(s["ss"], n0)
                    q.activation(stt["rs"][i][:], stt["ln"][i][:], AF.Exp,
                                 scale=-0.5).then_inc(s["sc"], 1)
                for ec in range(NEC):
                    gu = si * NEC + ec
                    q.wait_ge(s["pe"], peb[("cv", si, ec)])
                    if si >= 1:
                        q.wait_ge(s["pe"], peb[("xpd", si - 1)])
                    if gu >= 2:
                        q.wait_ge(s[f"su{gu % 2}"], 16 * ((gu - 2) // 2 + 1))
                    q.activation(u_act[ec][:],
                                 pool_v[pool_of[("cv", si, ec)] % 4],
                                 AF.Silu).then_inc(s["sc"], 1)
                for ec in range(NEC):
                    q.wait_ge(s["pe"], peb[("dl", si, ec)])
                    pv = pool_v[pool_of[("dl", si, ec)] % 4]
                    bb = dtb.ap()[:, ec:ec + 1]
                    if (si, ec) != (0, 0):
                        pec = ec - 1 if ec > 0 else NEC - 1
                        psi = si if ec > 0 else si - 1
                        q.wait_ge(s["ve"], veb[("dle", psi, pec)])
                    n1 = sinc(q.activation(a1s[:], pv, AF.Abs, bias=bb))
                    q.wait_ge(s["ss"], n1)
                    n2 = sinc(q.activation(e1s[:], a1s[:], AF.Exp, scale=-1.0))
                    q.wait_ge(s["ss"], n2)
                    n3 = sinc(q.activation(l1s[:], e1s[:], AF.Ln,
                                           bias=onec.ap()[:, 0:1]))
                    q.wait_ge(s["ss"], n3)
                    q.activation(r1s[:], pv, AF.Relu, bias=bb).then_inc(s["sc"], 1)

        def tensor_f(q):
            q.wait_ge(s["li"], 16)
            q.wait_ge(s["wl"], NWL)
            q.wait_ge(s["vs"], 64)
            for si in range(NT512):
                for i in range(4):
                    q.wait_ge(s["ve"], veb[("ln", si, i)])
                    for c in range(8):
                        jj = i * 8 + c
                        tot = si * 32 + jj
                        if tot >= 2:
                            pj = tot - 2
                            q.wait_ge(s["ve"], veb[("ev", pj // 32, pj % 32)])
                        q.transpose(tp_v[jj % 2],
                                    xn.ap()[:, c * 128:(c + 1) * 128],
                                    ident[:]).then_inc(s["pe"], 1)
                q.wait_ge(s["ve"], veb[("ev", si, 31)])
                for ec in range(NEC):
                    pv = pool_wait(q, ("u", si, ec))
                    for c in range(ND):
                        ins = q.matmul(pv, ipT[c].ap()[:, ec * 128:(ec + 1) * 128],
                                       xnT[c][:], start=(c == 0), stop=(c == ND - 1))
                        if c == ND - 1:
                            ins.then_inc(s["pe"], 1)
                    pv2 = pool_wait(q, ("cv", si, ec))
                    q.wait_ge(s["ve"], veb[("ue", si, ec)])
                    for j in range(4):
                        ins = q.matmul(pv2, diag[ec].ap()[:, j * 128:(j + 1) * 128],
                                       u_nat[ec].ap()[:, j:j + 512],
                                       start=(j == 0), stop=(j == 3))
                        if j == 3:
                            ins.then_inc(s["pe"], 1)
                for ec in range(NEC):
                    pv = pool_wait(q, ("z", si, ec))
                    for c in range(ND):
                        ins = q.matmul(pv, ipT[c].ap()[:, E + ec * 128:E + (ec + 1) * 128],
                                       xnT[c][:], start=(c == 0), stop=(c == ND - 1))
                        if c == ND - 1:
                            ins.then_inc(s["pe"], 1)
                if si >= 1:
                    q.wait_ge(s["ve"], veb[("B17", si - 1)])
                for k in range(NEC):
                    q.wait_ge(s["sc"], scb[("silu", si, k)])
                    ins = q.matmul(xpB_v, xpB[k][:], u_act[k][:],
                                   start=(k == 0), stop=(k == NEC - 1))
                    if k == NEC - 1:
                        ins.then_inc(s["pe"], 1)
                if si >= 1:
                    q.wait_ge(s["ve"], veb[("dt", si - 1)])
                for k in range(NEC):
                    ins = q.matmul(xpd_v, xpd[k][:], u_act[k][:],
                                   start=(k == 0), stop=(k == NEC - 1))
                    if k == NEC - 1:
                        ins.then_inc(s["pe"], 1)
                q.wait_ge(s["ve"], veb[("dt", si)])
                for ec in range(NEC):
                    pv = pool_wait(q, ("dl", si, ec))
                    q.matmul(pv, dtw.ap()[0:1, ec * 128:(ec + 1) * 128],
                             dt_sb[:], start=True, stop=True).then_inc(s["pe"], 1)

        blk.gpsimd(gpsimd_f)
        blk.sync(sync_f)
        blk.vector(vector_f)
        blk.scalar(scalar_f)
        blk.tensor(tensor_f)

    return nc


def build_mamba_b(p: Prog, io=None):
    nc = p.nc
    if io is None:
        u_in = nc.dram_tensor("u_scr", [E, T], BF16, kind="ExternalInput").ap()
        z_in = nc.dram_tensor("z_scr", [E, T], BF16, kind="ExternalInput").ap()
        d_in = nc.dram_tensor("d_scr", [E, T], BF16, kind="ExternalInput").ap()
        b_in = nc.dram_tensor("b_scr", [NST, T], F32, kind="ExternalInput").ap()
        ac_d = nc.dram_tensor("a_cols", [E, NST], F32, kind="ExternalInput").ap()
        dc_d = nc.dram_tensor("d_cols", [128, NEC], F32, kind="ExternalInput").ap()
        y_out = nc.dram_tensor("y_scr", [E, T], BF16, kind="ExternalOutput").ap()
    else:
        u_in, z_in, d_in, b_in, ac_d, dc_d, y_out = io

    acolr = [p.sb(f"acolr{k}", [128, NST], F32) for k in range(NEC)]
    acol = [p.sb(f"acol{k}", [128, NST], F32) for k in range(NEC)]
    dcol = p.sb("dcol", [128, NEC], F32)
    dch = [p.sb(f"dch{b}", [128, T], BF16) for b in range(2)]
    uch = [p.sb(f"uch{b}", [128, T], BF16) for b in range(2)]
    zch = [p.sb(f"zch{b}", [128, T], BF16) for b in range(2)]
    brep = [p.sb(f"brep{b}", [128, T], BF16) for b in range(2)]
    at = [p.sb(f"at{b}", [128, T], BF16) for b in range(2)]
    dbt = [p.sb(f"dbt{b}", [128, T], BF16) for b in range(2)]
    ht = p.sb("ht", [128, T], BF16)
    prod = p.sb("prod", [128, T], BF16)
    yacc = p.sb("yacc", [128, T], BF16)
    t1g = p.sb("t1g", [128, T], BF16)
    szg = p.sb("szg", [128, T], BF16)
    ygo = [p.sb(f"ygo{b}", [128, T], BF16) for b in range(2)]

    s = {nm: p.sem(nm) for nm in [
        "wl", "ld0", "ld1", "lu0", "lu1", "lz0", "lz1", "lb0", "lb1",
        "ve", "sc", "sy0", "sy1", "vs", "ss"]}

    with p.block() as blk:
        VEB = NEC * 0 + 16 * 3 + 2

        def ve_db(ec, n):
            return VEB * ec + 3 * n + 1

        def ve_scan(ec, n):
            return VEB * ec + 3 * n + 2

        def ve_acc(ec, n):
            return VEB * ec + 3 * n + 3

        def ve_t1(ec):
            return VEB * ec + 48 + 1

        def ve_yg(ec):
            return VEB * ec + 48 + 2

        SCB = 17

        def sc_a(ec, n):
            return SCB * ec + n + 1

        def sc_sz(ec):
            return SCB * ec + 17

        def gpsimd_f(q):
            for k in range(NEC):
                q.dma_start(out=acolr[k][:],
                            in_=ac_d[k * 128:(k + 1) * 128, :]).then_inc(s["wl"], 16)
            q.dma_start(out=dcol[:], in_=dc_d).then_inc(s["wl"], 16)
            for ec in range(NEC):
                for n in range(NST):
                    g = ec * NST + n
                    if g >= 2:
                        pg = g - 2
                        q.wait_ge(s["ve"], ve_acc(pg // NST, pg % NST))
                    q.dma_start(
                        out=brep[g % 2][:],
                        in_=b_in[n:n + 1, :].broadcast_to([128, T]),
                    ).then_inc(s[f"lb{g % 2}"], 16)
        NWL = 16 * (NEC + 1)

        def sync_f(q):
            for ec in range(NEC):
                if ec >= 2:
                    q.wait_ge(s["ve"], ve_acc(ec - 2, NST - 1))
                q.dma_start(out=dch[ec % 2][:],
                            in_=d_in[ec * 128:(ec + 1) * 128, :]).then_inc(s[f"ld{ec % 2}"], 16)
                if ec >= 2:
                    q.wait_ge(s["ve"], ve_t1(ec - 2))
                q.dma_start(out=uch[ec % 2][:],
                            in_=u_in[ec * 128:(ec + 1) * 128, :]).then_inc(s[f"lu{ec % 2}"], 16)
                if ec >= 2:
                    q.wait_ge(s["ve"], ve_yg(ec - 2))
                q.dma_start(out=zch[ec % 2][:],
                            in_=z_in[ec * 128:(ec + 1) * 128, :]).then_inc(s[f"lz{ec % 2}"], 16)
                q.wait_ge(s["ve"], ve_yg(ec))
                q.dma_start(out=y_out[ec * 128:(ec + 1) * 128, :],
                            in_=ygo[ec % 2][:]).then_inc(s[f"sy{ec % 2}"], 16)
            q.wait_ge(s["sy0"], 16 * 8)
            q.wait_ge(s["sy1"], 16 * 8)

        def vector_f(q):
            vsn = [0]

            def vinc(ins):
                ins.then_inc(s["vs"], 1)
                vsn[0] += 1
                return vsn[0]

            def vwait(n):
                if n > 0:
                    q.wait_ge(s["vs"], n)

            for ec in range(NEC):
                for n in range(NST):
                    g = ec * NST + n
                    q.wait_ge(s[f"lb{g % 2}"], 16 * (g // 2 + 1))
                    q.wait_ge(s[f"ld{ec % 2}"], 16 * (ec // 2 + 1))
                    db_ = vinc(q.tensor_tensor(dbt[g % 2][:], dch[ec % 2][:],
                                               brep[g % 2][:], ALU.mult))
                    ins = q.tensor_tensor(dbt[g % 2][:], dch[ec % 2][:],
                                          brep[g % 2][:], ALU.mult) if False else None
                    vwait(db_)
                    q.sem_inc(s["ve"], 1)
                    q.wait_ge(s["sc"], sc_a(ec, n))
                    sc_ = vinc(q.tensor_tensor_scan(ht[:], at[g % 2][:], dbt[g % 2][:],
                                                    0.0, ALU.mult, ALU.add))
                    vwait(sc_)
                    q.sem_inc(s["ve"], 1)
                    if n == 0:
                        pr = vinc(q.tensor_tensor(yacc[:], ht[:], brep[g % 2][:],
                                                  ALU.mult))
                    else:
                        pr = vinc(q.tensor_tensor(prod[:], ht[:], brep[g % 2][:],
                                                  ALU.mult))
                        vwait(pr)
                        pr = vinc(q.tensor_tensor(yacc[:], yacc[:], prod[:], ALU.add))
                    vwait(pr)
                    q.sem_inc(s["ve"], 1)
                q.wait_ge(s[f"lu{ec % 2}"], 16 * (ec // 2 + 1))
                t1_ = vinc(q.scalar_tensor_tensor(t1g[:], uch[ec % 2][:],
                                                  dcol.ap()[:, ec:ec + 1],
                                                  yacc[:], ALU.mult, ALU.add))
                vwait(t1_)
                q.sem_inc(s["ve"], 1)
                q.wait_ge(s["sc"], sc_sz(ec))
                if ec >= 2:
                    q.wait_ge(s[f"sy{ec % 2}"], 16 * ((ec - 2) // 2 + 1))
                yg_ = vinc(q.tensor_tensor(ygo[ec % 2][:], t1g[:], szg[:], ALU.mult))
                vwait(yg_)
                q.sem_inc(s["ve"], 1)

        def scalar_f(q):
            ssn = [0]

            def sinc(ins):
                ins.then_inc(s["ss"], 1)
                ssn[0] += 1
                return ssn[0]

            q.wait_ge(s["wl"], NWL)
            for k in range(NEC):
                n1 = sinc(q.activation(acol[k][:], acolr[k][:], AF.Exp))
                q.wait_ge(s["ss"], n1)
                n2 = sinc(q.activation(acol[k][:], acol[k][:], AF.Copy, scale=-1.0))
                q.wait_ge(s["ss"], n2)
            for ec in range(NEC):
                for n in range(NST):
                    g = ec * NST + n
                    q.wait_ge(s[f"ld{ec % 2}"], 16 * (ec // 2 + 1))
                    if g >= 2:
                        pg = g - 2
                        q.wait_ge(s["ve"], ve_scan(pg // NST, pg % NST))
                    q.activation(at[g % 2][:], dch[ec % 2][:], AF.Exp,
                                 scale=acol[ec].ap()[:, n:n + 1]).then_inc(s["sc"], 1)
                q.wait_ge(s[f"lz{ec % 2}"], 16 * (ec // 2 + 1))
                if ec >= 1:
                    q.wait_ge(s["ve"], ve_yg(ec - 1))
                q.activation(szg[:], zch[ec % 2][:], AF.Silu).then_inc(s["sc"], 1)

        blk.gpsimd(gpsimd_f)
        blk.sync(sync_f)
        blk.vector(vector_f)
        blk.scalar(scalar_f)

    return nc


def build_mamba_c(p: Prog, io=None):
    nc = p.nc
    if io is None:
        xin = nc.dram_tensor("xin", [T, D], F32, kind="ExternalInput").ap()
        y_in = nc.dram_tensor("y_scr", [E, T], BF16, kind="ExternalInput").ap()
        op_d = nc.dram_tensor("opT", [E, D], F32, kind="ExternalInput").ap()
        ob_d = nc.dram_tensor("ob_b", [128, D], F32, kind="ExternalInput").ap()
        xout = nc.dram_tensor("xout", [T, D], F32, kind="ExternalOutput").ap()
    else:
        xin, y_in, op_d, ob_d, xout = io

    opT = [p.sb(f"opT{k}", [128, D], BF16) for k in range(NEC)]
    ob = p.sb("ob", [128, D], F32)
    ygt = [[p.sb(f"ygt{b}_{k}", [128, 512], BF16) for k in range(NEC)]
           for b in range(2)]
    xr = [p.sb(f"xr{b}", [128, 512], F32) for b in range(2)]
    xo = [p.sb(f"xo{b}", [128, 512], F32) for b in range(2)]
    ps_o = [p.ps(f"ps_o{b}", [128, 512], F32) for b in range(2)]

    s = {nm: p.sem(nm) for nm in [
        "wl", "ly", "lrp0", "lrp1", "ve", "pe", "stp0", "stp1", "vs"]}

    def ve_res(gj):
        return gj + 1

    def pe_o(gj):
        return gj + 1

    with p.block() as blk:

        def gpsimd_f(q):
            for k in range(NEC):
                q.dma_start(out=opT[k][:],
                            in_=op_d[k * 128:(k + 1) * 128, :]).then_inc(s["wl"], 16)
            q.dma_start(out=ob[:], in_=ob_d).then_inc(s["wl"], 16)
        NWL = 16 * (NEC + 1)

        def sync_f(q):
            for si in range(NT512):
                for k in range(NEC):
                    if si >= 2:
                        q.wait_ge(s["pe"], pe_o((si - 2) * 8 + 7))
                    q.dma_start(out=ygt[si % 2][k][:],
                                in_=y_in[k * 128:(k + 1) * 128,
                                         si * 512:(si + 1) * 512]).then_inc(s["ly"], 16)
                for j in range(8):
                    gj = si * 8 + j
                    qs, dn = j // 2, j % 2
                    if gj >= 2:
                        pj = gj - 2
                        q.wait_ge(s["ve"], ve_res(pj))
                    q.dma_start(out=xr[gj % 2][:],
                                in_=xin[(si * 4 + qs) * 128:(si * 4 + qs + 1) * 128,
                                        dn * 512:(dn + 1) * 512],
                                ).then_inc(s[f"lrp{gj % 2}"], 16)
                    if gj >= 2:
                        pj = gj - 2
                        q.dma_start(
                            out=xout[(pj // 2) * 128:(pj // 2 + 1) * 128,
                                     (pj % 2) * 512:(pj % 2 + 1) * 512],
                            in_=xo[pj % 2][:]).then_inc(s[f"stp{pj % 2}"], 16)
            for gj in range(NT512 * 8 - 2, NT512 * 8):
                q.wait_ge(s["ve"], ve_res(gj))
                q.dma_start(
                    out=xout[(gj // 2) * 128:(gj // 2 + 1) * 128,
                             (gj % 2) * 512:(gj % 2 + 1) * 512],
                    in_=xo[gj % 2][:]).then_inc(s[f"stp{gj % 2}"], 16)
            q.wait_ge(s["stp0"], 16 * NT128)
            q.wait_ge(s["stp1"], 16 * NT128)

        def vector_f(q):
            vsn = [0]

            def vinc(ins):
                ins.then_inc(s["vs"], 1)
                vsn[0] += 1
                return vsn[0]

            def vwait(n):
                if n > 0:
                    q.wait_ge(s["vs"], n)

            for si in range(NT512):
                for j in range(8):
                    gj = si * 8 + j
                    qs, dn = j // 2, j % 2
                    q.wait_ge(s["pe"], pe_o(gj))
                    if gj >= 2:
                        q.wait_ge(s[f"stp{gj % 2}"], 16 * ((gj - 2) // 2 + 1))
                    ov = vinc(q.tensor_tensor(xo[gj % 2][:], ps_o[gj % 2][:],
                                              ob.ap()[:, dn * 512:(dn + 1) * 512],
                                              ALU.add))
                    vwait(ov)
                    q.sem_inc(s["ve"], 0) if False else None
                    q.wait_ge(s[f"lrp{gj % 2}"], 16 * (gj // 2 + 1))
                    o2 = vinc(q.tensor_tensor(xo[gj % 2][:], xo[gj % 2][:],
                                              xr[gj % 2][:], ALU.add))
                    vwait(o2)
                    q.sem_inc(s["ve"], 1)

        def tensor_f(q):
            q.wait_ge(s["wl"], NWL)
            for si in range(NT512):
                q.wait_ge(s["ly"], 16 * NEC * (si + 1))
                for j in range(8):
                    gj = si * 8 + j
                    qs, dn = j // 2, j % 2
                    if gj >= 2:
                        q.wait_ge(s["ve"], ve_res(gj - 2))
                    for k in range(NEC):
                        ins = q.matmul(ps_o[gj % 2][:],
                                       ygt[si % 2][k].ap()[:, qs * 128:(qs + 1) * 128],
                                       opT[k].ap()[:, dn * 512:(dn + 1) * 512],
                                       start=(k == 0), stop=(k == NEC - 1))
                        if k == NEC - 1:
                            ins.then_inc(s["pe"], 1)

        blk.gpsimd(gpsimd_f)
        blk.sync(sync_f)
        blk.vector(vector_f)
        blk.tensor(tensor_f)

    return nc


def _u16(d):
    return {k: (v.view(np.uint16) if v.dtype == BF else np.ascontiguousarray(v))
            for k, v in d.items()}


def build_fused(p: Prog):
    nc = p.nc
    dt = nc.dram_tensor
    x0 = dt("x0", [T, D], F32, kind="ExternalInput").ap()
    ident_d = dt("ident", [128, 128], BF16, kind="ExternalInput").ap()
    ip_d = dt("ipT", [D, 2 * E], F32, kind="ExternalInput").ap()
    cw_d = dt("cwcol", [128, NEC * 4], F32, kind="ExternalInput").ap()
    xpb_d = dt("xpTB", [E, NST], F32, kind="ExternalInput").ap()
    xpd_d = dt("xpTd", [E, 1], F32, kind="ExternalInput").ap()
    dtw_d = dt("dtw_row", [1, E], F32, kind="ExternalInput").ap()
    dtb_d = dt("dtb_col", [128, NEC], F32, kind="ExternalInput").ap()
    one_d = dt("one_col", [128, 1], F32, kind="ExternalInput").ap()
    ac_d = dt("a_cols", [E, NST], F32, kind="ExternalInput").ap()
    dc_d = dt("d_cols", [128, NEC], F32, kind="ExternalInput").ap()
    op_d = dt("opT", [E, D], F32, kind="ExternalInput").ap()
    ob_d = dt("ob_b", [128, D], F32, kind="ExternalInput").ap()
    wg0_d = dt("wg0", [D, H0], I32, kind="ExternalInput").ap()
    wu0_d = dt("wu0", [D, H0], I32, kind="ExternalInput").ap()
    wd0_d = dt("wd0", [H0, D], I32, kind="ExternalInput").ap()
    wq_d = dt("wq", [D, D], F32, kind="ExternalInput").ap()
    wk_d = dt("wk", [D, D], F32, kind="ExternalInput").ap()
    wv_d = dt("wv", [D, D], F32, kind="ExternalInput").ap()
    wo_d = dt("wo", [D, D], F32, kind="ExternalInput").ap()
    bcol_d = dt("bcol", [128, 16], F32, kind="ExternalInput").ap()
    bvb_d = dt("bvb", [128, D], F32, kind="ExternalInput").ap()
    bob_d = dt("bob", [128, D], F32, kind="ExternalInput").ap()
    cos_d = dt("cos_t", [128, T], BF16, kind="ExternalInput").ap()
    sin_d = dt("sin_t", [128, T], BF16, kind="ExternalInput").ap()
    msk_d = dt("msk", [128, 4 * 512], BF16, kind="ExternalInput").ap()
    negC_d = dt("negC", [128, 1], F32, kind="ExternalInput").ap()
    ones_d = dt("ones_r", [1, 64], F32, kind="ExternalInput").ap()
    wg1_d = dt("wg1", [I1, D], F32, kind="ExternalInput").ap()
    wu1_d = dt("wu1", [I1, D], F32, kind="ExternalInput").ap()
    wd1_d = dt("wd1", [D, I1], F32, kind="ExternalInput").ap()
    out_d = dt("out", [T, D], F32, kind="ExternalOutput").ap()
    u_scr, z_scr, d_scr, b_scr, y_scr = mamba_scratch(nc)
    x1 = dt("x1_scr", [T, D], F32, kind="Internal").ap()
    x2 = dt("x2_scr", [T, D], F32, kind="Internal").ap()
    x3 = dt("x3_scr", [T, D], F32, kind="Internal").ap()
    wgq1 = dt("wgq1_scr", [I1, D], BF16, kind="Internal").ap()
    wuq1 = dt("wuq1_scr", [I1, D], BF16, kind="Internal").ap()
    wdq1 = dt("wdq1_scr", [D, I1], BF16, kind="Internal").ap()

    p.push("ma")
    build_mamba_a(p, io=(x0, ip_d, cw_d, xpb_d, xpd_d, dtw_d, dtb_d, one_d,
                         ident_d, u_scr, z_scr, d_scr, b_scr),
                   fwprep=(wg1_d, wu1_d, wd1_d, wgq1, wuq1, wdq1))
    p.pop()
    p.push("mb")
    build_mamba_b(p, io=(u_scr, z_scr, d_scr, b_scr, ac_d, dc_d, y_scr))
    p.pop()
    p.push("mc")
    build_mamba_c(p, io=(x0, y_scr, op_d, ob_d, x1))
    p.pop()
    p.push("f0")
    build_ffn0(p, io=(x1, wg0_d, wu0_d, wd0_d, ident_d, x2))
    p.pop()
    p.push("at")
    build_attn(p, io=(x2, wq_d, wk_d, wv_d, wo_d, bcol_d, bvb_d, bob_d,
                      cos_d, sin_d, msk_d, ident_d, negC_d, ones_d, x3))
    p.pop()
    p.push("fa")
    build_ffn1(p, io=(x3, wgq1, wuq1, wdq1, ident_d, out_d))
    p.pop()
    return nc


def kernel_fused(**inputs):
    B = 8
    cores = list(range(B))
    hp = _host_prep(**inputs)
    p = Prog()
    nc = build_fused(p)
    maps = []
    for b in range(B):
        m = dict(hp["shared"])
        m["x0"] = hp["x"][b]
        maps.append(m)
    res = run_bass_kernel_spmd(nc, maps, core_ids=cores).results
    return np.stack([res[b]["out"] for b in range(B)], axis=0)


def _host_prep(L0_attn_x, L0_attn_in_proj_w, L0_attn_conv_w, L0_attn_x_proj_w,
               L0_attn_dt_w, L0_attn_dt_b, L0_attn_A_param, L0_attn_D_param,
               L0_attn_out_w, L0_attn_out_b, L0_ffn_w_gate, L0_ffn_w_up,
               L0_ffn_w_down, L1_attn_Wq, L1_attn_bq, L1_attn_Wk, L1_attn_bk,
               L1_attn_Wv, L1_attn_bv, L1_attn_Wo, L1_attn_bo, L1_ffn_w_gate,
               L1_ffn_w_up, L1_ffn_w_down):
    x = np.ascontiguousarray(np.asarray(L0_attn_x, dtype=np.float32))
    ident = np.eye(128).astype(BF)
    cwcol = np.zeros((128, NEC * 4), np.float32)
    cwa = np.asarray(L0_attn_conv_w)
    for k in range(NEC):
        for j in range(4):
            cwcol[:, k * 4 + j] = cwa[k * 128:(k + 1) * 128, 0, j]
    xpw = np.asarray(L0_attn_x_proj_w)
    dtb_col = np.zeros((128, NEC), np.float32)
    dtba = np.asarray(L0_attn_dt_b)
    for k in range(NEC):
        dtb_col[:, k] = dtba[k * 128:(k + 1) * 128]
    d_cols = np.zeros((128, NEC), np.float32)
    dpa = np.asarray(L0_attn_D_param)
    for k in range(NEC):
        d_cols[:, k] = dpa[k * 128:(k + 1) * 128]
    bq, bk = np.asarray(L1_attn_bq), np.asarray(L1_attn_bk)
    bcol = np.zeros((128, 16), np.float32)
    for c in range(8):
        bcol[:, c] = bq[c * 128:(c + 1) * 128]
        bcol[:, 8 + c] = bk[c * 128:(c + 1) * 128]
    shared = dict(
        ident=ident,
        ipT=np.ascontiguousarray(np.asarray(L0_attn_in_proj_w).T),
        cwcol=cwcol,
        xpTB=np.ascontiguousarray(xpw[1:].T),
        xpTd=np.ascontiguousarray(xpw[0:1].T),
        dtw_row=np.ascontiguousarray(np.asarray(L0_attn_dt_w).T),
        dtb_col=dtb_col,
        one_col=np.ones((128, 1), np.float32),
        a_cols=np.ascontiguousarray(np.asarray(L0_attn_A_param)),
        d_cols=d_cols,
        opT=np.ascontiguousarray(np.asarray(L0_attn_out_w).T),
        ob_b=np.ascontiguousarray(np.broadcast_to(np.asarray(L0_attn_out_b), (128, D))),
        wg0=np.ascontiguousarray(np.asarray(L0_ffn_w_gate).T),
        wu0=np.ascontiguousarray(np.asarray(L0_ffn_w_up).T),
        wd0=np.ascontiguousarray(np.asarray(L0_ffn_w_down).T),
        wq=np.ascontiguousarray(np.asarray(L1_attn_Wq).T),
        wk=np.ascontiguousarray(np.asarray(L1_attn_Wk).T),
        wv=np.ascontiguousarray(np.asarray(L1_attn_Wv).T),
        wo=np.ascontiguousarray(np.asarray(L1_attn_Wo).T),
        bcol=bcol,
        bvb=np.ascontiguousarray(np.broadcast_to(np.asarray(L1_attn_bv), (128, D))),
        bob=np.ascontiguousarray(np.broadcast_to(np.asarray(L1_attn_bo), (128, D))),
        wg1=np.ascontiguousarray(np.asarray(L1_ffn_w_gate)),
        wu1=np.ascontiguousarray(np.asarray(L1_ffn_w_up)),
        wd1=np.ascontiguousarray(np.asarray(L1_ffn_w_down)),
        **attn_host_consts(),
    )
    shared = _u16(shared)
    return dict(x=x, shared=shared)


def kernel(**inputs):
    return kernel_fused(**inputs)
